# version 01
# baseline (speedup 1.0000x reference)
import numpy as np
import ml_dtypes
from contextlib import ExitStack
import concourse.bass as bass
import concourse.mybir as mybir

F32 = mybir.dt.float32
BF16 = mybir.dt.bfloat16
AF = mybir.ActivationFunctionType
ALU = mybir.AluOpType
AX = mybir.AxisListType
npbf = ml_dtypes.bfloat16

NH, DK, DV = 16, 128, 128
HD = 2048
NCH = 16
FFH = 8192
NFH = 64
CONV = 4
C = 128
CB = 8
CT = 512
EPS_STAT = 1e-6
NORM_EPS = 1e-5
L2_EPS = 1e-6

ENGINES = ("pe", "act", "dve", "pool", "sp")
DMA_NCLASS = {"sp": 6, "pool": 6, "act": 2, "dve": 2}
CC_DOM = "ccD"
DMA_DOMS = [f"{e}D{r}" for e in DMA_NCLASS for r in range(DMA_NCLASS[e])]
ALL_DOMS = list(ENGINES) + DMA_DOMS + [CC_DOM]

class View:
    def __init__(self, ap):
        self._ap = ap
    def __getitem__(self, idx):
        if idx is Ellipsis:
            return self._ap
        return self._ap[idx]

class Buf:
    __slots__ = ("name", "writer", "readers", "is_psum")
    def __init__(self, name):
        self.name = name
        self.writer = None
        self.readers = []
        self.is_psum = False

class Op:
    __slots__ = ("eng", "fn", "deps", "kind", "inc", "tag", "count", "waits", "gidx", "eidx")
    def __init__(self, eng, fn, deps, kind, tag, gidx, eidx):
        self.eng = eng; self.fn = fn; self.deps = deps; self.kind = kind
        self.inc = False; self.tag = tag; self.count = None; self.waits = None
        self.gidx = gidx; self.eidx = eidx

class Sched:
    def __init__(self, debug_serial=False):
        self.debug_serial = debug_serial
        self.ops = {e: [] for e in ENGINES}
        self.glist = []
        self.tag = ""

    def _rec(self, eng, kind, fn, reads, writes):
        dom = eng
        idx = len(self.ops[eng])
        me = (eng, idx)
        def need(dep):
            e2, i2 = dep
            if e2 != eng:
                return True
            if self.ops[e2][i2].kind != "c":
                return True
            return eng != "pe"
        deps = set()
        for b in reads:
            if b.writer is not None and need(b.writer):
                deps.add(b.writer)
            if b.is_psum:
                for r in b.readers:
                    if r[0] != eng:
                        deps.add(r)
        for b in writes:
            if b.writer is not None and need(b.writer):
                deps.add(b.writer)
            for r in b.readers:
                if need(r):
                    deps.add(r)
        op = Op(eng, fn, sorted(deps), kind, self.tag, len(self.glist), idx)
        self.ops[eng].append(op)
        self.glist.append(op)
        for b in reads:
            b.readers.append(me)
        for b in writes:
            b.writer = me
            b.readers = [me]
        return op

    def comp(self, eng, fn, reads=(), writes=()):
        return self._rec(eng, "c", fn, reads, writes)

    def dma(self, eng, fn, reads=(), writes=()):
        return self._rec(eng, "d", fn, reads, writes)

    def cc(self, fn, reads=(), writes=()):
        return self._rec("pool", "cc", fn, reads, writes)

    def finalize(self):
        if self.debug_serial:
            prev = None
            for op in self.glist:
                op.deps = [(prev.eng, prev.eidx)] if (prev is not None and prev.eng != op.eng) else []
                prev = op
        for e in ENGINES:
            for op in self.ops[e]:
                for (e2, i2) in op.deps:
                    d = self.ops[e2][i2]
                    if d.kind == "c":
                        d.inc = True
        ccn = 0
        for e in ENGINES:
            c_comp = 0
            nd = 0
            for op in self.ops[e]:
                if op.kind == "c":
                    if op.inc:
                        c_comp += 1
                    op.count = ("c", c_comp, None)
                elif op.kind == "d":
                    r = nd % DMA_NCLASS[e]
                    i = nd // DMA_NCLASS[e]
                    op.count = ("d", i, r)
                    nd += 1
                else:
                    op.count = ("cc", ccn, None)
                    ccn += 1
        for e in ENGINES:
            for op in self.ops[e]:
                w = {}
                for (e2, i2) in op.deps:
                    d = self.ops[e2][i2]
                    k, v, r = d.count
                    if k == "c":
                        w[e2] = max(w.get(e2, 0), v)
                    elif k == "d":
                        dom = f"{e2}D{r}"
                        w[dom] = max(w.get(dom, 0), (v + 1) * 16)
                    else:
                        w[CC_DOM] = max(w.get(CC_DOM, 0), v + 1)
                if op.kind == "d":
                    k, i, r = op.count
                    if i > 0:
                        dom = f"{e}D{r}"
                        w[dom] = max(w.get(dom, 0), i * 16)
                elif op.kind == "cc":
                    k, i, r = op.count
                    if i > 0:
                        w[CC_DOM] = max(w.get(CC_DOM, 0), i)
                op.waits = w

    def _idx_of(self, op):
        return self.ops[op.eng].index(op)

    def stats(self):
        return {e: len(self.ops[e]) for e in ENGINES}

    def totals(self):
        tot = {}
        for e in ENGINES:
            nd = sum(1 for op in self.ops[e] if op.kind == "d")
            if e in DMA_NCLASS:
                for r in range(DMA_NCLASS[e]):
                    cnt = nd // DMA_NCLASS[e] + (1 if nd % DMA_NCLASS[e] > r else 0)
                    if cnt:
                        tot[f"{e}D{r}"] = cnt * 16
            ncc = sum(1 for op in self.ops[e] if op.kind == "cc")
            if ncc:
                tot[CC_DOM] = tot.get(CC_DOM, 0) + ncc
        return tot

    def emitter(self, e, sems):
        ops = self.ops[e]
        tot = self.totals()
        def run(h):
            wm = {d: 0 for d in ALL_DOMS}
            for op in ops:
                for d in sorted(op.waits):
                    v = op.waits[d]
                    if v > wm[d]:
                        h.wait_ge(sems[d], v)
                        wm[d] = v
                inst = op.fn(h)
                if op.kind == "c":
                    if op.inc:
                        assert inst is not None, f"op '{op.tag}' on {e} needs inc but returned None"
                        inst.then_inc(sems[e], 1)
                elif op.kind == "d":
                    k, i, r = op.count
                    inst.then_inc(sems[f"{e}D{r}"], 16)
                else:
                    inst.then_inc(sems[CC_DOM], 1)
            for d, v in sorted(tot.items()):
                if v > wm.get(d, 0):
                    h.wait_ge(sems[d], v)
        return run


def host_prep(inputs, n_tiles, dbg=False):
    T = n_tiles * 2 * CT
    x = inputs['L0_attn_hidden_states'][:, :T, :]
    f32 = np.float32
    def T_(a):
        return np.ascontiguousarray(np.asarray(a, f32).T)
    WqT = T_(inputs['L0_attn_Wq'])
    WkT = T_(inputs['L0_attn_Wk'])
    WvT = T_(inputs['L0_attn_Wv'])
    Wg1T = T_(inputs['L0_attn_Wg1'])
    Wg2T = T_(inputs['L0_attn_Wg2'])
    WoT = T_(inputs['L0_attn_Wo'])
    WsvT = T_(inputs['L0_attn_Wsv'])
    WgT = T_(inputs['L0_ffn_w_gate'])
    WuT = T_(inputs['L0_ffn_w_up'])
    WdT = T_(inputs['L0_ffn_w_down'])
    Wb1T = T_(inputs['L0_attn_Wb1'])
    Wa1T = T_(inputs['L0_attn_Wa1'])
    def z8(W2):
        z = np.zeros((32, 64), f32)
        for h in range(8):
            z[:, 8*h + h] = W2[0]
        return z
    Wb2Tz = z8(np.asarray(inputs['L0_attn_Wb2'], f32))
    Wa2Tz = z8(np.asarray(inputs['L0_attn_Wa2'], f32))
    ident = np.eye(128, dtype=npbf)
    shift = np.zeros((7, 128, 128), dtype=npbf)
    for d in range(1, 8):
        for j in range(128):
            m = j - d
            if m >= 0 and (j // CB) == (m // CB):
                shift[d-1, j, m] = 1
    zc = np.zeros((16, 128, 16), dtype=npbf)
    for r in range(16):
        zc[r, :, r] = 1
    zrow8 = np.zeros((8, 8, 128), dtype=npbf)
    for hh in range(8):
        zrow8[hh, hh, :] = 1
    ones_row = np.ones((1, 128), dtype=npbf)
    ones_col = np.ones((128, 1), f32)
    in_maps = []
    for core in range(8):
        b, hg = core // 2, core % 2
        own = np.concatenate([x[b, (2*tt+hg)*CT:(2*tt+hg+1)*CT, :] for tt in range(n_tiles)], 0)
        hsl = slice(hg*1024, (hg+1)*1024)
        m = {
            'xT': np.ascontiguousarray(own.T),
            'wgT': WgT, 'wuT': WuT, 'wdT': WdT,
            'wqT': np.ascontiguousarray(WqT[:, hsl]),
            'wkT': np.ascontiguousarray(WkT[:, hsl]),
            'wvT': np.ascontiguousarray(WvT[:, hsl]),
            'wg1T': Wg1T,
            'wg2T': np.ascontiguousarray(Wg2T[:, hsl]),
            'woT': np.ascontiguousarray(WoT[hg*1024:(hg+1)*1024, :]),
            'wsvT': np.ascontiguousarray(WsvT.astype(npbf)),
            'wb1T': np.ascontiguousarray(Wb1T.astype(npbf)),
            'wa1T': np.ascontiguousarray(Wa1T.astype(npbf)),
            'wb2Tz': np.ascontiguousarray(Wb2Tz.astype(npbf)),
            'wa2Tz': np.ascontiguousarray(Wa2Tz.astype(npbf)),
            'bb1c': np.asarray(inputs['L0_attn_bb1'], f32).reshape(32, 1),
            'ba1c': np.asarray(inputs['L0_attn_ba1'], f32).reshape(32, 1),
            'bb2c': np.full((8, 1), float(inputs['L0_attn_bb2'][0]), f32),
            'ba2c': np.full((8, 1), float(inputs['L0_attn_ba2'][0]), f32),
            'convw': np.stack([
                np.asarray(inputs['L0_attn_conv_q'], f32)[hg*1024:(hg+1)*1024].reshape(8, 128, 4).transpose(1, 0, 2),
                np.asarray(inputs['L0_attn_conv_k'], f32)[hg*1024:(hg+1)*1024].reshape(8, 128, 4).transpose(1, 0, 2),
                np.asarray(inputs['L0_attn_conv_v'], f32)[hg*1024:(hg+1)*1024].reshape(8, 128, 4).transpose(1, 0, 2),
            ], 0).copy(),
            'dtb': np.asarray(inputs['L0_attn_dt_bias'], f32)[hg*1024:(hg+1)*1024].reshape(8, 128).T.copy(),
            'alog': np.asarray(inputs['L0_attn_A_log'], f32)[hg*8:(hg+1)*8].reshape(1, 8).copy(),
            'bg2c': np.asarray(inputs['L0_attn_bg2'], f32)[hg*1024:(hg+1)*1024].reshape(8, 128).T.copy(),
            'normw': np.asarray(inputs['L0_attn_norm_w'], f32).reshape(128, 1).copy(),
            'ident': ident, 'shift': shift, 'zc': zc,
            'ones_row': ones_row, 'ones_col': ones_col,
            'identf': np.eye(16, dtype=f32),
            'zrow8': zrow8,
            'mselh': np.stack([np.full(128, 1.0 if hg == 0 else 0.0, f32),
                               np.full(128, 1.0 if hg == 1 else 0.0, f32)], 1),
        }
        in_maps.append(m)
    return in_maps


class Build:
    def __init__(self, n_tiles, dbg=False, debug_serial=False):
        self.nt = n_tiles
        self.T = n_tiles * 2 * CT
        self.dbg = dbg
        self.nc = bass.Bass("TRN2", target_bir_lowering=False, debug=False)
        self.S = Sched(debug_serial)
        self.es = ExitStack()
        self.bufs = {}
        self.vw_f = []
        self.vw_a = []
        self.sb = {}
        self.dr = {}
        self.ps = {}

    def buf(self, name):
        if name not in self.bufs:
            self.bufs[name] = Buf(name)
        return self.bufs[name]

    def dram(self, name, shape, dt, kind=None):
        t = self.nc.dram_tensor(name, shape, dt, kind=kind) if kind else self.nc.dram_tensor(name, shape, dt)
        self.dr[name] = t
        self.buf(name)
        return t

    def sbuf(self, name, shape, dt):
        t = self.es.enter_context(self.nc.sbuf_tensor(name, shape, dt))
        self.sb[name] = t
        self.buf(name)
        return t

    def psum(self, name, shape, dt=F32):
        t = self.es.enter_context(self.nc.psum_tensor(name, shape, dt))
        self.ps[name] = t
        self.buf(name).is_psum = True
        return t

    def alloc(self):
        nc, nt, T = self.nc, self.nt, self.T
        D = self.dram
        NT = nt * CT
        D('xT', [HD, NT], F32, kind="ExternalInput")
        for n, sh in [('wgT', [HD, FFH]), ('wuT', [HD, FFH]), ('wdT', [FFH, HD]),
                      ('wqT', [HD, 1024]), ('wkT', [HD, 1024]), ('wvT', [HD, 1024]),
                      ('wg1T', [HD, 128]), ('wg2T', [128, 1024]), ('woT', [1024, HD])]:
            D(n, sh, F32, kind="ExternalInput")
        for n, sh in [('wsvT', [128, 128]), ('wb1T', [3, 32]), ('wa1T', [3, 32]),
                      ('wb2Tz', [32, 64]), ('wa2Tz', [32, 64]),
                      ('ident', [128, 128]), ('shift', [7, 128, 128]), ('zc', [16, 128, 16]),
                      ('ones_row', [1, 128])]:
            D(n, sh, BF16, kind="ExternalInput")
        for n, sh in [('bb1c', [32, 1]), ('ba1c', [32, 1]), ('bb2c', [8, 1]), ('ba2c', [8, 1]),
                      ('convw', [3, 128, 8, 4]), ('dtb', [128, 8]), ('alog', [1, 8]),
                      ('bg2c', [128, 8]), ('normw', [128, 1]), ('ones_col', [128, 1]),
                      ('mselh', [128, 2]), ('identf', [16, 16])]:
            D(n, sh, F32, kind="ExternalInput")
        D('zrow8', [8, 8, 128], BF16, kind="ExternalInput")
        D('outT', [HD, NT], F32, kind="ExternalOutput")
        ik = "ExternalOutput" if self.dbg else None
        D('x1T', [HD, NT], F32, kind=ik)
        D('x2T', [HD, NT], F32, kind=ik)
        D('wg_c', [HD, FFH], BF16)
        D('wu_c', [HD, FFH], BF16)
        D('wd_c', [FFH, HD], BF16)
        D('wq_c', [HD, 1024], BF16)
        D('wk_c', [HD, 1024], BF16)
        D('wv_c', [HD, 1024], BF16)
        D('wg1_c', [HD, 128], BF16)
        D('wo_c', [1024, HD], BF16)
        for tt in range(nt):
            D(f'xr2_in{tt}', [HD, CT], BF16)
            D(f'xr2_out{tt}', [2*HD, CT], BF16)
        D('mr_in', [1, NT], F32)
        D('mr_out', [2, NT], F32)
        for tt in range(nt):
            D(f'pp_in{tt}', [2*HD, CT], F32)
            D(f'pp_out{tt}', [2*HD, CT], F32)
        if self.dbg:
            D('dbg_xr2', [2*HD, NT], BF16, kind="ExternalOutput")
            D('dbg_mr', [2, NT], F32, kind="ExternalOutput")

        B = self.sbuf
        B('c_ident', [128, 128], BF16)
        B('c_identf', [16, 16], F32)
        B('c_shift', [128, 7, 128], BF16)
        B('c_zc', [128, 16, 16], BF16)
        B('c_zrow8', [8, 8, 128], BF16)
        B('c_ones_row', [1, 128], BF16)
        B('c_ones_col', [128, 1], F32)
        B('c_ones_colb', [128, 1], BF16)
        B('c_onesrow_f', [1, 128], F32)
        B('c_wsv', [128, 128], BF16)
        B('c_wb1', [3, 32], BF16)
        B('c_wa1', [3, 32], BF16)
        B('c_wb2z', [32, 64], BF16)
        B('c_wa2z', [32, 64], BF16)
        B('c_bb1', [32, 1], F32)
        B('c_ba1', [32, 1], F32)
        B('c_bb2', [8, 1], F32)
        B('c_ba2', [8, 1], F32)
        B('c_convw', [128, 3, 8, 4], F32)
        B('c_dtb', [128, 8], F32)
        B('c_negea', [128, 8], F32)
        B('c_bg2', [128, 8], F32)
        B('c_normw', [128, 1], F32)
        B('c_wg2', [128, 8, 128], BF16)
        B('c_alog', [1, 8], F32)
        B('xch0', [128, CT], F32); B('xch1', [128, CT], F32)
        B('xsq0', [128, CT], F32)
        B('wgu0', [128, 2, 16, 128], BF16); B('wgu1', [128, 2, 16, 128], BF16)
        B('wdsl0', [128, 16, 128], BF16); B('wdsl1', [128, 16, 128], BF16)
        B('x1oc0', [128, CT], F32); B('x1oc1', [128, CT], F32)
        arena = self.sbuf('arena', [128, 17408], F32)
        av = arena[...]
        def fview(name, ap):
            self.vw_f.append(name)
            self.sb[name] = View(ap)
            self.buf(name)
        def aview(name, ap):
            self.vw_a.append(name)
            self.sb[name] = View(ap)
            self.buf(name)
        del self.sb['arena']
        fview('facc', av[:, 0:8192].rearrange("p (a b) -> p a b", a=16))
        fview('hti', av[:, 8192:12288].bitcast(BF16).rearrange("p (a b) -> p a b", a=16))
        fview('xr', av[:, 12288:16384].bitcast(BF16).rearrange("p (a b) -> p a b", a=16))
        fview('sil', av[:, 16384:16896])
        fview('toc', av[:, 16896:17408])
        aview('qc', av[:, 0:2048].bitcast(BF16).rearrange("p (a b) -> p a b", a=8))
        aview('kc', av[:, 2048:4096].bitcast(BF16).rearrange("p (a b) -> p a b", a=8))
        aview('vc', av[:, 4096:6144].bitcast(BF16).rearrange("p (a b) -> p a b", a=8))
        aview('sgate', av[:, 6144:8192].bitcast(BF16).rearrange("p (a b) -> p a b", a=8))
        aview('o_ct', av[:, 8192:10240].bitcast(BF16).rearrange("p (a b) -> p a b", a=8))
        aview('zt', av[:, 10240:10752])
        aview('sp1', av[:, 10752:11264])
        aview('ab', av[:, 11264:11776])
        aview('spt', av[:, 11776:12288])
        aview('gt', av[:, 12288:12800])
        aview('xr2t', av[:, 12800:16896].bitcast(BF16).rearrange("p (a b) -> p a b", a=16))
        aview('wosl0', av[:, 16896:17408].bitcast(BF16).rearrange("p (a b) -> p a b", a=8))
        rowsA = self.sbuf('rowsA', [128, CT], F32)
        rowsB = self.sbuf('rowsB', [128, CT], F32)
        rowsC = self.sbuf('rowsC', [128, CT], BF16)
        rowsD = self.sbuf('rowsD', [128, CT], BF16)
        del self.sb['rowsA']; del self.sb['rowsB']; del self.sb['rowsC']; del self.sb['rowsD']
        for nm, t2, p0 in [('rstd_row', rowsA, 0), ('t_row', rowsA, 32), ('mr_row', rowsA, 64),
                           ('amr_row', rowsA, 96), ('m_row', rowsB, 0), ('v_row', rowsB, 32)]:
            self.sb[nm] = View(t2[p0:p0+1, :])
            self.buf(nm)
        self.sb['mrn_bf'] = View(rowsC[...])
        self.buf('mrn_bf')
        self.sb['amrn_bf'] = View(rowsD[...])
        self.buf('amrn_bf')
        B('swg_row', [128, 22, 128], BF16)
        B('swu_row', [128, 22, 128], BF16)
        B('swq_row', [128, 3, 128], BF16)
        B('swk_row', [128, 3, 128], BF16)
        B('swv_row', [128, 3, 128], BF16)
        B('swg1_row', [128, 1, 128], BF16)
        B('up0', [128, 516], BF16)
        B('tls', [128, 3, 8, 3], BF16)
        B('glt', [128, CT], BF16)
        B('c_ones_f', [128, CT], F32)
        B('wp0', [128, 16, 128], BF16); B('wp1', [128, 16, 128], BF16)
        B('ct1', [128, CT], F32); B('ct2', [128, CT], F32)
        B('st_all', [8, 7, CT], BF16)
        st8A = self.sbuf('st8A', [128, CT], F32)
        st8B = self.sbuf('st8B', [128, CT], F32)
        st8C = self.sbuf('st8C', [128, CT], F32)
        for dn in ['st8A', 'st8B', 'st8C']:
            del self.sb[dn]
        st8E = self.sbuf('st8E', [128, CT], F32)
        del self.sb['st8E']
        for nm, t2, p0 in [('tmp8', st8A, 0), ('sl2t', st8A, 32), ('scost', st8A, 64), ('rqn8', st8A, 96),
                           ('beta8', st8B, 0), ('amp8', st8B, 32), ('ampz8', st8B, 64), ('rkn8', st8B, 96),
                           ('den2', st8C, 0), ('rsqk8', st8C, 32),
                           ('den1', st8E, 0)]:
            self.sb[nm] = View(t2[p0:p0+8, :])
            self.buf(nm)
        B('ampz8c', [8, CT], BF16); B('rkn8c', [8, CT], BF16); B('rqn8c', [8, CT], BF16)
        B('s3', [3, 8*CT], BF16)
        B('hb', [32, CT], BF16)
        B('betaT', [128, 32], F32); B('betaTn', [128, 32], F32)
        B('stsc0', [128, CT], BF16)
        B('vh_h', [128, CT], BF16); B('e_h', [128, CT], BF16)
        B('ampb', [128, CT], BF16); B('knb', [128, CT], BF16); B('qnb', [128, CT], BF16)
        B('D_h', [128, CT], BF16); B('G_h', [128, CT], F32)
        B('kn_h', [128, CT], BF16); B('qn_h', [128, CT], BF16)
        B('pc0', [128, CT], BF16); B('pc1', [128, CT], BF16)
        B('ttb', [128, CT], BF16); B('p3', [128, CT], BF16); B('p4', [128, CT], BF16)
        B('pmrow', [16, CT], BF16)
        B('TRt', [128, 4, 16], F32)
        B('c_zrow', [1, 256], BF16)
        B('m_bV', [128, 128], F32)
        for n in ['m_W', 'm_qt', 'm_Wb', 'm_WbT', 'm_At', 'm_R', 'm_U', 'm_Q',
                  'm_N1', 'm_N2', 'm_N4', 'm_N8', 'm_N16', 'm_N32', 'm_N64',
                  'm_X1', 'm_X2', 'm_X4', 'm_X8', 'm_X16', 'm_X32',
                  'lv_ae', 'lv_ak', 'lv_aq', 'lv_be', 'lv_bk',
                  'otm', 'otm2', 'diagt']:
            B(n, [128, 128], BF16)
        B('lv_ga', [128, 128], F32)
        self.sb['lv_gb'] = View(av[:, 12288:12416])
        self.bufs['lv_gb'] = self.buf('gt')
        for zn, wd in [(8, 8), (24, 8), (40, 8), (56, 8), (16, 16), (48, 16), (32, 32)]:
            B(f'padM_{zn}_{wd}', [128, 64], BF16)
        for zn, wd in [(16, 8), (48, 8), (32, 16), (32, 8)]:
            B(f'padA_{zn}_{wd}', [128, 64], BF16)
        B('colA', [128, 1], F32); B('lamc', [128, 1], F32); B('rms_c', [128, 1], F32)
        B('c_zero', [128, 1], F32)
        B('Sfp0', [128, 8, 128], F32)
        B('Stmp', [128, 128], F32)
        B('Sbf', [128, 8, 128], BF16)
        B('msel', [128, 2], F32)
        P = self.psum
        P('pb0', [128, CT]); P('pb1', [128, CT]); P('pb2', [128, CT]); P('pb3', [128, CT])
        P('pa0', [128, CT]); P('pa1', [128, CT]); P('pa2', [128, CT])
        P('patb', [128, 1024], BF16)
        self.sems = {}
        for d in ALL_DOMS:
            self.sems[d] = self.es.enter_context(self.nc.semaphore(f"sem_{d}"))

    def arena_barrier(self, to_side):
        S, b = self.S, self.buf
        rd = self.vw_f if to_side == 'a' else self.vw_a
        wr = self.vw_a if to_side == 'a' else self.vw_f
        S.comp('dve', lambda h: h.memset(self.sb['ct1'][0:1, 0:1], 0.0),
               reads=[b(x) for x in rd], writes=[b(x) for x in wr] + [b('ct1')])

    def swsl(self, name, i):
        return self.sb[name][32*(i % 3):32*(i % 3)+1, i // 3, :]

    def constname(self, val):
        return f"cc_{val!r}".replace('-', 'm').replace('.', 'p').replace('+', '')

    def constcol(self, val):
        key = self.constname(val)
        if key not in self.sb:
            t = self.sbuf(key, [128, 1], F32)
            self.S.comp('dve', (lambda t2=t: lambda h: h.memset(t2[...], float(val)))(),
                        writes=[self.buf(key)])
        return self.sb[key]

    def t(self, name):
        return self.sb[name] if name in self.sb else (self.ps[name] if name in self.ps else self.dr[name])

    def emit_const_load(self):
        S, t, b = self.S, self.t, self.buf
        dr, sb = self.dr, self.sb
        pairs = [
            ('ident', 'c_ident', None), ('ones_row', 'c_ones_row', None),
            ('ones_col', 'c_ones_col', None),
            ('wsvT', 'c_wsv', None), ('wb1T', 'c_wb1', None), ('wa1T', 'c_wa1', None),
            ('wb2Tz', 'c_wb2z', None), ('wa2Tz', 'c_wa2z', None),
            ('bb1c', 'c_bb1', None), ('ba1c', 'c_ba1', None), ('bb2c', 'c_bb2', None), ('ba2c', 'c_ba2', None),
            ('dtb', 'c_dtb', None), ('bg2c', 'c_bg2', None), ('normw', 'c_normw', None),
            ('alog', 'c_alog', None), ('identf', 'c_identf', None),
        ]
        for src, dst, _ in pairs:
            S.dma('sp', (lambda s=src, d2=dst: (lambda h: h.dma_start(out=self.sb[d2][...], in_=self.dr[s].ap())))(),
                  reads=[b(src)], writes=[b(dst)])
        S.dma('sp', lambda h: h.dma_start(out=sb['c_shift'][...],
                                          in_=dr['shift'].ap().rearrange("s p m -> p s m")),
              reads=[b('shift')], writes=[b('c_shift')])
        S.dma('sp', lambda h: h.dma_start(out=sb['c_zc'][...],
                                          in_=dr['zc'].ap().rearrange("s p m -> p s m")),
              reads=[b('zc')], writes=[b('c_zc')])
        S.dma('sp', lambda h: h.dma_start(out=sb['c_zrow8'][...], in_=dr['zrow8'].ap()),
              reads=[b('zrow8')], writes=[b('c_zrow8')])
        S.dma('sp', lambda h: h.dma_start(out=sb['c_convw'][...],
                                          in_=dr['convw'].ap().rearrange("s p h k -> p s h k")),
              reads=[b('convw')], writes=[b('c_convw')])
        S.comp('dve', lambda h: h.memset(sb['c_onesrow_f'][...], 1.0), writes=[b('c_onesrow_f')])
        S.comp('dve', lambda h: h.memset(sb['c_ones_colb'][...], 1.0), writes=[b('c_ones_colb')])
        S.comp('dve', lambda h: h.memset(sb['c_ones_f'][...], 1.0), writes=[b('c_ones_f')])

        S.comp('dve', lambda h: h.memset(sb['c_zrow'][...], 0.0), writes=[b('c_zrow')])
        S.comp('dve', lambda h: h.memset(sb['c_zero'][...], 0.0), writes=[b('c_zero')])
        padnames = [f'padM_{z}_{w}' for z, w in [(8,8),(24,8),(40,8),(56,8),(16,16),(48,16),(32,32)]] + \
                   [f'padA_{z}_{w}' for z, w in [(16,8),(48,8),(32,16),(32,8)]]
        for n in padnames + ['p3', 'p4', 'ttb', 'pc0', 'pc1', 'Sfp0', 'Sbf']:
            S.comp('dve', (lambda n2=n: lambda h: h.memset(sb[n2][...], 0.0))(), writes=[b(n)])
        S.dma('sp', lambda h: h.dma_start(out=sb['msel'][...], in_=dr['mselh'].ap()),
              reads=[b('mselh')], writes=[b('msel')])
        S.comp('act', lambda h: h.activation(out=sb['c_alog'][...], in_=sb['c_alog'][...], func=AF.Exp),
               reads=[b('c_alog')], writes=[b('c_alog')])
        S.comp('pe', lambda h: h.matmul(self.ps['pa0'][:, 0:8], sb['c_onesrow_f'][0:1, :],
                                        sb['c_alog'][...], start=True, stop=True),
               reads=[b('c_onesrow_f'), b('c_alog')], writes=[b('pa0')])
        S.comp('act', lambda h: h.activation(out=sb['c_negea'][...], in_=self.ps['pa0'][:, 0:8], func=AF.Copy),
               reads=[b('pa0')], writes=[b('c_negea')])
        S.comp('dve', lambda h: h.tensor_scalar(out=sb['c_negea'][...], in0=sb['c_negea'][...],
                                                scalar1=-1.0, scalar2=None, op0=ALU.mult),
               reads=[b('c_negea')], writes=[b('c_negea')])

    def emit_wconv_pre(self):
        S, b = self.S, self.buf
        dr, sb, ps = self.dr, self.sb, self.ps
        jobs = [
            ('wgT', 'wg_c', FFH, 128, 'swg_row'),
            ('wuT', 'wu_c', FFH, 128, 'swu_row'),
            ('wqT', 'wq_c', 1024, 128, 'swq_row'),
            ('wkT', 'wk_c', 1024, 128, 'swk_row'),
            ('wvT', 'wv_c', 1024, 128, 'swv_row'),
            ('wg1T', 'wg1_c', 128, 128, 'swg1_row'),
            ('wdT', 'wd_c', HD, 128, None),
            ('woT', 'wo_c', HD, 128, None),
        ]
        S.dma('pool', lambda h: h.dma_start(out=sb['c_wg2'][...],
                                            in_=dr['wg2T'].ap().rearrange("p (h v) -> p h v", h=8)),
              reads=[b('wg2T')], writes=[b('c_wg2')])
        stage_flat = sb['facc'][...].rearrange("p a b -> p (a b)").bitcast(BF16)
        for src, cache, ncols, cch, swr in jobs:
            srcT = dr[src]
            nrow_ch = srcT.shape[0] // 128
            n_sl = ncols // cch
            half = 8192
            for isl in range(n_sl):
                slot = 'facc'
                off = (isl % 2) * half
                dst = stage_flat[:, off:off + nrow_ch*cch].rearrange("p (r c) -> p r c", r=nrow_ch)
                src_ap = srcT.ap().rearrange("(r p) n -> p r n", p=128)[:, :, isl*cch:(isl+1)*cch]
                S.dma('pool', (lambda d2=dst, s2=src_ap: lambda h: h.dma_start(out=d2, in_=s2))(),
                      reads=[b(src)], writes=[b(slot)])
                cache_ap = dr[cache].ap().rearrange("(r p) n -> p r n", p=128)[:, :, isl*cch:(isl+1)*cch]
                S.dma('sp', (lambda d2=cache_ap, s2=dst: lambda h: h.dma_start(out=d2, in_=s2))(),
                      reads=[b(slot)], writes=[b(cache)])
                if swr is not None:
                    for r in range(nrow_ch):
                        S.comp('pe', (lambda r2=r, d2=dst, st=(r == 0), nr=nrow_ch, cc2=cch:
                                      lambda h: h.matmul(ps['pa1'][0:1, 0:cc2], sb['c_ones_colb'][:, 0:1],
                                                         d2[:, r2, :], start=st, stop=(r2 == nr-1)))(),
                               reads=[b(slot), b('c_ones_colb')], writes=[b('pa1')])
                    S.comp('act', (lambda i2=isl, swr2=swr, cc2=cch:
                                   lambda h: h.activation(out=self.swsl(swr2, i2),
                                                          in_=ps['pa1'][0:1, 0:cc2], func=AF.Copy))(),
                           reads=[b('pa1')], writes=[b(swr)])

    def _ln_rows(self):
        S, b = self.S, self.buf
        sb, ps = self.sb, self.ps
        S.comp('dve', lambda h: h.scalar_tensor_tensor(out=sb['v_row'][...], in0=sb['m_row'][...],
                                                       scalar=-1.0, in1=sb['m_row'][...],
                                                       op0=ALU.mult, op1=ALU.mult),
               reads=[b('m_row')], writes=[b('v_row')])
        S.comp('dve', lambda h: h.tensor_tensor(out=sb['v_row'][...], in0=sb['t_row'][...],
                                                in1=sb['v_row'][...], op=ALU.add),
               reads=[b('t_row'), b('v_row')], writes=[b('v_row')])
        epsc = self.constcol(NORM_EPS)
        S.comp('act', (lambda e2=epsc: lambda h: h.activation(out=sb['t_row'][...], in_=sb['v_row'][...],
                                             func=AF.Ln, bias=e2[0:1, 0:1]))(),
               reads=[b('v_row'), b(self.constname(NORM_EPS))], writes=[b('t_row')])
        S.comp('act', lambda h: h.activation(out=sb['rstd_row'][...], in_=sb['t_row'][...],
                                             func=AF.Exp, scale=-0.5),
               reads=[b('t_row')], writes=[b('rstd_row')])
        S.comp('dve', lambda h: h.tensor_tensor(out=sb['mr_row'][...], in0=sb['m_row'][...],
                                                in1=sb['rstd_row'][...], op=ALU.mult),
               reads=[b('m_row'), b('rstd_row')], writes=[b('mr_row')])
        for p0 in [0, 32, 64]:
            S.comp('dve', (lambda p2=p0: lambda h: h.tensor_scalar(
                out=sb['mrn_bf'][p2:p2+1, :], in0=sb['mr_row'][...],
                scalar1=-1.0, scalar2=None, op0=ALU.mult))(),
                reads=[b('mr_row')], writes=[b('mrn_bf')])
        S.comp('pe', lambda h: h.matmul(ps['pb1'][...], sb['c_onesrow_f'][0:1, :], sb['rstd_row'][...],
                                        start=True, stop=True),
               reads=[b('c_onesrow_f'), b('rstd_row')], writes=[b('pb1')])

    def emit_ln(self, tt, src_dram, xr_dst, bank, bank2):
        S, b = self.S, self.buf
        dr, sb, ps = self.dr, self.sb, self.ps
        xsl = [sb['xch0'], sb['xch1']]
        xsqs = [sb['xsq0'], sb['xsq0']]
        xbufs = [b('xch0'), b('xch1')]
        xsqb = [b('xsq0'), b('xsq0')]
        src = dr[src_dram].ap().rearrange("(c p) t -> p c t", p=128)
        for c in range(NCH):
            sl = xsl[c % 2]; bb2 = xbufs[c % 2]
            S.dma('sp', (lambda c2=c, sl2=sl: lambda h: h.dma_start(
                out=sl2[...], in_=src[:, c2, tt*CT:(tt+1)*CT]))(),
                reads=[b(src_dram)], writes=[bb2])
            S.comp('pe', (lambda c2=c, sl2=sl: lambda h: h.matmul(
                ps[bank][0:1, :], sb['c_ones_col'][:, 0:1], sl2[...], start=(c2 == 0), stop=(c2 == NCH-1)))(),
                reads=[bb2, b('c_ones_col')], writes=[b(bank)])
            S.comp('act', (lambda c2=c, sl2=sl, q=xsqs[c % 2]: lambda h: h.activation(
                out=q[...], in_=sl2[...], func=AF.Square))(),
                reads=[bb2], writes=[xsqb[c % 2]])
            S.comp('pe', (lambda c2=c, q=xsqs[c % 2]: lambda h: h.matmul(
                ps[bank2][0:1, :], sb['c_ones_col'][:, 0:1], q[...], start=(c2 == 0), stop=(c2 == NCH-1)))(),
                reads=[xsqb[c % 2], b('c_ones_col')], writes=[b(bank2)])
        S.comp('dve', lambda h: h.tensor_scalar(out=sb['m_row'][...], in0=ps[bank][0:1, :],
                                                scalar1=1.0/HD, scalar2=None, op0=ALU.mult),
               reads=[b(bank)], writes=[b('m_row')])
        S.comp('dve', lambda h: h.tensor_scalar(out=sb['t_row'][...], in0=ps[bank2][0:1, :],
                                                scalar1=1.0/HD, scalar2=None, op0=ALU.mult),
               reads=[b(bank2)], writes=[b('t_row')])
        S.comp('dve', lambda h: h.scalar_tensor_tensor(out=sb['v_row'][...], in0=sb['m_row'][...],
                                                       scalar=-1.0, in1=sb['m_row'][...],
                                                       op0=ALU.mult, op1=ALU.mult),
               reads=[b('m_row')], writes=[b('v_row')])
        S.comp('dve', lambda h: h.tensor_tensor(out=sb['v_row'][...], in0=sb['t_row'][...],
                                                in1=sb['v_row'][...], op=ALU.add),
               reads=[b('t_row'), b('v_row')], writes=[b('v_row')])
        epsc = self.constcol(NORM_EPS)
        S.comp('act', (lambda e2=epsc: lambda h: h.activation(out=sb['t_row'][...], in_=sb['v_row'][...],
                                             func=AF.Ln, bias=e2[0:1, 0:1]))(),
               reads=[b('v_row'), b(self.constname(NORM_EPS))], writes=[b('t_row')])
        S.comp('act', lambda h: h.activation(out=sb['rstd_row'][...], in_=sb['t_row'][...],
                                             func=AF.Exp, scale=-0.5),
               reads=[b('t_row')], writes=[b('rstd_row')])
        S.comp('dve', lambda h: h.tensor_tensor(out=sb['mr_row'][...], in0=sb['m_row'][...],
                                                in1=sb['rstd_row'][...], op=ALU.mult),
               reads=[b('m_row'), b('rstd_row')], writes=[b('mr_row')])
        for p0 in [0, 32, 64]:
            S.comp('dve', (lambda p2=p0: lambda h: h.tensor_scalar(
                out=sb['mrn_bf'][p2:p2+1, :], in0=sb['mr_row'][...],
                scalar1=-1.0, scalar2=None, op0=ALU.mult))(),
                reads=[b('mr_row')], writes=[b('mrn_bf')])
        S.comp('pe', lambda h: h.matmul(ps['pb1'][...], sb['c_onesrow_f'][0:1, :], sb['rstd_row'][...],
                                        start=True, stop=True),
               reads=[b('c_onesrow_f'), b('rstd_row')], writes=[b('pb1')])
        for c in range(NCH):
            sl = xsl[c % 2]; bb2 = xbufs[c % 2]
            S.dma('sp', (lambda c2=c, sl2=sl: lambda h: h.dma_start(
                out=sl2[...], in_=src[:, c2, tt*CT:(tt+1)*CT]))(),
                reads=[b(src_dram)], writes=[bb2])
            S.comp('dve', (lambda c2=c, sl2=sl: lambda h: h.tensor_tensor(
                out=sb[xr_dst][:, c2, :], in0=sl2[...], in1=ps['pb1'][...], op=ALU.mult))(),
                reads=[bb2, b('pb1')], writes=[b(xr_dst)])

    def emit_ffn(self, tt, phase):
        S, b = self.S, self.buf
        dr, sb, ps = self.dr, self.sb, self.ps
        wgu = [sb['wgu0'], sb['wgu1']]
        wgub = [b('wgu0'), b('wgu1')]
        wds = [sb['wdsl0'], sb['wdsl1']]
        wdsb = [b('wdsl0'), b('wdsl1')]
        wgc = dr['wg_c'].ap().rearrange("(r p) n -> p r n", p=128)
        wuc = dr['wu_c'].ap().rearrange("(r p) n -> p r n", p=128)
        wdc = dr['wd_c'].ap().rearrange("(r p) n -> p r n", p=128)
        NG = 4
        GH = NFH // NG
        dbank = ['pb0', 'pb1']
        src_x = dr['xT' if phase == 'f1' else 'x2T'].ap().rearrange("(c p) t -> p c t", p=128)
        dstn = 'x1T' if phase == 'f1' else 'outT'
        dst_ap = dr[dstn].ap().rearrange("(c p) t -> p c t", p=128)
        stat_bank = 'pb2'
        for G in range(NG):
            for hcl in range(GH):
                hc = G*GH + hcl
                slot = wgu[hc % 2]; slb = wgub[hc % 2]
                S.dma('sp', (lambda hc2=hc, s2=slot: lambda h: h.dma_start(
                    out=s2[:, 0, :, :], in_=wgc[:, :, hc2*128:(hc2+1)*128]))(),
                    reads=[b('wg_c')], writes=[slb])
                S.dma('sp', (lambda hc2=hc, s2=slot: lambda h: h.dma_start(
                    out=s2[:, 1, :, :], in_=wuc[:, :, hc2*128:(hc2+1)*128]))(),
                    reads=[b('wg_c')], writes=[slb])
                for gi, bank in [(0, 'pb2'), (1, 'pb3')]:
                    for c2 in range(NCH):
                        S.comp('pe', (lambda s2=slot, gi2=gi, c3=c2, bank2=bank:
                                      lambda h: h.matmul(ps[bank2][...], s2[:, gi2, c3, :],
                                                         sb['xr'][:, c3, :], start=(c3 == 0), stop=False))(),
                               reads=[slb, b('xr')], writes=[b(bank)])
                    swr = 'swg_row' if gi == 0 else 'swu_row'
                    S.comp('pe', (lambda bank2=bank, swr2=swr, hc2=hc:
                                  lambda h: h.matmul(ps[bank2][...], self.swsl(swr2, hc2),
                                                     sb['mrn_bf'][32*(hc2 % 3):32*(hc2 % 3)+1, :],
                                                     start=False, stop=True))(),
                           reads=[b(swr), b('mrn_bf')], writes=[b(bank)])
                S.comp('act', (lambda: lambda h: h.activation(out=sb['sil'][...], in_=ps['pb2'][...],
                                                              func=AF.Silu))(),
                       reads=[b('pb2')], writes=[b('sil')])
                S.comp('dve', (lambda hcl2=hcl: lambda h: h.tensor_tensor(
                    out=sb['hti'][:, hcl2, :], in0=ps['pb3'][...], in1=sb['sil'][...], op=ALU.mult))(),
                    reads=[b('pb3'), b('sil')], writes=[b('hti')])
            for oc in range(NCH):
                dbk = dbank[oc % 2]
                slot = wds[oc % 2]; slb = wdsb[oc % 2]
                S.dma('sp', (lambda G2=G, oc2=oc, s2=slot: lambda h: h.dma_start(
                    out=s2[...], in_=wdc[:, G2*GH:(G2+1)*GH, oc2*128:(oc2+1)*128]))(),
                    reads=[b('wd_c')], writes=[slb])
                for hcl in range(GH):
                    S.comp('pe', (lambda s2=slot, hcl2=hcl, dbk2=dbk:
                                  lambda h: h.matmul(ps[dbk2][...], s2[:, hcl2, :],
                                                     sb['hti'][:, hcl2, :], start=(hcl2 == 0),
                                                     stop=(hcl2 == GH-1)))(),
                           reads=[slb, b('hti')], writes=[b(dbk)])
                if G == 0:
                    S.comp('dve', (lambda oc2=oc, dbk2=dbk: lambda h: h.tensor_copy(
                        out=sb['facc'][:, oc2, :], in_=ps[dbk2][...]))(),
                        reads=[b(dbk)], writes=[b('facc')])
                elif G < NG-1:
                    S.comp('dve', (lambda oc2=oc, dbk2=dbk: lambda h: h.tensor_tensor(
                        out=sb['facc'][:, oc2, :], in0=ps[dbk2][...], in1=sb['facc'][:, oc2, :],
                        op=ALU.add))(),
                        reads=[b(dbk)], writes=[b('facc')])
                else:
                    xb = [b('x1oc0'), b('x1oc1')][oc % 2]
                    xoc = [sb['x1oc0'], sb['x1oc1']][oc % 2]
                    S.comp('dve', (lambda oc2=oc, dbk2=dbk: lambda h: h.tensor_tensor(
                        out=sb['toc'][...], in0=ps[dbk2][...], in1=sb['facc'][:, oc2, :], op=ALU.add))(),
                        reads=[b(dbk), b('facc')], writes=[b('toc')])
                    xs = [sb['xch0'], sb['xch1']][oc % 2]
                    xsb = [b('xch0'), b('xch1')][oc % 2]
                    S.dma('sp', (lambda oc2=oc, xs2=xs: lambda h: h.dma_start(
                        out=xs2[...], in_=src_x[:, oc2, tt*CT:(tt+1)*CT]))(),
                        reads=[b('xT' if phase == 'f1' else 'x2T')], writes=[xsb])
                    S.comp('dve', (lambda xs2=xs, xoc2=xoc: lambda h: h.scalar_tensor_tensor(
                        out=xoc2[...], in0=sb['toc'][...], scalar=0.5, in1=xs2[...],
                        op0=ALU.mult, op1=ALU.add))(),
                        reads=[b('toc'), xsb], writes=[xb])
                    S.dma('sp', (lambda oc2=oc, xoc2=xoc: lambda h: h.dma_start(
                        out=dst_ap[:, oc2, tt*CT:(tt+1)*CT], in_=xoc2[...]))(),
                        reads=[xb], writes=[b(dstn)])
                    if phase == 'f1':
                        S.comp('pe', (lambda oc2=oc, xoc2=xoc: lambda h: h.matmul(
                            ps['pa0'][0:1, :], sb['c_ones_col'][:, 0:1], xoc2[...],
                            start=(oc2 == 0), stop=(oc2 == NCH-1)))(),
                            reads=[xb, b('c_ones_col')], writes=[b('pa0')])
                        sq = sb['xsq0']
                        sqb = b('xsq0')
                        S.comp('act', (lambda xoc2=xoc, sq2=sq: lambda h: h.activation(
                            out=sq2[...], in_=xoc2[...], func=AF.Square))(),
                            reads=[xb], writes=[sqb])
                        S.comp('pe', (lambda oc2=oc, sq2=sq: lambda h: h.matmul(
                            ps['pa1'][0:1, :], sb['c_ones_col'][:, 0:1], sq2[...],
                            start=(oc2 == 0), stop=(oc2 == NCH-1)))(),
                            reads=[sqb, b('c_ones_col')], writes=[b('pa1')])

    def emit_ln2_and_export(self, tt):
        S, b = self.S, self.buf
        dr, sb, ps = self.dr, self.sb, self.ps
        S.comp('dve', lambda h: h.tensor_scalar(out=sb['m_row'][...], in0=ps['pa0'][0:1, :],
                                                scalar1=1.0/HD, scalar2=None, op0=ALU.mult),
               reads=[b('pa0')], writes=[b('m_row')])
        S.comp('dve', lambda h: h.tensor_scalar(out=sb['t_row'][...], in0=ps['pa1'][0:1, :],
                                                scalar1=1.0/HD, scalar2=None, op0=ALU.mult),
               reads=[b('pa1')], writes=[b('t_row')])
        self._ln_rows()
        S.dma('sp', (lambda tt2=tt: lambda h: h.dma_start(out=dr['mr_in'].ap()[0:1, tt2*CT:(tt2+1)*CT],
                                                          in_=sb['mr_row'][...]))(),
              reads=[b('mr_row')], writes=[b('mr_in')])
        src = dr['x1T'].ap().rearrange("(c p) t -> p c t", p=128)
        xr2ap = dr[f'xr2_in{tt}'].ap().rearrange("(c p) t -> p c t", p=128)
        for c in range(NCH):
            xs = [sb['xch0'], sb['xch1']][c % 2]
            xsb = [b('xch0'), b('xch1')][c % 2]
            S.dma('sp', (lambda c2=c, xs2=xs: lambda h: h.dma_start(
                out=xs2[...], in_=src[:, c2, tt*CT:(tt+1)*CT]))(),
                reads=[b('x1T')], writes=[xsb])
            S.comp('dve', (lambda c2=c, xs2=xs: lambda h: h.tensor_tensor(
                out=sb['xr'][:, c2, :], in0=xs2[...], in1=ps['pb1'][...], op=ALU.mult))(),
                reads=[xsb, b('pb1')], writes=[b('xr')])
            S.dma('sp', (lambda c2=c: lambda h: h.dma_start(
                out=xr2ap[:, c2, :], in_=sb['xr'][:, c2, :]))(),
                reads=[b('xr')], writes=[b(f'xr2_in{tt}')])
        S.cc((lambda tt2=tt: lambda h: h.collective_compute(
            "AllGather", ALU.bypass,
            replica_groups=[[0, 1], [2, 3], [4, 5], [6, 7]],
            ins=[dr[f'xr2_in{tt2}'].ap().opt()],
            outs=[dr[f'xr2_out{tt2}'].ap().opt()]))(),
            reads=[b(f'xr2_in{tt}')], writes=[b(f'xr2_out{tt}')])

    def emit_mr_cc(self):
        S, b = self.S, self.buf
        dr = self.dr
        S.cc(lambda h: h.collective_compute(
            "AllGather", ALU.bypass,
            replica_groups=[[0, 1], [2, 3], [4, 5], [6, 7]],
            ins=[dr['mr_in'].ap().opt()],
            outs=[dr['mr_out'].ap().opt()]),
            reads=[b('mr_in')], writes=[b('mr_out')])
        if self.dbg:
            for tt in range(self.nt):
                S.dma('pool', (lambda tt2=tt: lambda h: h.dma_start(
                    out=dr['dbg_xr2'].ap()[:, tt2*CT:(tt2+1)*CT], in_=dr[f'xr2_out{tt2}'].ap()))(),
                    reads=[b(f'xr2_out{tt}')], writes=[b('dbg_xr2')])
            S.dma('pool', lambda h: h.dma_start(out=dr['dbg_mr'].ap(), in_=dr['mr_out'].ap()),
                  reads=[b('mr_out')], writes=[b('dbg_mr')])


    def emit_attn_ct(self, g, sub=99):
        S, b = self.S, self.buf
        dr, sb, ps = self.dr, self.sb, self.ps
        first = (g == 0)
        half, tt = g % 2, g // 2
        src = dr[f'xr2_out{tt}'].ap()[half*HD:(half+1)*HD, :].rearrange("(c p) t -> p c t", p=128)
        S.dma('pool', (lambda s2=src: lambda h: h.dma_start(out=sb['xr2t'][...], in_=s2))(),
              reads=[b(f'xr2_out{tt}')], writes=[b('xr2t')])
        S.dma('sp', (lambda half2=half, tt2=tt: lambda h: h.dma_start(
            out=sb['amr_row'][...], in_=dr['mr_out'].ap()[half2:half2+1, tt2*CT:(tt2+1)*CT]))(),
            reads=[b('mr_out')], writes=[b('amr_row')])
        for p0 in [0, 32, 64]:
            S.comp('dve', (lambda p2=p0: lambda h: h.tensor_scalar(
                out=sb['amrn_bf'][p2:p2+1, :], in0=sb['amr_row'][...],
                scalar1=-1.0, scalar2=None, op0=ALU.mult))(),
                reads=[b('amr_row')], writes=[b('amrn_bf')])
        jobs = [('wq_c', 'wqT', 'swq_row', 0, 'qc'), ('wk_c', 'wkT', 'swk_row', 1, 'kc'),
                ('wv_c', 'wvT', 'swv_row', 2, 'vc')]
        wslots = [('wp0', sb['wp0']), ('wp1', sb['wp1'])]
        ups = [('up0', sb['up0']), ('up0', sb['up0'])]
        nio = 0
        for ji, (wc, wsrc, swr, ti, cn) in enumerate(jobs):
            for oc in range(8):
                sn, slot = wslots[nio % 2]
                un, ut = ups[nio % 2]
                nio += 1
                cap = dr[wc].ap().rearrange("(r p) n -> p r n", p=128)[:, :, oc*128:(oc+1)*128]
                if first:
                    sap = dr[wsrc].ap().rearrange("(r p) n -> p r n", p=128)[:, :, oc*128:(oc+1)*128]
                    S.dma('pool', (lambda d2=slot, s2=sap: lambda h: h.dma_start(out=d2[...], in_=s2))(),
                          reads=[b(wsrc)], writes=[b(sn)])
                    S.dma('sp', (lambda d2=cap, s2=slot: lambda h: h.dma_start(out=d2, in_=s2[...]))(),
                          reads=[b(sn)], writes=[b(wc)])
                else:
                    S.dma('sp', (lambda d2=slot, s2=cap: lambda h: h.dma_start(out=d2[...], in_=s2))(),
                          reads=[b(wc)], writes=[b(sn)])
                bank = ['pa0', 'pa1'][oc % 2]
                for c2 in range(NCH):
                    S.comp('pe', (lambda s2=slot, c3=c2, bank2=bank: lambda h: h.matmul(
                        ps[bank2][...], s2[:, c3, :], sb['xr2t'][:, c3, :],
                        start=(c3 == 0), stop=False))(),
                        reads=[b(sn), b('xr2t')], writes=[b(bank)])
                S.comp('pe', (lambda bank2=bank, swr2=swr, oc2=oc: lambda h: h.matmul(
                    ps[bank2][...], self.swsl(swr2, oc2),
                    sb['amrn_bf'][32*(oc2 % 3):32*(oc2 % 3)+1, :],
                    start=False, stop=True))(),
                    reads=[b(swr), b('amrn_bf')], writes=[b(bank)])
                if first:
                    S.comp('dve', (lambda u2=ut: lambda h: h.memset(u2[:, 0:3], 0.0))(),
                           writes=[b(un)])
                else:
                    S.comp('dve', (lambda u2=ut, t2=ti, o2=oc: lambda h: h.tensor_copy(
                        out=u2[:, 0:3], in_=sb['tls'][:, t2, o2, :]))(),
                        reads=[b('tls')], writes=[b(un)])
                S.comp('act', (lambda bank2=bank, u2=ut: lambda h: h.activation(
                    out=u2[:, 3:515], in_=ps[bank2][...], func=AF.Copy))(),
                    reads=[b(bank)], writes=[b(un)])
                S.comp('dve', (lambda u2=ut, t2=ti, o2=oc: lambda h: h.tensor_copy(
                    out=sb['tls'][:, t2, o2, :], in_=u2[:, 512:515]))(),
                    reads=[b(un)], writes=[b('tls')])
                wv = lambda k2, t2=ti, h2=oc: sb['c_convw'][:, t2, h2, k2:k2+1]
                S.comp('dve', (lambda u2=ut, w0=wv(0): lambda h: h.tensor_scalar(
                    out=sb['ct1'][...], in0=u2[:, 0:512], scalar1=w0, scalar2=None, op0=ALU.mult))(),
                    reads=[b(un), b('c_convw')], writes=[b('ct1')])
                S.comp('dve', (lambda u2=ut, w1=wv(1): lambda h: h.scalar_tensor_tensor(
                    out=sb['ct2'][...], in0=u2[:, 1:513], scalar=w1, in1=sb['ct1'][...],
                    op0=ALU.mult, op1=ALU.add))(),
                    reads=[b(un), b('ct1'), b('c_convw')], writes=[b('ct2')])
                S.comp('dve', (lambda u2=ut, w2=wv(2): lambda h: h.scalar_tensor_tensor(
                    out=sb['ct1'][...], in0=u2[:, 2:514], scalar=w2, in1=sb['ct2'][...],
                    op0=ALU.mult, op1=ALU.add))(),
                    reads=[b(un), b('ct2'), b('c_convw')], writes=[b('ct1')])
                S.comp('dve', (lambda u2=ut, w3=wv(3): lambda h: h.scalar_tensor_tensor(
                    out=sb['ct2'][...], in0=u2[:, 3:515], scalar=w3, in1=sb['ct1'][...],
                    op0=ALU.mult, op1=ALU.add))(),
                    reads=[b(un), b('ct1'), b('c_convw')], writes=[b('ct2')])
                S.comp('act', (lambda cn2=cn, o2=oc: lambda h: h.activation(
                    out=sb[cn2][:, o2, :], in_=sb['ct2'][...], func=AF.Silu))(),
                    reads=[b('ct2')], writes=[b(cn)])
        sn, slot = wslots[nio % 2]; nio += 1
        capg = dr['wg1_c'].ap().rearrange("(r p) n -> p r n", p=128)
        if first:
            sapg = dr['wg1T'].ap().rearrange("(r p) n -> p r n", p=128)
            S.dma('pool', (lambda d2=slot, s2=sapg: lambda h: h.dma_start(out=d2[...], in_=s2))(),
                  reads=[b('wg1T')], writes=[b(sn)])
            S.dma('sp', (lambda d2=capg, s2=slot: lambda h: h.dma_start(out=d2, in_=s2[...]))(),
                  reads=[b(sn)], writes=[b('wg1_c')])
        else:
            S.dma('sp', (lambda d2=slot, s2=capg: lambda h: h.dma_start(out=d2[...], in_=s2))(),
                  reads=[b('wg1_c')], writes=[b(sn)])
        for c2 in range(NCH):
            S.comp('pe', (lambda s2=slot, c3=c2: lambda h: h.matmul(
                ps['pa0'][...], s2[:, c3, :], sb['xr2t'][:, c3, :], start=(c3 == 0), stop=False))(),
                reads=[b(sn), b('xr2t')], writes=[b('pa0')])
        S.comp('pe', lambda h: h.matmul(ps['pa0'][...], self.swsl('swg1_row', 0),
                                        sb['amrn_bf'][0:1, :], start=False, stop=True),
               reads=[b('swg1_row'), b('amrn_bf')], writes=[b('pa0')])
        S.comp('act', lambda h: h.activation(out=sb['glt'][...], in_=ps['pa0'][...], func=AF.Copy),
               reads=[b('pa0')], writes=[b('glt')])
        if sub < 1:
            return
        for h_ in range(8):
            S.comp('pe', (lambda h2=h_: lambda h: h.matmul(
                ps['pa0'][...], sb['c_wsv'][...], sb['kc'][:, h2, :], start=True, stop=True))(),
                reads=[b('c_wsv'), b('kc')], writes=[b('pa0')])
            S.comp('act', lambda h: h.activation(out=sb['vh_h'][...], in_=ps['pa0'][...], func=AF.Copy),
                   reads=[b('pa0')], writes=[b('vh_h')])
            S.comp('dve', (lambda h2=h_: lambda h: h.tensor_tensor(
                out=sb['e_h'][...], in0=sb['vh_h'][...], in1=sb['vc'][:, h2, :], op=ALU.subtract))(),
                reads=[b('vh_h'), b('vc')], writes=[b('e_h')])
            stats = [
                ('act', lambda h: h.activation(out=sb['stsc0'][...], in_=sb['e_h'][...], func=AF.Square), ['e_h'], 'stsc0'),
                ('act', lambda h: h.activation(out=sb['stsc0'][...], in_=sb['e_h'][...], func=AF.Abs), ['e_h'], 'stsc0'),
                ('dve', (lambda h2=h_: lambda h: h.tensor_tensor(out=sb['stsc0'][...], in0=sb['vh_h'][...], in1=sb['vc'][:, h2, :], op=ALU.mult))(), ['vh_h', 'vc'], 'stsc0'),
                ('act', lambda h: h.activation(out=sb['stsc0'][...], in_=sb['vh_h'][...], func=AF.Square), ['vh_h'], 'stsc0'),
                ('act', (lambda h2=h_: lambda h: h.activation(out=sb['stsc0'][...], in_=sb['vc'][:, h2, :], func=AF.Square))(), ['vc'], 'stsc0'),
                ('act', (lambda h2=h_: lambda h: h.activation(out=sb['stsc0'][...], in_=sb['kc'][:, h2, :], func=AF.Square))(), ['kc'], 'stsc0'),
                ('act', (lambda h2=h_: lambda h: h.activation(out=sb['stsc0'][...], in_=sb['qc'][:, h2, :], func=AF.Square))(), ['qc'], 'stsc0'),
            ]
            for si, (eng, fn, rd, scr) in enumerate(stats):
                S.comp(eng, fn, reads=[b(x) for x in rd], writes=[b(scr)])
                S.comp('pe', (lambda si2=si, scr2=scr: lambda h: h.matmul(
                    ps['pa1'][0:8, :], sb['c_zc'][:, si2, 0:8], sb[scr2][...],
                    start=(si2 == 0), stop=(si2 == 6)))(),
                    reads=[b(scr), b('c_zc')], writes=[b('pa1')])
            S.comp('act', lambda h: h.activation(out=sb['tmp8'][...], in_=ps['pa1'][0:8, :], func=AF.Copy),
                   reads=[b('pa1')], writes=[b('tmp8')])
            S.dma('pool', (lambda h2=h_: lambda h: h.dma_start(
                out=sb['st_all'][h2:h2+1, :, :], in_=sb['tmp8'][0:7, :]))(),
                reads=[b('tmp8')], writes=[b('st_all')])
        if sub < 2:
            return
        ec = self.constcol(EPS_STAT)
        l2c = self.constcol(L2_EPS)
        def act8(out, in_ap, func, scale=None, bias=None, rd=(), wr=(), bias_buf=None):
            kw = {}
            if scale is not None: kw['scale'] = scale
            if bias is not None: kw['bias'] = bias
            rds = list(rd) + ([bias_buf] if bias_buf else [])
            S.comp('act', (lambda o2=out, i2=in_ap, f2=func, kw2=dict(kw):
                           lambda h: h.activation(out=o2, in_=i2, func=f2, **kw2))(),
                   reads=[b(x) for x in rds], writes=[b(x) for x in wr])
        act8(sb['sl2t'][...], sb['st_all'][:, 0, :], AF.Ln, bias=ec[0:8, 0:1], rd=['st_all'], wr=['sl2t'], bias_buf=self.constname(EPS_STAT))
        act8(sb['sl2t'][...], sb['sl2t'][...], AF.Exp, scale=0.5, rd=['sl2t'], wr=['sl2t'])
        act8(sb['den1'][...], sb['st_all'][:, 3, :], AF.Ln, bias=ec[0:8, 0:1], rd=['st_all'], wr=['den1'], bias_buf=self.constname(EPS_STAT))
        act8(sb['den1'][...], sb['den1'][...], AF.Exp, scale=0.5, rd=['den1'], wr=['den1'])
        act8(sb['den2'][...], sb['st_all'][:, 4, :], AF.Ln, bias=ec[0:8, 0:1], rd=['st_all'], wr=['den2'], bias_buf=self.constname(EPS_STAT))
        act8(sb['den2'][...], sb['den2'][...], AF.Exp, scale=0.5, rd=['den2'], wr=['den2'])
        S.comp('dve', lambda h: h.tensor_tensor(out=sb['den1'][...], in0=sb['den1'][...],
                                                in1=sb['den2'][...], op=ALU.mult),
               reads=[b('den1'), b('den2')], writes=[b('den1')])
        S.comp('dve', lambda h: h.tensor_scalar(out=sb['den1'][...], in0=sb['den1'][...],
                                                scalar1=EPS_STAT, scalar2=None, op0=ALU.add),
               reads=[b('den1')], writes=[b('den1')])
        S.comp('dve', lambda h: h.reciprocal(out=sb['den2'][...], in_=sb['den1'][...]),
               reads=[b('den1')], writes=[b('den2')])
        S.comp('dve', lambda h: h.tensor_tensor(out=sb['scost'][...], in0=sb['st_all'][:, 2, :],
                                                in1=sb['den2'][...], op=ALU.mult),
               reads=[b('st_all'), b('den2')], writes=[b('scost')])
        S.comp('dve', lambda h: h.tensor_scalar(out=sb['scost'][...], in0=sb['scost'][...],
                                                scalar1=-1.0, scalar2=1.0, op0=ALU.mult, op1=ALU.add),
               reads=[b('scost')], writes=[b('scost')])
        S.dma('pool', lambda h: h.dma_start(out=sb['s3'][0:1, :], in_=sb['sl2t'][...]),
              reads=[b('sl2t')], writes=[b('s3')])
        S.dma('pool', lambda h: h.dma_start(out=sb['s3'][1:2, :], in_=sb['st_all'][:, 1, :]),
              reads=[b('st_all')], writes=[b('s3')])
        S.dma('pool', lambda h: h.dma_start(out=sb['s3'][2:3, :], in_=sb['scost'][...]),
              reads=[b('scost')], writes=[b('s3')])
        for h_ in range(8):
            for w1n, w2n, bnk, b1 in [('c_wb1', 'c_wb2z', 'pa1', 'c_bb1'),
                                      ('c_wa1', 'c_wa2z', 'pa2', 'c_ba1')]:
                S.comp('pe', (lambda h2=h_, w1=w1n: lambda h: h.matmul(
                    ps['pa0'][0:32, :], sb[w1][...], sb['s3'][:, h2*CT:(h2+1)*CT],
                    start=True, stop=True))(),
                    reads=[b(w1n), b('s3')], writes=[b('pa0')])
                S.comp('act', (lambda b1n=b1: lambda h: h.activation(
                    out=sb['hb'][...], in_=ps['pa0'][0:32, :], func=AF.Silu, bias=sb[b1n][...]))(),
                    reads=[b('pa0'), b(b1)], writes=[b('hb')])
                S.comp('pe', (lambda h2=h_, w2=w2n, bnk2=bnk: lambda h: h.matmul(
                    ps[bnk2][0:8, :], sb[w2][:, h2*8:(h2+1)*8], sb['hb'][...],
                    start=(h2 == 0), stop=(h2 == 7)))(),
                    reads=[b(w2n), b('hb')], writes=[b(bnk)])
        S.comp('act', lambda h: h.activation(out=sb['beta8'][...], in_=ps['pa1'][0:8, :],
                                             func=AF.Sigmoid, bias=sb['c_bb2'][...]),
               reads=[b('pa1'), b('c_bb2')], writes=[b('beta8')])
        S.comp('act', lambda h: h.activation(out=sb['amp8'][...], in_=ps['pa2'][0:8, :],
                                             func=AF.Identity, bias=sb['c_ba2'][...]),
               reads=[b('pa2'), b('c_ba2')], writes=[b('amp8')])
        act8(sb['rsqk8'][...], sb['st_all'][:, 5, :], AF.Ln, rd=['st_all'], wr=['rsqk8'])
        act8(sb['rsqk8'][...], sb['rsqk8'][...], AF.Exp, scale=0.5, rd=['rsqk8'], wr=['rsqk8'])
        S.comp('dve', lambda h: h.tensor_scalar(out=sb['rsqk8'][...], in0=sb['rsqk8'][...],
                                                scalar1=1e-12, scalar2=None, op0=ALU.max),
               reads=[b('rsqk8')], writes=[b('rsqk8')])
        S.comp('dve', lambda h: h.reciprocal(out=sb['rsqk8'][...], in_=sb['rsqk8'][...]),
               reads=[b('rsqk8')], writes=[b('rsqk8')])
        S.comp('dve', lambda h: h.tensor_tensor(out=sb['ampz8'][...], in0=sb['amp8'][...],
                                                in1=sb['rsqk8'][...], op=ALU.mult),
               reads=[b('amp8'), b('rsqk8')], writes=[b('ampz8')])
        act8(sb['rkn8'][...], sb['st_all'][:, 5, :], AF.Ln, bias=l2c[0:8, 0:1], rd=['st_all'], wr=['rkn8'], bias_buf=self.constname(L2_EPS))
        act8(sb['rkn8'][...], sb['rkn8'][...], AF.Exp, scale=-0.5, rd=['rkn8'], wr=['rkn8'])
        act8(sb['rqn8'][...], sb['st_all'][:, 6, :], AF.Ln, bias=l2c[0:8, 0:1], rd=['st_all'], wr=['rqn8'], bias_buf=self.constname(L2_EPS))
        qkc2 = self.constcol(float(-0.5*np.log(DK)))
        act8(sb['rqn8'][...], sb['rqn8'][...], AF.Exp, scale=-0.5, bias=qkc2[0:8, 0:1], rd=['rqn8'], wr=['rqn8'], bias_buf=self.constname(float(-0.5*np.log(DK))))
        for rsrc, rdst in [('ampz8', 'ampz8c'), ('rkn8', 'rkn8c'), ('rqn8', 'rqn8c')]:
            S.comp('dve', (lambda s2=rsrc, d2=rdst: lambda h: h.tensor_copy(
                out=sb[d2][...], in_=sb[s2][...]))(),
                reads=[b(rsrc)], writes=[b(rdst)])
        for cc in range(4):
            S.comp('pe', (lambda cc2=cc: lambda h: h.transpose(
                ps['pa2'][0:128, 0:8], sb['beta8'][:, cc2*128:(cc2+1)*128], sb['c_identf'][0:8, 0:8]))(),
                reads=[b('beta8'), b('c_identf')], writes=[b('pa2')])
            S.comp('dve', (lambda cc2=cc: lambda h: h.tensor_copy(
                out=sb['betaT'][:, cc2*8:(cc2+1)*8], in_=ps['pa2'][0:128, 0:8]))(),
                reads=[b('pa2')], writes=[b('betaT')])
        S.comp('dve', lambda h: h.tensor_scalar(out=sb['betaTn'][...], in0=sb['betaT'][...],
                                                scalar1=-1.0, scalar2=None, op0=ALU.mult),
               reads=[b('betaT')], writes=[b('betaTn')])
        if sub < 3:
            return
        for h_ in range(8):
            for rn, dst in [('ampz8c', 'ampb'), ('rkn8c', 'knb'), ('rqn8c', 'qnb')]:
                S.comp('pe', (lambda rn2=rn, h2=h_: lambda h: h.matmul(
                    ps['pa0'][...], sb['c_zrow8'][:, h2, :], sb[rn2][...],
                    start=True, stop=True))(),
                    reads=[b(rn), b('c_zrow8')], writes=[b('pa0')])
                S.comp('act', (lambda d2=dst: lambda h: h.activation(
                    out=sb[d2][...], in_=ps['pa0'][...], func=AF.Copy))(),
                    reads=[b('pa0')], writes=[b(dst)])
            S.comp('dve', (lambda h2=h_: lambda h: h.tensor_tensor(
                out=sb['zt'][...], in0=sb['kc'][:, h2, :], in1=sb['ampb'][...], op=ALU.mult))(),
                reads=[b('kc'), b('ampb')], writes=[b('zt')])
            dtb_h = sb['c_dtb'][:, h_:h_+1]
            S.comp('act', (lambda d2=dtb_h: lambda h: h.activation(
                out=sb['sp1'][...], in_=sb['zt'][...], func=AF.Relu, bias=d2))(),
                reads=[b('zt'), b('c_dtb')], writes=[b('sp1')])
            S.comp('act', (lambda d2=dtb_h: lambda h: h.activation(
                out=sb['ab'][...], in_=sb['zt'][...], func=AF.Abs, bias=d2))(),
                reads=[b('zt'), b('c_dtb')], writes=[b('ab')])
            S.comp('act', lambda h: h.activation(out=sb['ab'][...], in_=sb['ab'][...],
                                                 func=AF.Exp, scale=-1.0),
                   reads=[b('ab')], writes=[b('ab')])
            onec = self.constcol(1.0)
            S.comp('act', (lambda o2=onec: lambda h: h.activation(
                out=sb['ab'][...], in_=sb['ab'][...], func=AF.Ln, bias=o2[:, 0:1]))(),
                reads=[b('ab'), b(self.constname(1.0))], writes=[b('ab')])
            S.comp('dve', lambda h: h.tensor_tensor(out=sb['spt'][...], in0=sb['sp1'][...],
                                                    in1=sb['ab'][...], op=ALU.add),
                   reads=[b('sp1'), b('ab')], writes=[b('spt')])
            S.comp('dve', (lambda h2=h_: lambda h: h.tensor_scalar(
                out=sb['gt'][...], in0=sb['spt'][...], scalar1=sb['c_negea'][:, h2:h2+1],
                scalar2=None, op0=ALU.mult))(),
                reads=[b('spt'), b('c_negea')], writes=[b('gt')])
            S.comp('act', lambda h: h.activation(out=sb['D_h'][...], in_=sb['gt'][...], func=AF.Exp),
                   reads=[b('gt')], writes=[b('D_h')])
            S.comp('dve', lambda h: h.tensor_tensor_scan(out=sb['G_h'][...], data0=sb['c_ones_f'][...],
                                                         data1=sb['gt'][...], initial=0.0,
                                                         op0=ALU.mult, op1=ALU.add),
                   reads=[b('c_ones_f'), b('gt')], writes=[b('G_h')])
            S.comp('dve', (lambda h2=h_: lambda h: h.tensor_tensor(
                out=sb['kn_h'][...], in0=sb['kc'][:, h2, :], in1=sb['knb'][...], op=ALU.mult))(),
                reads=[b('kc'), b('knb')], writes=[b('kn_h')])
            S.comp('dve', (lambda h2=h_: lambda h: h.tensor_tensor(
                out=sb['qn_h'][...], in0=sb['qc'][:, h2, :], in1=sb['qnb'][...], op=ALU.mult))(),
                reads=[b('qc'), b('qnb')], writes=[b('qn_h')])
            S.comp('dve', lambda h: h.tensor_tensor(out=sb['p4'][...], in0=sb['qn_h'][...],
                                                    in1=sb['kn_h'][...], op=ALU.mult),
                   reads=[b('qn_h'), b('kn_h')], writes=[b('p4')])
            S.comp('pe', lambda h: h.matmul(ps['pa1'][0:16, :], sb['c_zc'][:, 8, :], sb['p4'][...],
                                            start=True, stop=False),
                   reads=[b('c_zc'), b('p4')], writes=[b('pa1')])
            pcur = 'D_h'
            for dd in range(1, 8):
                if dd >= 2:
                    pnew = ['pc0', 'pc1'][dd % 2]
                    S.comp('dve', (lambda d2=dd, pc=pcur, pn=pnew: lambda h: h.tensor_tensor(
                        out=sb[pn][:, d2-1:512], in0=sb[pc][:, d2-1:512],
                        in1=sb['D_h'][:, 0:512-(d2-1)], op=ALU.mult))(),
                        reads=[b(pcur), b('D_h')], writes=[b(pnew)])
                    pcur = pnew
                S.comp('dve', (lambda d2=dd, pc=pcur: lambda h: h.tensor_tensor(
                    out=sb['ttb'][:, d2:512], in0=sb[pc][:, d2:512],
                    in1=sb['kn_h'][:, 0:512-d2], op=ALU.mult))(),
                    reads=[b(pcur), b('kn_h')], writes=[b('ttb')])
                S.comp('dve', (lambda d2=dd: lambda h: h.tensor_tensor(
                    out=sb['p3'][:, d2:512], in0=sb['ttb'][:, d2:512],
                    in1=sb['kn_h'][:, d2:512], op=ALU.mult))(),
                    reads=[b('ttb'), b('kn_h')], writes=[b('p3')])
                S.comp('pe', (lambda d2=dd: lambda h: h.matmul(
                    ps['pa1'][0:16, :], sb['c_zc'][:, d2, :], sb['p3'][...],
                    start=False, stop=False))(),
                    reads=[b('c_zc'), b('p3')], writes=[b('pa1')])
                S.comp('dve', (lambda d2=dd: lambda h: h.tensor_tensor(
                    out=sb['p4'][:, d2:512], in0=sb['ttb'][:, d2:512],
                    in1=sb['qn_h'][:, d2:512], op=ALU.mult))(),
                    reads=[b('ttb'), b('qn_h')], writes=[b('p4')])
                S.comp('pe', (lambda d2=dd: lambda h: h.matmul(
                    ps['pa1'][0:16, :], sb['c_zc'][:, 8+d2, :], sb['p4'][...],
                    start=False, stop=(d2 == 7)))(),
                    reads=[b('c_zc'), b('p4')], writes=[b('pa1')])
            S.comp('act', lambda h: h.activation(out=sb['pmrow'][...], in_=ps['pa1'][0:16, :], func=AF.Copy),
                   reads=[b('pa1')], writes=[b('pmrow')])
            for cc in range(4):
                S.comp('pe', (lambda cc2=cc: lambda h: h.transpose(
                    ps['patb'][:, 0:16], sb['pmrow'][:, cc2*128:(cc2+1)*128], sb['c_ident'][0:16, 0:16]))(),
                    reads=[b('pmrow'), b('c_ident')], writes=[b('patb')])
                S.comp('dve', (lambda cc2=cc: lambda h: h.tensor_copy(
                    out=sb['TRt'][:, cc2, :], in_=ps['patb'][:, 0:16]))(),
                    reads=[b('patb')], writes=[b('TRt')])
            S.comp('pe', (lambda h2=h_: lambda h: h.matmul(
                ps['pa0'][...], sb['c_wg2'][:, h2, :], sb['glt'][...], start=True, stop=True))(),
                reads=[b('c_wg2'), b('glt')], writes=[b('pa0')])
            S.comp('act', (lambda h2=h_: lambda h: h.activation(
                out=sb['sgate'][:, h2, :], in_=ps['pa0'][...], func=AF.Sigmoid,
                bias=sb['c_bg2'][:, h2:h2+1]))(),
                reads=[b('pa0'), b('c_bg2')], writes=[b('sgate')])
            if sub >= 4:
                for cc in range(4):
                    self.emit_chunk(g, h_, cc, csub=self.csub)
        if sub < 5:
            return
        woap = dr['wo_c'].ap().rearrange("(r p) n -> p r n", p=128)
        ppap = dr[f'pp_in{tt}'].ap().rearrange("(s c p) t -> s p c t", s=2, p=128)
        wopair = [('wosl0', sb['wosl0']), ('wosl0', sb['wosl0'])]
        for oc in range(NCH):
            bank = ['pa0', 'pa1'][oc % 2]
            wn, wt = wopair[oc % 2]
            if first:
                wsap = dr['woT'].ap().rearrange("(r p) n -> p r n", p=128)[:, :, oc*128:(oc+1)*128]
                S.dma('pool', (lambda d2=wt, s2=wsap: lambda h: h.dma_start(out=d2[...], in_=s2))(),
                      reads=[b('woT')], writes=[b(wn)])
                cap2 = woap[:, :, oc*128:(oc+1)*128]
                S.dma('sp', (lambda d2=cap2, s2=wt: lambda h: h.dma_start(out=d2, in_=s2[...]))(),
                      reads=[b(wn)], writes=[b('wo_c')])
            else:
                cap2 = woap[:, :, oc*128:(oc+1)*128]
                S.dma('sp', (lambda d2=wt, s2=cap2: lambda h: h.dma_start(out=d2[...], in_=s2))(),
                      reads=[b('wo_c')], writes=[b(wn)])
            for r in range(8):
                S.comp('pe', (lambda r2=r, bank2=bank, wt2=wt: lambda h: h.matmul(
                    ps[bank2][...], wt2[:, r2, :], sb['o_ct'][:, r2, :],
                    start=(r2 == 0), stop=(r2 == 7)))(),
                    reads=[b(wn), b('o_ct')], writes=[b(bank)])
            pn = ['zt', 'sp1'][oc % 2]
            if oc % 2 == 0:
                S.comp('act', (lambda bank2=bank, pn2=pn: lambda h: h.activation(
                    out=sb[pn2][...], in_=ps[bank2][...], func=AF.Copy))(),
                    reads=[b(bank)], writes=[b(pn)])
            else:
                S.comp('dve', (lambda bank2=bank, pn2=pn: lambda h: h.tensor_copy(
                    out=sb[pn2][...], in_=ps[bank2][...]))(),
                    reads=[b(bank)], writes=[b(pn)])
            S.dma('sp', (lambda half2=half, oc2=oc, pn2=pn: lambda h: h.dma_start(
                out=ppap[half2, :, oc2, :], in_=sb[pn2][...]))(),
                reads=[b(pn)], writes=[b(f'pp_in{tt}')])

    def emit_chunk(self, g, h_, cc, csub=99):
        S, b = self.S, self.buf
        dr, sb, ps = self.dr, self.sb, self.ps
        s0 = cc * 128
        sl = slice(s0, s0 + 128)
        Ssrc = 'Sfp0'
        Sdst = 'Sfp0'
        seq_first = (g == 0 and cc == 0)
        if cc == 0:
            colA_src = 'c_zero'
        else:
            S.comp('dve', (lambda s2=s0: lambda h: h.tensor_scalar(
                out=sb['colA'][...], in0=sb['G_h'][:, s2-1:s2], scalar1=-1.0, scalar2=None,
                op0=ALU.mult))(),
                reads=[b('G_h')], writes=[b('colA')])
            colA_src = 'colA'
        S.comp('act', (lambda s2=sl, ca=colA_src: lambda h: h.activation(
            out=sb['lv_ae'][...], in_=sb['G_h'][:, s2], func=AF.Exp, bias=sb[ca][:, 0:1]))(),
            reads=[b('G_h'), b(colA_src)], writes=[b('lv_ae')])
        S.comp('dve', (lambda s2=sl: lambda h: h.tensor_tensor(
            out=sb['m_W'][...], in0=sb['kn_h'][:, s2], in1=sb['lv_ae'][...], op=ALU.mult))(),
            reads=[b('kn_h'), b('lv_ae')], writes=[b('m_W')])
        S.comp('dve', (lambda s2=sl: lambda h: h.tensor_tensor(
            out=sb['m_qt'][...], in0=sb['qn_h'][:, s2], in1=sb['lv_ae'][...], op=ALU.mult))(),
            reads=[b('qn_h'), b('lv_ae')], writes=[b('m_qt')])
        S.comp('act', (lambda s2=sl, e2=s0+127: lambda h: h.activation(
            out=sb['lv_be'][...], in_=sb['G_h'][:, s2], func=AF.Exp, scale=-1.0,
            bias=sb['G_h'][:, e2:e2+1]))(),
            reads=[b('G_h')], writes=[b('lv_be')])
        S.comp('dve', (lambda s2=sl: lambda h: h.tensor_tensor(
            out=sb['m_Wb'][...], in0=sb['kn_h'][:, s2], in1=sb['lv_be'][...], op=ALU.mult))(),
            reads=[b('kn_h'), b('lv_be')], writes=[b('m_Wb')])
        S.comp('act', (lambda e2=s0+127, ca=colA_src: lambda h: h.activation(
            out=sb['lamc'][...], in_=sb['G_h'][:, e2:e2+1], func=AF.Exp, bias=sb[ca][:, 0:1]))(),
            reads=[b('G_h'), b(colA_src)], writes=[b('lamc')])
        S.comp('pe', (lambda h2=h_, s2=sl: lambda h: h.transpose(
            ps['patb'][:, 0:128], sb['vc'][:, h2, s2], sb['c_ident'][...]))(),
            reads=[b('vc'), b('c_ident')], writes=[b('patb')])
        S.comp('dve', (lambda h2=h_, cc2=cc: lambda h: h.tensor_scalar(
            out=sb['m_bV'][...], in0=ps['patb'][:, 0:128], scalar1=sb['betaT'][:, cc2*8+h2:cc2*8+h2+1],
            scalar2=None, op0=ALU.mult))(),
            reads=[b('patb'), b('betaT')], writes=[b('m_bV')])
        S.comp('pe', lambda h: h.transpose(ps['patb'][:, 128:256], sb['m_Wb'][...], sb['c_ident'][...]),
               reads=[b('m_Wb'), b('c_ident')], writes=[b('patb')])
        S.comp('dve', lambda h: h.tensor_copy(out=sb['m_WbT'][...], in_=ps['patb'][:, 128:256]),
               reads=[b('patb')], writes=[b('m_WbT')])
        if csub < 1:
            return
        asm_lv = csub >= 22 or csub < 20
        asm_ab = csub >= 23 or csub < 20
        asm_mb = csub >= 24 or csub < 20
        asm_ev = csub >= 25 or csub < 20
        S.comp('pe', lambda h: h.matmul(ps['pa2'][:, 0:256], sb['c_zrow'][0:1, 0:128],
                                        sb['c_zrow'][0:1, 0:256], start=True, stop=False),
               reads=[b('c_zrow')], writes=[b('pa2')])
        nlast = [None]
        for lv in ([64, 32, 16, 8] if (csub not in (13, 14) and asm_lv) else []):
            ng = 128 // (2*lv)
            gc3 = sb['G_h'][:, sl].rearrange("p (a q) -> p a q", q=2*lv)
            refb = gc3[:, :, lv-1:lv].broadcast_to((128, ng, 2*lv))
            S.comp('dve', (lambda g3=gc3, r3=refb: lambda h: h.tensor_tensor(
                out=sb['lv_ga'][...].rearrange("p (a q) -> p a q", q=g3.shape[2]),
                in0=g3, in1=r3, op=ALU.subtract))(),
                reads=[b('G_h')], writes=[b('lv_ga')])
            S.comp('dve', lambda h: h.tensor_scalar(out=sb['lv_gb'][...], in0=sb['lv_ga'][...],
                                                    scalar1=-1.0, scalar2=0.0, op0=ALU.mult, op1=ALU.min),
                   reads=[b('lv_ga')], writes=[b('lv_gb')])
            S.comp('dve', lambda h: h.tensor_scalar(out=sb['lv_ga'][...], in0=sb['lv_ga'][...],
                                                    scalar1=0.0, scalar2=None, op0=ALU.min),
                   reads=[b('lv_ga')], writes=[b('lv_ga')])
            S.comp('act', lambda h: h.activation(out=sb['lv_ae'][...], in_=sb['lv_ga'][...], func=AF.Exp),
                   reads=[b('lv_ga')], writes=[b('lv_ae')])
            S.comp('act', lambda h: h.activation(out=sb['lv_be'][...], in_=sb['lv_gb'][...], func=AF.Exp),
                   reads=[b('lv_gb')], writes=[b('lv_be')])
            S.comp('dve', (lambda s2=sl: lambda h: h.tensor_tensor(
                out=sb['lv_ak'][...], in0=sb['kn_h'][:, s2], in1=sb['lv_ae'][...], op=ALU.mult))(),
                reads=[b('kn_h'), b('lv_ae')], writes=[b('lv_ak')])
            S.comp('dve', (lambda s2=sl: lambda h: h.tensor_tensor(
                out=sb['lv_aq'][...], in0=sb['qn_h'][:, s2], in1=sb['lv_ae'][...], op=ALU.mult))(),
                reads=[b('qn_h'), b('lv_ae')], writes=[b('lv_aq')])
            S.comp('dve', (lambda s2=sl: lambda h: h.tensor_tensor(
                out=sb['lv_bk'][...], in0=sb['kn_h'][:, s2], in1=sb['lv_be'][...], op=ALU.mult))(),
                reads=[b('kn_h'), b('lv_be')], writes=[b('lv_bk')])
            for a in range(ng):
                r0 = 2*lv*a + lv
                c0 = 2*lv*a
                if r0 % 32 == 0 and r0 != 96:
                    S.comp('pe', (lambda r2=r0, c2=c0, l2=lv: lambda h: h.matmul(
                        ps['pa2'][r2:r2+l2, c2:c2+l2], sb['lv_ak'][:, r2:r2+l2],
                        sb['lv_bk'][:, c2:c2+l2], start=False, stop=False))(),
                        reads=[b('lv_ak'), b('lv_bk')], writes=[b('pa2')])
                else:
                    zone = r0 % 64
                    win = r0 - zone
                    padn = f'padM_{zone}_{lv}'
                    S.comp('dve', (lambda p2=padn, z2=zone, r2=r0, l2=lv: lambda h: h.tensor_copy(
                        out=sb[p2][:, z2:z2+l2], in_=sb['lv_ak'][:, r2:r2+l2]))(),
                        reads=[b('lv_ak')], writes=[b(padn)])
                    S.comp('pe', (lambda p2=padn, w2=win, c2=c0, l2=lv: lambda h: h.matmul(
                        ps['pa2'][w2:w2+64, c2:c2+l2], sb[p2][...],
                        sb['lv_bk'][:, c2:c2+l2], start=False, stop=False))(),
                        reads=[b(padn), b('lv_bk')], writes=[b('pa2')])
                if c0 % 32 == 0 and c0 != 96:
                    S.comp('pe', (lambda r2=r0, c2=c0, l2=lv: lambda h: h.matmul(
                        ps['pa2'][c2:c2+l2, 128+r2:128+r2+l2], sb['lv_bk'][:, c2:c2+l2],
                        sb['lv_aq'][:, r2:r2+l2], start=False, stop=False))(),
                        reads=[b('lv_bk'), b('lv_aq')], writes=[b('pa2')])
                else:
                    zone = c0 % 64
                    win = c0 - zone
                    padn = f'padA_{zone}_{lv}'
                    S.comp('dve', (lambda p2=padn, z2=zone, c2=c0, l2=lv: lambda h: h.tensor_copy(
                        out=sb[p2][:, z2:z2+l2], in_=sb['lv_bk'][:, c2:c2+l2]))(),
                        reads=[b('lv_bk')], writes=[b(padn)])
                    S.comp('pe', (lambda p2=padn, w2=win, r2=r0, l2=lv: lambda h: h.matmul(
                        ps['pa2'][w2:w2+64, 128+r2:128+r2+l2], sb[p2][...],
                        sb['lv_aq'][:, r2:r2+l2], start=False, stop=False))(),
                        reads=[b(padn), b('lv_aq')], writes=[b('pa2')])
        for dd in (range(0, 8) if (csub not in (11, 14) and asm_ab) else []):
            S.comp('dve', (lambda d2=dd, cc2=cc: lambda h: h.tensor_scalar(
                out=sb['diagt'][...], in0=sb['c_ident'][...], scalar1=sb['TRt'][:, cc2, 8+d2:9+d2],
                scalar2=None, op0=ALU.mult))(),
                reads=[b('c_ident'), b('TRt')], writes=[b('diagt')])
            shn_ap = (lambda d2=dd: (sb['c_ident'][...] if d2 == 0 else sb['c_shift'][:, d2-1, :]))()
            S.comp('pe', (lambda s2=shn_ap: lambda h: h.matmul(
                ps['pa2'][:, 128:256], s2, sb['diagt'][...], start=False, stop=False))(),
                reads=[b('c_shift' if dd > 0 else 'c_ident'), b('diagt')], writes=[b('pa2')])
        for dd in (range(1, 8) if (csub not in (11, 12, 14) and asm_mb) else []):
            S.comp('dve', (lambda d2=dd, cc2=cc: lambda h: h.tensor_scalar(
                out=sb['diagt'][...], in0=sb['c_ident'][...], scalar1=sb['TRt'][:, cc2, d2:d2+1],
                scalar2=None, op0=ALU.mult))(),
                reads=[b('c_ident'), b('TRt')], writes=[b('diagt')])
            S.comp('pe', (lambda d2=dd: lambda h: h.matmul(
                ps['pa2'][:, 0:128], sb['diagt'][...], sb['c_shift'][:, d2-1, :],
                start=False, stop=(d2 == 7)))(),
                reads=[b('diagt'), b('c_shift')], writes=[b('pa2')])
        if not asm_ev:
            return
        S.comp('dve', (lambda h2=h_, cc2=cc: lambda h: h.tensor_scalar(
            out=sb['m_N1'][...], in0=ps['pa2'][:, 0:128],
            scalar1=sb['betaTn'][:, cc2*8+h2:cc2*8+h2+1], scalar2=None, op0=ALU.mult))(),
            reads=[b('pa2'), b('betaTn')], writes=[b('m_N1')])
        S.comp('act', lambda h: h.activation(out=sb['m_At'][...], in_=ps['pa2'][:, 128:256], func=AF.Copy),
               reads=[b('pa2')], writes=[b('m_At')])
        S.comp('pe', lambda h: h.transpose(ps['patb'][:, 256:384], sb['m_N1'][...], sb['c_ident'][...]),
               reads=[b('m_N1'), b('c_ident')], writes=[b('patb')])
        S.comp('dve', lambda h: h.tensor_copy(out=sb['m_X1'][...], in_=ps['patb'][:, 256:384]),
               reads=[b('patb')], writes=[b('m_X1')])
        if csub < 2:
            return
        sqjobs = [(1, 2), (2, 4), (4, 8), (8, 16), (16, 32), (32, 64)]
        pb_i = 0
        for i, j in sqjobs:
            bank = ['pa0', 'pa1'][pb_i % 2]; pb_i += 1
            S.comp('pe', (lambda i2=i, bank2=bank: lambda h: h.matmul(
                ps[bank2][:, 0:128], sb[f'm_X{i2}'][...], sb[f'm_N{i2}'][...], start=True, stop=True))(),
                reads=[b(f'm_X{i}'), b(f'm_N{i}')], writes=[b(bank)])
            eng = 'act' if pb_i % 2 == 0 else 'dve'
            if eng == 'act':
                S.comp('act', (lambda j2=j, bank2=bank: lambda h: h.activation(
                    out=sb[f'm_N{j2}'][...], in_=ps[bank2][:, 0:128], func=AF.Copy))(),
                    reads=[b(bank)], writes=[b(f'm_N{j}')])
            else:
                S.comp('dve', (lambda j2=j, bank2=bank: lambda h: h.tensor_copy(
                    out=sb[f'm_N{j2}'][...], in_=ps[bank2][:, 0:128]))(),
                    reads=[b(bank)], writes=[b(f'm_N{j}')])
            if j < 64:
                bank = ['pa0', 'pa1'][pb_i % 2]; pb_i += 1
                S.comp('pe', (lambda i2=i, bank2=bank: lambda h: h.matmul(
                    ps[bank2][:, 0:128], sb[f'm_N{i2}'][...], sb[f'm_X{i2}'][...], start=True, stop=True))(),
                    reads=[b(f'm_N{i}'), b(f'm_X{i}')], writes=[b(bank)])
                eng = 'act' if pb_i % 2 == 0 else 'dve'
                if eng == 'act':
                    S.comp('act', (lambda j2=j, bank2=bank: lambda h: h.activation(
                        out=sb[f'm_X{j2}'][...], in_=ps[bank2][:, 0:128], func=AF.Copy))(),
                        reads=[b(bank)], writes=[b(f'm_X{j}')])
                else:
                    S.comp('dve', (lambda j2=j, bank2=bank: lambda h: h.tensor_copy(
                        out=sb[f'm_X{j2}'][...], in_=ps[bank2][:, 0:128]))(),
                        reads=[b(bank)], writes=[b(f'm_X{j}')])
        qprev = 'c_ident'
        for i in [1, 2, 4, 8, 16, 32, 64]:
            bank = ['pa0', 'pa1'][pb_i % 2]; pb_i += 1
            S.comp('pe', (lambda i2=i, q2=qprev, bank2=bank: lambda h: h.matmul(
                ps[bank2][:, 0:128], sb[f'm_N{i2}'][...], sb[q2][...], start=True, stop=False))(),
                reads=[b(f'm_N{i}'), b(qprev)], writes=[b(bank)])
            S.comp('pe', (lambda q2=qprev, bank2=bank: lambda h: h.matmul(
                ps[bank2][:, 0:128], sb['c_ident'][...], sb[q2][...], start=False, stop=True))(),
                reads=[b('c_ident'), b(qprev)], writes=[b(bank)])
            S.comp('dve' if pb_i % 2 == 0 else 'act',
                   (lambda bank2=bank, e2=('dve' if pb_i % 2 == 0 else 'act'): lambda h: (
                       h.tensor_copy(out=sb['m_Q'][...], in_=ps[bank2][:, 0:128]) if e2 == 'dve'
                       else h.activation(out=sb['m_Q'][...], in_=ps[bank2][:, 0:128], func=AF.Copy)))(),
                   reads=[b(bank)], writes=[b('m_Q')])
            qprev = 'm_Q'
        if csub < 3:
            return
        S.comp('pe', (lambda h2=h_: lambda h: h.matmul(
            ps['pa0'][:, 0:128], sb['m_W'][...], sb['Sbf'][:, h2, :], start=True, stop=True))(),
            reads=[b('m_W'), b('Sbf')], writes=[b('pa0')])
        S.comp('dve', (lambda h2=h_, cc2=cc: lambda h: h.scalar_tensor_tensor(
            out=sb['m_R'][...], in0=ps['pa0'][:, 0:128],
            scalar=sb['betaTn'][:, cc2*8+h2:cc2*8+h2+1], in1=sb['m_bV'][...],
            op0=ALU.mult, op1=ALU.add))(),
            reads=[b('pa0'), b('betaTn'), b('m_bV')], writes=[b('m_R')])
        S.comp('pe', lambda h: h.matmul(ps['pa1'][:, 0:128], sb['m_Q'][...], sb['m_R'][...],
                                        start=True, stop=True),
               reads=[b('m_Q'), b('m_R')], writes=[b('pa1')])
        S.comp('dve', lambda h: h.tensor_copy(out=sb['m_U'][...], in_=ps['pa1'][:, 0:128]),
               reads=[b('pa1')], writes=[b('m_U')])
        S.comp('pe', (lambda h2=h_: lambda h: h.matmul(
            ps['pa2'][:, 256:384], sb['m_qt'][...], sb['Sbf'][:, h2, :], start=True, stop=False))(),
            reads=[b('m_qt'), b('Sbf')], writes=[b('pa2')])
        S.comp('pe', lambda h: h.matmul(ps['pa2'][:, 256:384], sb['m_At'][...], sb['m_U'][...],
                                        start=False, stop=True),
               reads=[b('m_At'), b('m_U')], writes=[b('pa2')])
        S.comp('pe', lambda h: h.matmul(ps['pa0'][:, 0:128], sb['m_WbT'][...], sb['m_U'][...],
                                        start=True, stop=True),
               reads=[b('m_WbT'), b('m_U')], writes=[b('pa0')])
        S.comp('dve', (lambda h2=h_: lambda h: h.scalar_tensor_tensor(
            out=sb['Stmp'][...], in0=sb['Sfp0'][:, h2, :], scalar=sb['lamc'][:, 0:1],
            in1=ps['pa0'][:, 0:128], op0=ALU.mult, op1=ALU.add))(),
            reads=[b('Sfp0'), b('lamc'), b('pa0')], writes=[b('Stmp')])
        S.comp('dve', (lambda h2=h_: lambda h: h.tensor_copy(
            out=sb['Sfp0'][:, h2, :], in_=sb['Stmp'][...]))(),
            reads=[b('Stmp')], writes=[b('Sfp0')])
        S.comp('act', (lambda h2=h_: lambda h: h.activation(
            out=sb['Sbf'][:, h2, :], in_=sb['Stmp'][...], func=AF.Copy))(),
            reads=[b('Stmp')], writes=[b('Sbf')])
        if csub < 4:
            return
        S.comp('act', lambda h: h.activation(out=sb['otm'][...], in_=ps['pa2'][:, 256:384], func=AF.Copy),
               reads=[b('pa2')], writes=[b('otm')])
        S.comp('act', lambda h: h.activation(out=sb['diagt'][...], in_=sb['otm'][...], func=AF.Square),
               reads=[b('otm')], writes=[b('diagt')])
        S.comp('dve', lambda h: h.tensor_reduce(out=sb['rms_c'][...], in_=sb['diagt'][...],
                                                axis=AX.X, op=ALU.add),
               reads=[b('diagt')], writes=[b('rms_c')])
        S.comp('dve', lambda h: h.tensor_scalar(out=sb['rms_c'][...], in0=sb['rms_c'][...],
                                                scalar1=1.0/DV, scalar2=NORM_EPS,
                                                op0=ALU.mult, op1=ALU.add),
               reads=[b('rms_c')], writes=[b('rms_c')])
        S.comp('act', lambda h: h.activation(out=sb['rms_c'][...], in_=sb['rms_c'][...], func=AF.Ln),
               reads=[b('rms_c')], writes=[b('rms_c')])
        S.comp('act', lambda h: h.activation(out=sb['rms_c'][...], in_=sb['rms_c'][...],
                                             func=AF.Exp, scale=-0.5),
               reads=[b('rms_c')], writes=[b('rms_c')])
        S.comp('dve', lambda h: h.tensor_scalar(out=sb['otm2'][...], in0=sb['otm'][...],
                                                scalar1=sb['rms_c'][:, 0:1], scalar2=None, op0=ALU.mult),
               reads=[b('otm'), b('rms_c')], writes=[b('otm2')])
        S.comp('pe', lambda h: h.transpose(ps['patb'][:, 384:512], sb['otm2'][...], sb['c_ident'][...]),
               reads=[b('otm2'), b('c_ident')], writes=[b('patb')])
        S.comp('dve', (lambda h2=h_, s2=sl: lambda h: h.scalar_tensor_tensor(
            out=sb['o_ct'][:, h2, s2], in0=ps['patb'][:, 384:512], scalar=sb['c_normw'][:, 0:1],
            in1=sb['sgate'][:, h2, s2], op0=ALU.mult, op1=ALU.mult))(),
            reads=[b('patb'), b('c_normw'), b('sgate')], writes=[b('o_ct')])

    def emit_pp_cc(self, tt):
        S, b = self.S, self.buf
        dr = self.dr
        S.cc((lambda tt2=tt: lambda h: h.collective_compute(
            "AllReduce", ALU.add,
            replica_groups=[[0, 1], [2, 3], [4, 5], [6, 7]],
            ins=[dr[f'pp_in{tt2}'].ap().opt()],
            outs=[dr[f'pp_out{tt2}'].ap().opt()]))(),
            reads=[b(f'pp_in{tt}')], writes=[b(f'pp_out{tt}')])

    def emit_wo_f2(self, tt):
        S, b = self.S, self.buf
        dr, sb, ps = self.dr, self.sb, self.ps
        ppo = dr[f'pp_out{tt}'].ap().rearrange("(s c p) t -> s p c t", s=2, p=128)
        x1ap = dr['x1T'].ap().rearrange("(c p) t -> p c t", p=128)
        x2ap = dr['x2T'].ap().rearrange("(c p) t -> p c t", p=128)
        for oc in range(NCH):
            b0 = ['x2b0', 'x2b1'][0]; b1s = ['x2b0', 'x2b1'][1]
            S.dma('sp', (lambda oc2=oc: lambda h: h.dma_start(
                out=sb['ct1'][...], in_=ppo[0, :, oc2, :]))(),
                reads=[b(f'pp_out{tt}')], writes=[b('ct1')])
            S.dma('sp', (lambda oc2=oc: lambda h: h.dma_start(
                out=sb['ct2'][...], in_=ppo[1, :, oc2, :]))(),
                reads=[b(f'pp_out{tt}')], writes=[b('ct2')])
            xs = [sb['xch0'], sb['xch1']][oc % 2]
            xsb = [b('xch0'), b('xch1')][oc % 2]
            S.dma('sp', (lambda oc2=oc, xs2=xs: lambda h: h.dma_start(
                out=xs2[...], in_=x1ap[:, oc2, tt*CT:(tt+1)*CT]))(),
                reads=[b('x1T')], writes=[xsb])
            S.comp('dve', (lambda xs2=xs: lambda h: h.scalar_tensor_tensor(
                out=sb['toc'][...], in0=sb['ct1'][...], scalar=sb['msel'][:, 0:1], in1=xs2[...],
                op0=ALU.mult, op1=ALU.add))(),
                reads=[b('ct1'), b('msel'), xsb], writes=[b('toc')])
            xoc = [sb['x1oc0'], sb['x1oc1']][oc % 2]
            xb = [b('x1oc0'), b('x1oc1')][oc % 2]
            S.comp('dve', (lambda xoc2=xoc: lambda h: h.scalar_tensor_tensor(
                out=xoc2[...], in0=sb['ct2'][...], scalar=sb['msel'][:, 1:2], in1=sb['toc'][...],
                op0=ALU.mult, op1=ALU.add))(),
                reads=[b('ct2'), b('msel'), b('toc')], writes=[xb])
            S.dma('sp', (lambda oc2=oc, xoc2=xoc: lambda h: h.dma_start(
                out=x2ap[:, oc2, tt*CT:(tt+1)*CT], in_=xoc2[...]))(),
                reads=[xb], writes=[b('x2T')])
            S.comp('pe', (lambda oc2=oc, xoc2=xoc: lambda h: h.matmul(
                ps['pb0'][0:1, :], sb['c_ones_col'][:, 0:1], xoc2[...],
                start=(oc2 == 0), stop=(oc2 == NCH-1)))(),
                reads=[xb, b('c_ones_col')], writes=[b('pb0')])
            sq = sb['xsq0']
            sqb = b('xsq0')
            S.comp('act', (lambda xoc2=xoc, sq2=sq: lambda h: h.activation(
                out=sq2[...], in_=xoc2[...], func=AF.Square))(),
                reads=[xb], writes=[sqb])
            S.comp('pe', (lambda oc2=oc, sq2=sq: lambda h: h.matmul(
                ps['pb2'][0:1, :], sb['c_ones_col'][:, 0:1], sq2[...],
                start=(oc2 == 0), stop=(oc2 == NCH-1)))(),
                reads=[sqb, b('c_ones_col')], writes=[b('pb2')])
        S.comp('dve', lambda h: h.tensor_scalar(out=sb['m_row'][...], in0=ps['pb0'][0:1, :],
                                                scalar1=1.0/HD, scalar2=None, op0=ALU.mult),
               reads=[b('pb0')], writes=[b('m_row')])
        S.comp('dve', lambda h: h.tensor_scalar(out=sb['t_row'][...], in0=ps['pb2'][0:1, :],
                                                scalar1=1.0/HD, scalar2=None, op0=ALU.mult),
               reads=[b('pb2')], writes=[b('t_row')])
        self._ln_rows()
        src = dr['x2T'].ap().rearrange("(c p) t -> p c t", p=128)
        for c in range(NCH):
            xs = [sb['xch0'], sb['xch1']][c % 2]
            xsb = [b('xch0'), b('xch1')][c % 2]
            S.dma('sp', (lambda c2=c, xs2=xs: lambda h: h.dma_start(
                out=xs2[...], in_=src[:, c2, tt*CT:(tt+1)*CT]))(),
                reads=[b('x2T')], writes=[xsb])
            S.comp('dve', (lambda c2=c, xs2=xs: lambda h: h.tensor_tensor(
                out=sb['xr'][:, c2, :], in0=xs2[...], in1=ps['pb1'][...], op=ALU.mult))(),
                reads=[xsb, b('pb1')], writes=[b('xr')])
        self.emit_ffn(tt, 'f2')

    def load_wosl(self, g):
        pass

    def build_full(self, upto=99, sub=99, csub=99):
        self.sub = sub
        self.csub = csub
        self.alloc()
        self.emit_const_load()
        self.emit_wconv_pre()
        if upto >= 1:
            for tt in range(self.nt):
                self.emit_ln(tt, 'xT', 'xr', 'pb0', 'pb2')
                self.emit_ffn(tt, 'f1')
                self.emit_ln2_and_export(tt)
            self.emit_mr_cc()
        if upto >= 2:
            self.arena_barrier('a')
            for g in range(2 * self.nt):
                if upto < 3 + g:
                    break
                self.emit_attn_ct(g, sub=self.sub)
                if g % 2 == 1 and self.sub >= 5:
                    self.emit_pp_cc(g // 2)
        if upto >= 3 + 2*self.nt:
            self.arena_barrier('f')
            for tt in range(self.nt):
                self.emit_wo_f2(tt)
        self.S.finalize()
        sems = self.sems
        with self.nc.Block() as block:
            block.tensor(self.S.emitter('pe', sems))
            block.scalar(self.S.emitter('act', sems))
            block.vector(self.S.emitter('dve', sems))
            block.gpsimd(self.S.emitter('pool', sems))
            block.sync(self.S.emitter('sp', sems))
        return self.nc

    def build_f1_only(self):
        self.alloc()
        self.emit_const_load()
        self.emit_wconv_pre()
        for tt in range(self.nt):
            self.emit_ln(tt, 'xT', 'xr', 'pb0', 'pb2')
            self.emit_ffn(tt, 'f1')
            self.emit_ln2_and_export(tt)
        self.emit_mr_cc()
        self.S.finalize()
        sems = self.sems
        with self.nc.Block() as block:
            block.tensor(self.S.emitter('pe', sems))
            block.scalar(self.S.emitter('act', sems))
            block.vector(self.S.emitter('dve', sems))
            block.gpsimd(self.S.emitter('pool', sems))
            block.sync(self.S.emitter('sp', sems))
        return self.nc



def kernel(**inputs):
    from concourse.bass_utils import run_bass_kernel_spmd
    n_tiles = 4
    B = Build(n_tiles, dbg=False)
    nc = B.build_full()
    in_maps = host_prep(inputs, n_tiles)
    res = run_bass_kernel_spmd(nc, in_maps, core_ids=list(range(8)))
    out = np.zeros((4, 4096, HD), np.float32)
    for core in range(8):
        bi, hg = core // 2, core % 2
        ot = np.asarray(res.results[core]['outT'], np.float32)
        for tt in range(n_tiles):
            gt = 2 * tt + hg
            out[bi, gt*CT:(gt+1)*CT, :] = ot[:, tt*CT:(tt+1)*CT].T
    return out


# version 02
# speedup vs baseline: 1.0004x; 1.0004x over previous
import numpy as np
import ml_dtypes
from contextlib import ExitStack
import concourse.bass as bass
import concourse.mybir as mybir

F32 = mybir.dt.float32
BF16 = mybir.dt.bfloat16
AF = mybir.ActivationFunctionType
ALU = mybir.AluOpType
AX = mybir.AxisListType
npbf = ml_dtypes.bfloat16

NH, DK, DV = 16, 128, 128
HD = 2048
NCH = 16
FFH = 8192
NFH = 64
CONV = 4
C = 128
CB = 8
CT = 512
EPS_STAT = 1e-6
NORM_EPS = 1e-5
L2_EPS = 1e-6

PAR_TILES = {
    'vh_h', 'e_h', 'tmp8', 'ampb', 'knb', 'qnb', 'zt', 'sp1', 'ab', 'spt', 'gt',
    'D_h', 'G_h', 'kn_h', 'qn_h', 'pc0', 'pc1', 'ttb', 'p3', 'p4', 'pmrow',
    'TRt', 'lv_gb', 'lv_ga', 'lv_ae', 'lv_ak', 'lv_aq', 'lv_be', 'lv_bk',
    'm_W', 'm_qt', 'm_Wb', 'm_WbT', 'm_bV', 'm_At', 'm_R', 'm_U', 'm_Q',
    'm_N1', 'm_N2', 'm_N4', 'm_N8', 'm_N16', 'm_N32', 'm_N64',
    'm_X1', 'm_X2', 'm_X4', 'm_X8', 'm_X16', 'm_X32',
    'otm', 'otm2', 'diagt', 'colA', 'lamc', 'rms_c', 'Stmp',
}
PAR_BANKS = {'pa0': 'pb0', 'pa1': 'pb1', 'pa2': 'pb2', 'patb': 'patb2'}
PAR_STATE = {'Sfp0', 'Sbf', 'sgate'}

ENGINES = ("pe", "act", "dve", "pool", "sp")
DMA_NCLASS = {"sp": 6, "pool": 6, "act": 2, "dve": 2}
CC_DOM = "ccD"
DMA_DOMS = [f"{e}D{r}" for e in DMA_NCLASS for r in range(DMA_NCLASS[e])]
ALL_DOMS = list(ENGINES) + DMA_DOMS + [CC_DOM]

class View:
    def __init__(self, ap):
        self._ap = ap
    def __getitem__(self, idx):
        if idx is Ellipsis:
            return self._ap
        return self._ap[idx]

class Buf:
    __slots__ = ("name", "writer", "readers", "is_psum")
    def __init__(self, name):
        self.name = name
        self.writer = None
        self.readers = []
        self.is_psum = False

class Op:
    __slots__ = ("eng", "fn", "deps", "kind", "inc", "tag", "count", "waits", "gidx", "eidx")
    def __init__(self, eng, fn, deps, kind, tag, gidx, eidx):
        self.eng = eng; self.fn = fn; self.deps = deps; self.kind = kind
        self.inc = False; self.tag = tag; self.count = None; self.waits = None
        self.gidx = gidx; self.eidx = eidx

class Sched:
    def __init__(self, debug_serial=False):
        self.debug_serial = debug_serial
        self.ops = {e: [] for e in ENGINES}
        self.glist = []
        self.tag = ""

    def _rec(self, eng, kind, fn, reads, writes):
        dom = eng
        idx = len(self.ops[eng])
        me = (eng, idx)
        def need(dep):
            e2, i2 = dep
            if e2 != eng:
                return True
            if self.ops[e2][i2].kind != "c":
                return True
            return eng != "pe"
        deps = set()
        for b in reads:
            if b.writer is not None and need(b.writer):
                deps.add(b.writer)
            if b.is_psum:
                for r in b.readers:
                    if r[0] != eng:
                        deps.add(r)
        for b in writes:
            if b.writer is not None and need(b.writer):
                deps.add(b.writer)
            for r in b.readers:
                if need(r):
                    deps.add(r)
        op = Op(eng, fn, sorted(deps), kind, self.tag, len(self.glist), idx)
        self.ops[eng].append(op)
        self.glist.append(op)
        for b in reads:
            b.readers.append(me)
        for b in writes:
            b.writer = me
            b.readers = [me]
        return op

    def comp(self, eng, fn, reads=(), writes=()):
        return self._rec(eng, "c", fn, reads, writes)

    def dma(self, eng, fn, reads=(), writes=()):
        return self._rec(eng, "d", fn, reads, writes)

    def cc(self, fn, reads=(), writes=()):
        return self._rec("pool", "cc", fn, reads, writes)

    def finalize(self):
        if self.debug_serial:
            prev = None
            for op in self.glist:
                op.deps = [(prev.eng, prev.eidx)] if (prev is not None and prev.eng != op.eng) else []
                prev = op
        for e in ENGINES:
            for op in self.ops[e]:
                for (e2, i2) in op.deps:
                    d = self.ops[e2][i2]
                    if d.kind == "c":
                        d.inc = True
        ccn = 0
        for e in ENGINES:
            c_comp = 0
            nd = 0
            for op in self.ops[e]:
                if op.kind == "c":
                    if op.inc:
                        c_comp += 1
                    op.count = ("c", c_comp, None)
                elif op.kind == "d":
                    r = nd % DMA_NCLASS[e]
                    i = nd // DMA_NCLASS[e]
                    op.count = ("d", i, r)
                    nd += 1
                else:
                    op.count = ("cc", ccn, None)
                    ccn += 1
        for e in ENGINES:
            for op in self.ops[e]:
                w = {}
                for (e2, i2) in op.deps:
                    d = self.ops[e2][i2]
                    k, v, r = d.count
                    if k == "c":
                        w[e2] = max(w.get(e2, 0), v)
                    elif k == "d":
                        dom = f"{e2}D{r}"
                        w[dom] = max(w.get(dom, 0), (v + 1) * 16)
                    else:
                        w[CC_DOM] = max(w.get(CC_DOM, 0), v + 1)
                if op.kind == "d":
                    k, i, r = op.count
                    if i > 0:
                        dom = f"{e}D{r}"
                        w[dom] = max(w.get(dom, 0), i * 16)
                elif op.kind == "cc":
                    k, i, r = op.count
                    if i > 0:
                        w[CC_DOM] = max(w.get(CC_DOM, 0), i)
                op.waits = w

    def _idx_of(self, op):
        return self.ops[op.eng].index(op)

    def stats(self):
        return {e: len(self.ops[e]) for e in ENGINES}

    def totals(self):
        tot = {}
        for e in ENGINES:
            nd = sum(1 for op in self.ops[e] if op.kind == "d")
            if e in DMA_NCLASS:
                for r in range(DMA_NCLASS[e]):
                    cnt = nd // DMA_NCLASS[e] + (1 if nd % DMA_NCLASS[e] > r else 0)
                    if cnt:
                        tot[f"{e}D{r}"] = cnt * 16
            ncc = sum(1 for op in self.ops[e] if op.kind == "cc")
            if ncc:
                tot[CC_DOM] = tot.get(CC_DOM, 0) + ncc
        return tot

    def emitter(self, e, sems):
        ops = self.ops[e]
        tot = self.totals()
        def run(h):
            wm = {d: 0 for d in ALL_DOMS}
            for op in ops:
                for d in sorted(op.waits):
                    v = op.waits[d]
                    if v > wm[d]:
                        h.wait_ge(sems[d], v)
                        wm[d] = v
                inst = op.fn(h)
                if op.kind == "c":
                    if op.inc:
                        assert inst is not None, f"op '{op.tag}' on {e} needs inc but returned None"
                        inst.then_inc(sems[e], 1)
                elif op.kind == "d":
                    k, i, r = op.count
                    inst.then_inc(sems[f"{e}D{r}"], 16)
                else:
                    inst.then_inc(sems[CC_DOM], 1)
            for d, v in sorted(tot.items()):
                if v > wm.get(d, 0):
                    h.wait_ge(sems[d], v)
        return run


def host_prep(inputs, n_tiles, dbg=False):
    T = n_tiles * 2 * CT
    x = inputs['L0_attn_hidden_states'][:, :T, :]
    f32 = np.float32
    def T_(a):
        return np.ascontiguousarray(np.asarray(a, f32).T)
    WqT = T_(inputs['L0_attn_Wq'])
    WkT = T_(inputs['L0_attn_Wk'])
    WvT = T_(inputs['L0_attn_Wv'])
    Wg1T = T_(inputs['L0_attn_Wg1'])
    Wg2T = T_(inputs['L0_attn_Wg2'])
    WoT = T_(inputs['L0_attn_Wo'])
    WsvT = T_(inputs['L0_attn_Wsv'])
    WgT = T_(inputs['L0_ffn_w_gate'])
    WuT = T_(inputs['L0_ffn_w_up'])
    WdT = T_(inputs['L0_ffn_w_down'])
    Wb1T = T_(inputs['L0_attn_Wb1'])
    Wa1T = T_(inputs['L0_attn_Wa1'])
    def z8(W2):
        z = np.zeros((32, 64), f32)
        for h in range(8):
            z[:, 8*h + h] = W2[0]
        return z
    Wb2Tz = z8(np.asarray(inputs['L0_attn_Wb2'], f32))
    Wa2Tz = z8(np.asarray(inputs['L0_attn_Wa2'], f32))
    ident = np.eye(128, dtype=npbf)
    shift = np.zeros((7, 128, 128), dtype=npbf)
    for d in range(1, 8):
        for j in range(128):
            m = j - d
            if m >= 0 and (j // CB) == (m // CB):
                shift[d-1, j, m] = 1
    zc = np.zeros((16, 128, 16), dtype=npbf)
    for r in range(16):
        zc[r, :, r] = 1
    zrow8 = np.zeros((8, 8, 128), dtype=npbf)
    for hh in range(8):
        zrow8[hh, hh, :] = 1
    ones_row = np.ones((1, 128), dtype=npbf)
    ones_col = np.ones((128, 1), f32)
    in_maps = []
    for core in range(8):
        b, hg = core // 2, core % 2
        own = np.concatenate([x[b, (2*tt+hg)*CT:(2*tt+hg+1)*CT, :] for tt in range(n_tiles)], 0)
        hsl = slice(hg*1024, (hg+1)*1024)
        m = {
            'xT': np.ascontiguousarray(own.T),
            'wgT': WgT, 'wuT': WuT, 'wdT': WdT,
            'wqT': np.ascontiguousarray(WqT[:, hsl]),
            'wkT': np.ascontiguousarray(WkT[:, hsl]),
            'wvT': np.ascontiguousarray(WvT[:, hsl]),
            'wg1T': Wg1T,
            'wg2T': np.ascontiguousarray(Wg2T[:, hsl]),
            'woT': np.ascontiguousarray(WoT[hg*1024:(hg+1)*1024, :]),
            'wsvT': np.ascontiguousarray(WsvT.astype(npbf)),
            'wb1T': np.ascontiguousarray(Wb1T.astype(npbf)),
            'wa1T': np.ascontiguousarray(Wa1T.astype(npbf)),
            'wb2Tz': np.ascontiguousarray(Wb2Tz.astype(npbf)),
            'wa2Tz': np.ascontiguousarray(Wa2Tz.astype(npbf)),
            'bb1c': np.asarray(inputs['L0_attn_bb1'], f32).reshape(32, 1),
            'ba1c': np.asarray(inputs['L0_attn_ba1'], f32).reshape(32, 1),
            'bb2c': np.full((8, 1), float(inputs['L0_attn_bb2'][0]), f32),
            'ba2c': np.full((8, 1), float(inputs['L0_attn_ba2'][0]), f32),
            'convw': np.stack([
                np.asarray(inputs['L0_attn_conv_q'], f32)[hg*1024:(hg+1)*1024].reshape(8, 128, 4).transpose(1, 0, 2),
                np.asarray(inputs['L0_attn_conv_k'], f32)[hg*1024:(hg+1)*1024].reshape(8, 128, 4).transpose(1, 0, 2),
                np.asarray(inputs['L0_attn_conv_v'], f32)[hg*1024:(hg+1)*1024].reshape(8, 128, 4).transpose(1, 0, 2),
            ], 0).copy(),
            'dtb': np.asarray(inputs['L0_attn_dt_bias'], f32)[hg*1024:(hg+1)*1024].reshape(8, 128).T.copy(),
            'alog': np.asarray(inputs['L0_attn_A_log'], f32)[hg*8:(hg+1)*8].reshape(1, 8).copy(),
            'bg2c': np.asarray(inputs['L0_attn_bg2'], f32)[hg*1024:(hg+1)*1024].reshape(8, 128).T.copy(),
            'normw': np.asarray(inputs['L0_attn_norm_w'], f32).reshape(128, 1).copy(),
            'ident': ident, 'shift': shift, 'zc': zc,
            'ones_row': ones_row, 'ones_col': ones_col,
            'identf': np.eye(16, dtype=f32),
            'zrow8': zrow8,
            'mselh': np.stack([np.full(128, 1.0 if hg == 0 else 0.0, f32),
                               np.full(128, 1.0 if hg == 1 else 0.0, f32)], 1),
        }
        in_maps.append(m)
    return in_maps


class Build:
    def __init__(self, n_tiles, dbg=False, debug_serial=False):
        self.nt = n_tiles
        self.T = n_tiles * 2 * CT
        self.dbg = dbg
        self.nc = bass.Bass("TRN2", target_bir_lowering=False, debug=False)
        self.S = Sched(debug_serial)
        self.es = ExitStack()
        self.bufs = {}
        self.vw_f = []
        self.vw_a = []
        self.sb = {}
        self.dr = {}
        self.ps = {}

    def buf(self, name):
        if name not in self.bufs:
            self.bufs[name] = Buf(name)
        return self.bufs[name]

    def dram(self, name, shape, dt, kind=None):
        t = self.nc.dram_tensor(name, shape, dt, kind=kind) if kind else self.nc.dram_tensor(name, shape, dt)
        self.dr[name] = t
        self.buf(name)
        return t

    def sbuf(self, name, shape, dt):
        t = self.es.enter_context(self.nc.sbuf_tensor(name, shape, dt))
        self.sb[name] = t
        self.buf(name)
        return t

    def psum(self, name, shape, dt=F32):
        t = self.es.enter_context(self.nc.psum_tensor(name, shape, dt))
        self.ps[name] = t
        self.buf(name).is_psum = True
        return t

    def alloc(self):
        nc, nt, T = self.nc, self.nt, self.T
        D = self.dram
        NT = nt * CT
        D('xT', [HD, NT], F32, kind="ExternalInput")
        for n, sh in [('wgT', [HD, FFH]), ('wuT', [HD, FFH]), ('wdT', [FFH, HD]),
                      ('wqT', [HD, 1024]), ('wkT', [HD, 1024]), ('wvT', [HD, 1024]),
                      ('wg1T', [HD, 128]), ('wg2T', [128, 1024]), ('woT', [1024, HD])]:
            D(n, sh, F32, kind="ExternalInput")
        for n, sh in [('wsvT', [128, 128]), ('wb1T', [3, 32]), ('wa1T', [3, 32]),
                      ('wb2Tz', [32, 64]), ('wa2Tz', [32, 64]),
                      ('ident', [128, 128]), ('shift', [7, 128, 128]), ('zc', [16, 128, 16]),
                      ('ones_row', [1, 128])]:
            D(n, sh, BF16, kind="ExternalInput")
        for n, sh in [('bb1c', [32, 1]), ('ba1c', [32, 1]), ('bb2c', [8, 1]), ('ba2c', [8, 1]),
                      ('convw', [3, 128, 8, 4]), ('dtb', [128, 8]), ('alog', [1, 8]),
                      ('bg2c', [128, 8]), ('normw', [128, 1]), ('ones_col', [128, 1]),
                      ('mselh', [128, 2]), ('identf', [16, 16])]:
            D(n, sh, F32, kind="ExternalInput")
        D('zrow8', [8, 8, 128], BF16, kind="ExternalInput")
        D('outT', [HD, NT], F32, kind="ExternalOutput")
        ik = "ExternalOutput" if self.dbg else None
        D('x1T', [HD, NT], F32, kind=ik)
        D('x2T', [HD, NT], F32, kind=ik)
        D('wg_c', [HD, FFH], BF16)
        D('wu_c', [HD, FFH], BF16)
        D('wd_c', [FFH, HD], BF16)
        D('wq_c', [HD, 1024], BF16)
        D('wk_c', [HD, 1024], BF16)
        D('wv_c', [HD, 1024], BF16)
        D('wg1_c', [HD, 128], BF16)
        D('wo_c', [1024, HD], BF16)
        for tt in range(nt):
            D(f'xr2_in{tt}', [HD, CT], BF16)
            D(f'xr2_out{tt}', [2*HD, CT], BF16)
        D('mr_in', [1, NT], F32)
        D('mr_out', [2, NT], F32)
        for tt in range(nt):
            D(f'pp_in{tt}', [2*HD, CT], F32)
            D(f'pp_out{tt}', [2*HD, CT], F32)
        if self.dbg:
            D('dbg_xr2', [2*HD, NT], BF16, kind="ExternalOutput")
            D('dbg_mr', [2, NT], F32, kind="ExternalOutput")

        B = self.sbuf
        B('c_ident', [128, 128], BF16)
        B('c_identf', [16, 16], F32)
        B('c_shift', [128, 7, 128], BF16)
        B('c_zc', [128, 16, 16], BF16)
        B('c_zrow8', [8, 8, 128], BF16)
        B('c_ones_row', [1, 128], BF16)
        B('c_ones_col', [128, 1], F32)
        B('c_ones_colb', [128, 1], BF16)
        B('c_onesrow_f', [1, 128], F32)
        B('c_wsv', [128, 128], BF16)
        B('c_wb1', [3, 32], BF16)
        B('c_wa1', [3, 32], BF16)
        B('c_wb2z', [32, 64], BF16)
        B('c_wa2z', [32, 64], BF16)
        B('c_bb1', [32, 1], F32)
        B('c_ba1', [32, 1], F32)
        B('c_bb2', [8, 1], F32)
        B('c_ba2', [8, 1], F32)
        B('c_convw', [128, 3, 8, 4], F32)
        B('c_dtb', [128, 8], F32)
        B('c_negea', [128, 8], F32)
        B('c_bg2', [128, 8], F32)
        B('c_normw', [128, 1], F32)
        B('c_wg2', [128, 8, 128], BF16)
        B('c_alog', [1, 8], F32)
        B('xch0', [128, CT], F32); B('xch1', [128, CT], F32)
        B('xsq0', [128, CT], F32)
        B('wgu0', [128, 2, 16, 128], BF16); B('wgu1', [128, 2, 16, 128], BF16)
        B('wdsl0', [128, 16, 128], BF16); B('wdsl1', [128, 16, 128], BF16)
        B('x1oc0', [128, CT], F32); B('x1oc1', [128, CT], F32)
        arena = self.sbuf('arena', [128, 17408], F32)
        av = arena[...]
        def fview(name, ap):
            self.vw_f.append(name)
            self.sb[name] = View(ap)
            self.buf(name)
        def aview(name, ap):
            self.vw_a.append(name)
            self.sb[name] = View(ap)
            self.buf(name)
        del self.sb['arena']
        fview('facc', av[:, 0:8192].rearrange("p (a b) -> p a b", a=16))
        fview('hti', av[:, 8192:12288].bitcast(BF16).rearrange("p (a b) -> p a b", a=16))
        fview('xr', av[:, 12288:16384].bitcast(BF16).rearrange("p (a b) -> p a b", a=16))
        fview('sil', av[:, 16384:16896])
        fview('toc', av[:, 16896:17408])
        aview('qc', av[:, 0:2048].bitcast(BF16).rearrange("p (a b) -> p a b", a=8))
        aview('kc', av[:, 2048:4096].bitcast(BF16).rearrange("p (a b) -> p a b", a=8))
        aview('vc', av[:, 4096:6144].bitcast(BF16).rearrange("p (a b) -> p a b", a=8))
        aview('sgate', av[:, 6144:8192].bitcast(BF16).rearrange("p (a b) -> p a b", a=8))
        aview('o_ct', av[:, 8192:10240].bitcast(BF16).rearrange("p (a b) -> p a b", a=8))
        aview('zt', av[:, 10240:10752])
        aview('sp1', av[:, 10752:11264])
        aview('ab', av[:, 11264:11776])
        aview('spt', av[:, 11776:12288])
        aview('gt', av[:, 12288:12800])
        aview('xr2t', av[:, 12800:16896].bitcast(BF16).rearrange("p (a b) -> p a b", a=16))
        aview('wosl0', av[:, 16896:17408].bitcast(BF16).rearrange("p (a b) -> p a b", a=8))
        rowsA = self.sbuf('rowsA', [128, CT], F32)
        rowsB = self.sbuf('rowsB', [128, CT], F32)
        rowsC = self.sbuf('rowsC', [128, CT], BF16)
        rowsD = self.sbuf('rowsD', [128, CT], BF16)
        del self.sb['rowsA']; del self.sb['rowsB']; del self.sb['rowsC']; del self.sb['rowsD']
        for nm, t2, p0 in [('rstd_row', rowsA, 0), ('t_row', rowsA, 32), ('mr_row', rowsA, 64),
                           ('amr_row', rowsA, 96), ('m_row', rowsB, 0), ('v_row', rowsB, 32)]:
            self.sb[nm] = View(t2[p0:p0+1, :])
            self.buf(nm)
        self.sb['mrn_bf'] = View(rowsC[...])
        self.buf('mrn_bf')
        self.sb['amrn_bf'] = View(rowsD[...])
        self.buf('amrn_bf')
        B('swg_row', [128, 22, 128], BF16)
        B('swu_row', [128, 22, 128], BF16)
        B('swq_row', [128, 3, 128], BF16)
        B('swk_row', [128, 3, 128], BF16)
        B('swv_row', [128, 3, 128], BF16)
        B('swg1_row', [128, 1, 128], BF16)
        B('up0', [128, 516], BF16)
        B('tls', [128, 3, 8, 3], BF16)
        B('glt', [128, CT], BF16)
        B('c_ones_f', [128, CT], F32)
        B('wp0', [128, 16, 128], BF16); B('wp1', [128, 16, 128], BF16)
        B('ct1', [128, CT], F32); B('ct2', [128, CT], F32)
        B('st_all', [8, 7, CT], BF16)
        st8A = self.sbuf('st8A', [128, CT], F32)
        st8B = self.sbuf('st8B', [128, CT], F32)
        st8C = self.sbuf('st8C', [128, CT], F32)
        for dn in ['st8A', 'st8B', 'st8C']:
            del self.sb[dn]
        st8E = self.sbuf('st8E', [128, CT], F32)
        del self.sb['st8E']
        for nm, t2, p0 in [('tmp8', st8A, 0), ('sl2t', st8A, 32), ('scost', st8A, 64), ('rqn8', st8A, 96),
                           ('beta8', st8B, 0), ('amp8', st8B, 32), ('ampz8', st8B, 64), ('rkn8', st8B, 96),
                           ('den2', st8C, 0), ('rsqk8', st8C, 32),
                           ('den1', st8E, 0)]:
            self.sb[nm] = View(t2[p0:p0+8, :])
            self.buf(nm)
        B('ampz8c', [8, CT], BF16); B('rkn8c', [8, CT], BF16); B('rqn8c', [8, CT], BF16)
        B('s3', [3, 8*CT], BF16)
        B('hb', [32, CT], BF16)
        B('betaT', [128, 32], F32); B('betaTn', [128, 32], F32)
        B('stsc0', [128, CT], BF16)
        B('vh_h', [128, CT], BF16); B('e_h', [128, CT], BF16)
        B('ampb', [128, CT], BF16); B('knb', [128, CT], BF16); B('qnb', [128, CT], BF16)
        B('D_h', [128, CT], BF16); B('G_h', [128, CT], F32)
        B('kn_h', [128, CT], BF16); B('qn_h', [128, CT], BF16)
        B('pc0', [128, CT], BF16); B('pc1', [128, CT], BF16)
        B('ttb', [128, CT], BF16); B('p3', [128, CT], BF16); B('p4', [128, CT], BF16)
        B('pmrow', [16, CT], BF16)
        B('TRt', [128, 4, 16], F32)
        B('c_zrow', [1, 256], BF16)
        B('m_bV', [128, 128], F32)
        for n in ['m_W', 'm_qt', 'm_Wb', 'm_WbT', 'm_At', 'm_R', 'm_U', 'm_Q',
                  'm_N1', 'm_N2', 'm_N4', 'm_N8', 'm_N16', 'm_N32', 'm_N64',
                  'm_X1', 'm_X2', 'm_X4', 'm_X8', 'm_X16', 'm_X32',
                  'lv_ae', 'lv_ak', 'lv_aq', 'lv_be', 'lv_bk',
                  'otm', 'otm2', 'diagt']:
            B(n, [128, 128], BF16)
        B('lv_ga', [128, 128], F32)
        self.sb['lv_gb'] = View(av[:, 12288:12416])
        self.bufs['lv_gb'] = self.buf('gt')
        for zn, wd in [(8, 8), (24, 8), (40, 8), (56, 8), (16, 16), (48, 16), (32, 32)]:
            B(f'padM_{zn}_{wd}', [128, 64], BF16)
        for zn, wd in [(16, 8), (48, 8), (32, 16), (32, 8)]:
            B(f'padA_{zn}_{wd}', [128, 64], BF16)
        B('colA', [128, 1], F32); B('lamc', [128, 1], F32); B('rms_c', [128, 1], F32)
        B('c_zero', [128, 1], F32)
        B('Sfp0', [128, 8, 128], F32)
        B('Stmp', [128, 128], F32)
        B('Sbf', [128, 8, 128], BF16)
        B('msel', [128, 2], F32)
        borrowed = ['wgu0', 'wgu1', 'wdsl0', 'wdsl1', 'xch0', 'xch1', 'xsq0',
                    'x1oc0', 'x1oc1', 'mrn_bf']
        regions = [
            self.sb['wgu0'][...].rearrange("p a b c -> p (a b c)").bitcast(F32),
            self.sb['wgu1'][...].rearrange("p a b c -> p (a b c)").bitcast(F32),
            self.sb['wdsl0'][...].rearrange("p a b -> p (a b)").bitcast(F32),
            self.sb['wdsl1'][...].rearrange("p a b -> p (a b)").bitcast(F32),
            self.sb['xch0'][...], self.sb['xch1'][...], self.sb['xsq0'][...],
            self.sb['x1oc0'][...], self.sb['x1oc1'][...],
            self.sb['mrn_bf'][...].bitcast(F32),
        ]
        _cur = [[ap, 0] for ap in regions]
        def carve(nf32):
            for ent in _cur:
                if ent[0].shape[-1] - ent[1] >= nf32:
                    ent[1] += nf32
                    return ent[0][:, ent[1]-nf32:ent[1]]
            raise RuntimeError(f"no room for twin of {nf32} f32 cols")
        def twin(name, nf32, dt2=BF16, shape3=None):
            apv = carve(nf32)
            if dt2 == BF16:
                apv = apv.bitcast(BF16)
            if shape3 is not None:
                apv = apv.rearrange("p (a b) -> p a b", a=shape3)
            self.vw_a.append(name)
            self.sb[name] = View(apv)
            self.buf(name)
            return apv
        for nm in ['zt1', 'sp11', 'ab1', 'spt1', 'G_h1']:
            twin(nm, 512, dt2=F32)
        gt1ap = twin('gt1', 512, dt2=F32)
        self.sb['lv_gb1'] = View(gt1ap[:, 0:128])
        self.bufs['lv_gb1'] = self.buf('gt1')
        for nm in ['kn_h1', 'qn_h1', 'D_h1', 'vh_h1', 'e_h1', 'ampb1', 'knb1',
                   'qnb1', 'pc01', 'pc11', 'ttb1', 'p31', 'p41']:
            twin(nm, 256)
        for nm in ['m_bV1', 'lv_ga1', 'Stmp1']:
            twin(nm, 128, dt2=F32)
        twin('TRt1', 64, dt2=F32, shape3=4)
        for nm in ['m_W1', 'm_qt1', 'm_Wb1', 'm_WbT1', 'm_At1', 'm_R1', 'm_U1',
                   'm_Q1', 'm_N11', 'm_N21', 'm_N41', 'm_N81', 'm_N161', 'm_N321',
                   'm_N641', 'm_X11', 'm_X21', 'm_X41', 'm_X81', 'm_X161', 'm_X321',
                   'lv_ae1', 'lv_ak1', 'lv_aq1', 'lv_be1', 'lv_bk1',
                   'otm1', 'otm21', 'diagt1']:
            twin(nm, 64)
        for nm in ['colA1', 'lamc1', 'rms_c1']:
            twin(nm, 1, dt2=F32)
        self.sb['pmrow1'] = View(st8C[64:80, 0:256].bitcast(BF16))
        self.buf('pmrow1'); self.vw_a.append('pmrow1')
        self.sb['tmp81'] = View(st8C[96:104, :])
        self.buf('tmp81'); self.vw_a.append('tmp81')
        self.vw_f.extend(borrowed)
        P = self.psum
        P('pb0', [128, CT]); P('pb1', [128, CT]); P('pb2', [128, CT]); P('pb3', [128, CT])
        P('pa0', [128, CT]); P('pa1', [128, CT]); P('pa2', [128, CT])
        P('patb', [128, 1024], BF16)
        self.ps['patb2'] = View(self.ps['pb3'][...].bitcast(BF16))
        self.bufs['patb2'] = self.buf('pb3')
        self.sems = {}
        for d in ALL_DOMS:
            self.sems[d] = self.es.enter_context(self.nc.semaphore(f"sem_{d}"))

    def _bufs_of(self, names):
        out = []
        for n in names:
            if n in PAR_STATE:
                out += [self.buf(n + 'p0'), self.buf(n + 'p1')]
            else:
                out.append(self.buf(n))
        return out

    def arena_barrier(self, to_side):
        S = self.S
        allv = self._bufs_of(self.vw_f) + self._bufs_of(self.vw_a)
        S.comp('dve', lambda h: h.memset(self.sb['ct1'][0:1, 0:1], 0.0),
               reads=[], writes=allv + [self.buf('ct1')])

    def par_ctx(self, par):
        par = par % 2
        class _M:
            def __init__(self, base, mapper):
                self._base = base; self._map = mapper
            def __getitem__(self, k):
                return self._base[self._map(k)]
        def tmap(k):
            return k + '1' if (par and k in PAR_TILES) else k
        def pmap(k):
            return PAR_BANKS[k] if (par and k in PAR_BANKS) else k
        def bmap(n):
            if n in PAR_STATE:
                return self.buf(n + ('p1' if par else 'p0'))
            if n in PAR_BANKS:
                return self.buf(pmap(n))
            return self.buf(tmap(n))
        return _M(self.sb, tmap), _M(self.ps, pmap), bmap

    def swsl(self, name, i):
        return self.sb[name][32*(i % 3):32*(i % 3)+1, i // 3, :]

    def constname(self, val):
        return f"cc_{val!r}".replace('-', 'm').replace('.', 'p').replace('+', '')

    def constcol(self, val):
        key = self.constname(val)
        if key not in self.sb:
            t = self.sbuf(key, [128, 1], F32)
            self.S.comp('dve', (lambda t2=t: lambda h: h.memset(t2[...], float(val)))(),
                        writes=[self.buf(key)])
        return self.sb[key]

    def t(self, name):
        return self.sb[name] if name in self.sb else (self.ps[name] if name in self.ps else self.dr[name])

    def emit_const_load(self):
        S, t, b = self.S, self.t, self.buf
        dr, sb = self.dr, self.sb
        pairs = [
            ('ident', 'c_ident', None), ('ones_row', 'c_ones_row', None),
            ('ones_col', 'c_ones_col', None),
            ('wsvT', 'c_wsv', None), ('wb1T', 'c_wb1', None), ('wa1T', 'c_wa1', None),
            ('wb2Tz', 'c_wb2z', None), ('wa2Tz', 'c_wa2z', None),
            ('bb1c', 'c_bb1', None), ('ba1c', 'c_ba1', None), ('bb2c', 'c_bb2', None), ('ba2c', 'c_ba2', None),
            ('dtb', 'c_dtb', None), ('bg2c', 'c_bg2', None), ('normw', 'c_normw', None),
            ('alog', 'c_alog', None), ('identf', 'c_identf', None),
        ]
        for src, dst, _ in pairs:
            S.dma('sp', (lambda s=src, d2=dst: (lambda h: h.dma_start(out=self.sb[d2][...], in_=self.dr[s].ap())))(),
                  reads=[b(src)], writes=[b(dst)])
        S.dma('sp', lambda h: h.dma_start(out=sb['c_shift'][...],
                                          in_=dr['shift'].ap().rearrange("s p m -> p s m")),
              reads=[b('shift')], writes=[b('c_shift')])
        S.dma('sp', lambda h: h.dma_start(out=sb['c_zc'][...],
                                          in_=dr['zc'].ap().rearrange("s p m -> p s m")),
              reads=[b('zc')], writes=[b('c_zc')])
        S.dma('sp', lambda h: h.dma_start(out=sb['c_zrow8'][...], in_=dr['zrow8'].ap()),
              reads=[b('zrow8')], writes=[b('c_zrow8')])
        S.dma('sp', lambda h: h.dma_start(out=sb['c_convw'][...],
                                          in_=dr['convw'].ap().rearrange("s p h k -> p s h k")),
              reads=[b('convw')], writes=[b('c_convw')])
        S.comp('dve', lambda h: h.memset(sb['c_onesrow_f'][...], 1.0), writes=[b('c_onesrow_f')])
        S.comp('dve', lambda h: h.memset(sb['c_ones_colb'][...], 1.0), writes=[b('c_ones_colb')])
        S.comp('dve', lambda h: h.memset(sb['c_ones_f'][...], 1.0), writes=[b('c_ones_f')])

        S.comp('dve', lambda h: h.memset(sb['c_zrow'][...], 0.0), writes=[b('c_zrow')])
        S.comp('dve', lambda h: h.memset(sb['c_zero'][...], 0.0), writes=[b('c_zero')])
        padnames = [f'padM_{z}_{w}' for z, w in [(8,8),(24,8),(40,8),(56,8),(16,16),(48,16),(32,32)]] + \
                   [f'padA_{z}_{w}' for z, w in [(16,8),(48,8),(32,16),(32,8)]]
        for n in padnames + ['p3', 'p4', 'ttb', 'pc0', 'pc1',
                             'p31', 'p41', 'ttb1', 'pc01', 'pc11']:
            S.comp('dve', (lambda n2=n: lambda h: h.memset(sb[n2][...], 0.0))(), writes=[b(n)])
        for n in ['Sfp0', 'Sbf']:
            S.comp('dve', (lambda n2=n: lambda h: h.memset(sb[n2][...], 0.0))(),
                   writes=[b(n + 'p0'), b(n + 'p1')])
        S.dma('sp', lambda h: h.dma_start(out=sb['msel'][...], in_=dr['mselh'].ap()),
              reads=[b('mselh')], writes=[b('msel')])
        S.comp('act', lambda h: h.activation(out=sb['c_alog'][...], in_=sb['c_alog'][...], func=AF.Exp),
               reads=[b('c_alog')], writes=[b('c_alog')])
        S.comp('pe', lambda h: h.matmul(self.ps['pa0'][:, 0:8], sb['c_onesrow_f'][0:1, :],
                                        sb['c_alog'][...], start=True, stop=True),
               reads=[b('c_onesrow_f'), b('c_alog')], writes=[b('pa0')])
        S.comp('act', lambda h: h.activation(out=sb['c_negea'][...], in_=self.ps['pa0'][:, 0:8], func=AF.Copy),
               reads=[b('pa0')], writes=[b('c_negea')])
        S.comp('dve', lambda h: h.tensor_scalar(out=sb['c_negea'][...], in0=sb['c_negea'][...],
                                                scalar1=-1.0, scalar2=None, op0=ALU.mult),
               reads=[b('c_negea')], writes=[b('c_negea')])

    def emit_wconv_pre(self):
        S, b = self.S, self.buf
        dr, sb, ps = self.dr, self.sb, self.ps
        jobs = [
            ('wgT', 'wg_c', FFH, 128, 'swg_row'),
            ('wuT', 'wu_c', FFH, 128, 'swu_row'),
            ('wqT', 'wq_c', 1024, 128, 'swq_row'),
            ('wkT', 'wk_c', 1024, 128, 'swk_row'),
            ('wvT', 'wv_c', 1024, 128, 'swv_row'),
            ('wg1T', 'wg1_c', 128, 128, 'swg1_row'),
            ('wdT', 'wd_c', HD, 128, None),
            ('woT', 'wo_c', HD, 128, None),
        ]
        S.dma('pool', lambda h: h.dma_start(out=sb['c_wg2'][...],
                                            in_=dr['wg2T'].ap().rearrange("p (h v) -> p h v", h=8)),
              reads=[b('wg2T')], writes=[b('c_wg2')])
        stage_flat = sb['facc'][...].rearrange("p a b -> p (a b)").bitcast(BF16)
        for src, cache, ncols, cch, swr in jobs:
            srcT = dr[src]
            nrow_ch = srcT.shape[0] // 128
            n_sl = ncols // cch
            half = 8192
            for isl in range(n_sl):
                slot = 'facc'
                off = (isl % 2) * half
                dst = stage_flat[:, off:off + nrow_ch*cch].rearrange("p (r c) -> p r c", r=nrow_ch)
                src_ap = srcT.ap().rearrange("(r p) n -> p r n", p=128)[:, :, isl*cch:(isl+1)*cch]
                S.dma('pool', (lambda d2=dst, s2=src_ap: lambda h: h.dma_start(out=d2, in_=s2))(),
                      reads=[b(src)], writes=[b(slot)])
                cache_ap = dr[cache].ap().rearrange("(r p) n -> p r n", p=128)[:, :, isl*cch:(isl+1)*cch]
                S.dma('sp', (lambda d2=cache_ap, s2=dst: lambda h: h.dma_start(out=d2, in_=s2))(),
                      reads=[b(slot)], writes=[b(cache)])
                if swr is not None:
                    for r in range(nrow_ch):
                        S.comp('pe', (lambda r2=r, d2=dst, st=(r == 0), nr=nrow_ch, cc2=cch:
                                      lambda h: h.matmul(ps['pa1'][0:1, 0:cc2], sb['c_ones_colb'][:, 0:1],
                                                         d2[:, r2, :], start=st, stop=(r2 == nr-1)))(),
                               reads=[b(slot), b('c_ones_colb')], writes=[b('pa1')])
                    S.comp('act', (lambda i2=isl, swr2=swr, cc2=cch:
                                   lambda h: h.activation(out=self.swsl(swr2, i2),
                                                          in_=ps['pa1'][0:1, 0:cc2], func=AF.Copy))(),
                           reads=[b('pa1')], writes=[b(swr)])

    def _ln_rows(self):
        S, b = self.S, self.buf
        sb, ps = self.sb, self.ps
        S.comp('dve', lambda h: h.scalar_tensor_tensor(out=sb['v_row'][...], in0=sb['m_row'][...],
                                                       scalar=-1.0, in1=sb['m_row'][...],
                                                       op0=ALU.mult, op1=ALU.mult),
               reads=[b('m_row')], writes=[b('v_row')])
        S.comp('dve', lambda h: h.tensor_tensor(out=sb['v_row'][...], in0=sb['t_row'][...],
                                                in1=sb['v_row'][...], op=ALU.add),
               reads=[b('t_row'), b('v_row')], writes=[b('v_row')])
        epsc = self.constcol(NORM_EPS)
        S.comp('act', (lambda e2=epsc: lambda h: h.activation(out=sb['t_row'][...], in_=sb['v_row'][...],
                                             func=AF.Ln, bias=e2[0:1, 0:1]))(),
               reads=[b('v_row'), b(self.constname(NORM_EPS))], writes=[b('t_row')])
        S.comp('act', lambda h: h.activation(out=sb['rstd_row'][...], in_=sb['t_row'][...],
                                             func=AF.Exp, scale=-0.5),
               reads=[b('t_row')], writes=[b('rstd_row')])
        S.comp('dve', lambda h: h.tensor_tensor(out=sb['mr_row'][...], in0=sb['m_row'][...],
                                                in1=sb['rstd_row'][...], op=ALU.mult),
               reads=[b('m_row'), b('rstd_row')], writes=[b('mr_row')])
        for p0 in [0, 32, 64]:
            S.comp('dve', (lambda p2=p0: lambda h: h.tensor_scalar(
                out=sb['mrn_bf'][p2:p2+1, :], in0=sb['mr_row'][...],
                scalar1=-1.0, scalar2=None, op0=ALU.mult))(),
                reads=[b('mr_row')], writes=[b('mrn_bf')])
        S.comp('pe', lambda h: h.matmul(ps['pb1'][...], sb['c_onesrow_f'][0:1, :], sb['rstd_row'][...],
                                        start=True, stop=True),
               reads=[b('c_onesrow_f'), b('rstd_row')], writes=[b('pb1')])

    def emit_ln(self, tt, src_dram, xr_dst, bank, bank2):
        S, b = self.S, self.buf
        dr, sb, ps = self.dr, self.sb, self.ps
        xsl = [sb['xch0'], sb['xch1']]
        xsqs = [sb['xsq0'], sb['xsq0']]
        xbufs = [b('xch0'), b('xch1')]
        xsqb = [b('xsq0'), b('xsq0')]
        src = dr[src_dram].ap().rearrange("(c p) t -> p c t", p=128)
        for c in range(NCH):
            sl = xsl[c % 2]; bb2 = xbufs[c % 2]
            S.dma('sp', (lambda c2=c, sl2=sl: lambda h: h.dma_start(
                out=sl2[...], in_=src[:, c2, tt*CT:(tt+1)*CT]))(),
                reads=[b(src_dram)], writes=[bb2])
            S.comp('pe', (lambda c2=c, sl2=sl: lambda h: h.matmul(
                ps[bank][0:1, :], sb['c_ones_col'][:, 0:1], sl2[...], start=(c2 == 0), stop=(c2 == NCH-1)))(),
                reads=[bb2, b('c_ones_col')], writes=[b(bank)])
            S.comp('act', (lambda c2=c, sl2=sl, q=xsqs[c % 2]: lambda h: h.activation(
                out=q[...], in_=sl2[...], func=AF.Square))(),
                reads=[bb2], writes=[xsqb[c % 2]])
            S.comp('pe', (lambda c2=c, q=xsqs[c % 2]: lambda h: h.matmul(
                ps[bank2][0:1, :], sb['c_ones_col'][:, 0:1], q[...], start=(c2 == 0), stop=(c2 == NCH-1)))(),
                reads=[xsqb[c % 2], b('c_ones_col')], writes=[b(bank2)])
        S.comp('dve', lambda h: h.tensor_scalar(out=sb['m_row'][...], in0=ps[bank][0:1, :],
                                                scalar1=1.0/HD, scalar2=None, op0=ALU.mult),
               reads=[b(bank)], writes=[b('m_row')])
        S.comp('dve', lambda h: h.tensor_scalar(out=sb['t_row'][...], in0=ps[bank2][0:1, :],
                                                scalar1=1.0/HD, scalar2=None, op0=ALU.mult),
               reads=[b(bank2)], writes=[b('t_row')])
        S.comp('dve', lambda h: h.scalar_tensor_tensor(out=sb['v_row'][...], in0=sb['m_row'][...],
                                                       scalar=-1.0, in1=sb['m_row'][...],
                                                       op0=ALU.mult, op1=ALU.mult),
               reads=[b('m_row')], writes=[b('v_row')])
        S.comp('dve', lambda h: h.tensor_tensor(out=sb['v_row'][...], in0=sb['t_row'][...],
                                                in1=sb['v_row'][...], op=ALU.add),
               reads=[b('t_row'), b('v_row')], writes=[b('v_row')])
        epsc = self.constcol(NORM_EPS)
        S.comp('act', (lambda e2=epsc: lambda h: h.activation(out=sb['t_row'][...], in_=sb['v_row'][...],
                                             func=AF.Ln, bias=e2[0:1, 0:1]))(),
               reads=[b('v_row'), b(self.constname(NORM_EPS))], writes=[b('t_row')])
        S.comp('act', lambda h: h.activation(out=sb['rstd_row'][...], in_=sb['t_row'][...],
                                             func=AF.Exp, scale=-0.5),
               reads=[b('t_row')], writes=[b('rstd_row')])
        S.comp('dve', lambda h: h.tensor_tensor(out=sb['mr_row'][...], in0=sb['m_row'][...],
                                                in1=sb['rstd_row'][...], op=ALU.mult),
               reads=[b('m_row'), b('rstd_row')], writes=[b('mr_row')])
        for p0 in [0, 32, 64]:
            S.comp('dve', (lambda p2=p0: lambda h: h.tensor_scalar(
                out=sb['mrn_bf'][p2:p2+1, :], in0=sb['mr_row'][...],
                scalar1=-1.0, scalar2=None, op0=ALU.mult))(),
                reads=[b('mr_row')], writes=[b('mrn_bf')])
        S.comp('pe', lambda h: h.matmul(ps['pb1'][...], sb['c_onesrow_f'][0:1, :], sb['rstd_row'][...],
                                        start=True, stop=True),
               reads=[b('c_onesrow_f'), b('rstd_row')], writes=[b('pb1')])
        for c in range(NCH):
            sl = xsl[c % 2]; bb2 = xbufs[c % 2]
            S.dma('sp', (lambda c2=c, sl2=sl: lambda h: h.dma_start(
                out=sl2[...], in_=src[:, c2, tt*CT:(tt+1)*CT]))(),
                reads=[b(src_dram)], writes=[bb2])
            S.comp('dve', (lambda c2=c, sl2=sl: lambda h: h.tensor_tensor(
                out=sb[xr_dst][:, c2, :], in0=sl2[...], in1=ps['pb1'][...], op=ALU.mult))(),
                reads=[bb2, b('pb1')], writes=[b(xr_dst)])

    def emit_ffn(self, tt, phase):
        S, b = self.S, self.buf
        dr, sb, ps = self.dr, self.sb, self.ps
        wgu = [sb['wgu0'], sb['wgu1']]
        wgub = [b('wgu0'), b('wgu1')]
        wds = [sb['wdsl0'], sb['wdsl1']]
        wdsb = [b('wdsl0'), b('wdsl1')]
        wgc = dr['wg_c'].ap().rearrange("(r p) n -> p r n", p=128)
        wuc = dr['wu_c'].ap().rearrange("(r p) n -> p r n", p=128)
        wdc = dr['wd_c'].ap().rearrange("(r p) n -> p r n", p=128)
        NG = 4
        GH = NFH // NG
        dbank = ['pb0', 'pb1']
        src_x = dr['xT' if phase == 'f1' else 'x2T'].ap().rearrange("(c p) t -> p c t", p=128)
        dstn = 'x1T' if phase == 'f1' else 'outT'
        dst_ap = dr[dstn].ap().rearrange("(c p) t -> p c t", p=128)
        stat_bank = 'pb2'
        for G in range(NG):
            for hcl in range(GH):
                hc = G*GH + hcl
                slot = wgu[hc % 2]; slb = wgub[hc % 2]
                S.dma('sp', (lambda hc2=hc, s2=slot: lambda h: h.dma_start(
                    out=s2[:, 0, :, :], in_=wgc[:, :, hc2*128:(hc2+1)*128]))(),
                    reads=[b('wg_c')], writes=[slb])
                S.dma('sp', (lambda hc2=hc, s2=slot: lambda h: h.dma_start(
                    out=s2[:, 1, :, :], in_=wuc[:, :, hc2*128:(hc2+1)*128]))(),
                    reads=[b('wg_c')], writes=[slb])
                for gi, bank in [(0, 'pb2'), (1, 'pb3')]:
                    for c2 in range(NCH):
                        S.comp('pe', (lambda s2=slot, gi2=gi, c3=c2, bank2=bank:
                                      lambda h: h.matmul(ps[bank2][...], s2[:, gi2, c3, :],
                                                         sb['xr'][:, c3, :], start=(c3 == 0), stop=False))(),
                               reads=[slb, b('xr')], writes=[b(bank)])
                    swr = 'swg_row' if gi == 0 else 'swu_row'
                    S.comp('pe', (lambda bank2=bank, swr2=swr, hc2=hc:
                                  lambda h: h.matmul(ps[bank2][...], self.swsl(swr2, hc2),
                                                     sb['mrn_bf'][32*(hc2 % 3):32*(hc2 % 3)+1, :],
                                                     start=False, stop=True))(),
                           reads=[b(swr), b('mrn_bf')], writes=[b(bank)])
                S.comp('act', (lambda: lambda h: h.activation(out=sb['sil'][...], in_=ps['pb2'][...],
                                                              func=AF.Silu))(),
                       reads=[b('pb2')], writes=[b('sil')])
                S.comp('dve', (lambda hcl2=hcl: lambda h: h.tensor_tensor(
                    out=sb['hti'][:, hcl2, :], in0=ps['pb3'][...], in1=sb['sil'][...], op=ALU.mult))(),
                    reads=[b('pb3'), b('sil')], writes=[b('hti')])
            for oc in range(NCH):
                dbk = dbank[oc % 2]
                slot = wds[oc % 2]; slb = wdsb[oc % 2]
                S.dma('sp', (lambda G2=G, oc2=oc, s2=slot: lambda h: h.dma_start(
                    out=s2[...], in_=wdc[:, G2*GH:(G2+1)*GH, oc2*128:(oc2+1)*128]))(),
                    reads=[b('wd_c')], writes=[slb])
                for hcl in range(GH):
                    S.comp('pe', (lambda s2=slot, hcl2=hcl, dbk2=dbk:
                                  lambda h: h.matmul(ps[dbk2][...], s2[:, hcl2, :],
                                                     sb['hti'][:, hcl2, :], start=(hcl2 == 0),
                                                     stop=(hcl2 == GH-1)))(),
                           reads=[slb, b('hti')], writes=[b(dbk)])
                if G == 0:
                    S.comp('dve', (lambda oc2=oc, dbk2=dbk: lambda h: h.tensor_copy(
                        out=sb['facc'][:, oc2, :], in_=ps[dbk2][...]))(),
                        reads=[b(dbk)], writes=[b('facc')])
                elif G < NG-1:
                    S.comp('dve', (lambda oc2=oc, dbk2=dbk: lambda h: h.tensor_tensor(
                        out=sb['facc'][:, oc2, :], in0=ps[dbk2][...], in1=sb['facc'][:, oc2, :],
                        op=ALU.add))(),
                        reads=[b(dbk)], writes=[b('facc')])
                else:
                    xb = [b('x1oc0'), b('x1oc1')][oc % 2]
                    xoc = [sb['x1oc0'], sb['x1oc1']][oc % 2]
                    S.comp('dve', (lambda oc2=oc, dbk2=dbk: lambda h: h.tensor_tensor(
                        out=sb['toc'][...], in0=ps[dbk2][...], in1=sb['facc'][:, oc2, :], op=ALU.add))(),
                        reads=[b(dbk), b('facc')], writes=[b('toc')])
                    xs = [sb['xch0'], sb['xch1']][oc % 2]
                    xsb = [b('xch0'), b('xch1')][oc % 2]
                    S.dma('sp', (lambda oc2=oc, xs2=xs: lambda h: h.dma_start(
                        out=xs2[...], in_=src_x[:, oc2, tt*CT:(tt+1)*CT]))(),
                        reads=[b('xT' if phase == 'f1' else 'x2T')], writes=[xsb])
                    S.comp('dve', (lambda xs2=xs, xoc2=xoc: lambda h: h.scalar_tensor_tensor(
                        out=xoc2[...], in0=sb['toc'][...], scalar=0.5, in1=xs2[...],
                        op0=ALU.mult, op1=ALU.add))(),
                        reads=[b('toc'), xsb], writes=[xb])
                    S.dma('sp', (lambda oc2=oc, xoc2=xoc: lambda h: h.dma_start(
                        out=dst_ap[:, oc2, tt*CT:(tt+1)*CT], in_=xoc2[...]))(),
                        reads=[xb], writes=[b(dstn)])
                    if phase == 'f1':
                        S.comp('pe', (lambda oc2=oc, xoc2=xoc: lambda h: h.matmul(
                            ps['pa0'][0:1, :], sb['c_ones_col'][:, 0:1], xoc2[...],
                            start=(oc2 == 0), stop=(oc2 == NCH-1)))(),
                            reads=[xb, b('c_ones_col')], writes=[b('pa0')])
                        sq = sb['xsq0']
                        sqb = b('xsq0')
                        S.comp('act', (lambda xoc2=xoc, sq2=sq: lambda h: h.activation(
                            out=sq2[...], in_=xoc2[...], func=AF.Square))(),
                            reads=[xb], writes=[sqb])
                        S.comp('pe', (lambda oc2=oc, sq2=sq: lambda h: h.matmul(
                            ps['pa1'][0:1, :], sb['c_ones_col'][:, 0:1], sq2[...],
                            start=(oc2 == 0), stop=(oc2 == NCH-1)))(),
                            reads=[sqb, b('c_ones_col')], writes=[b('pa1')])

    def emit_ln2_and_export(self, tt):
        S, b = self.S, self.buf
        dr, sb, ps = self.dr, self.sb, self.ps
        S.comp('dve', lambda h: h.tensor_scalar(out=sb['m_row'][...], in0=ps['pa0'][0:1, :],
                                                scalar1=1.0/HD, scalar2=None, op0=ALU.mult),
               reads=[b('pa0')], writes=[b('m_row')])
        S.comp('dve', lambda h: h.tensor_scalar(out=sb['t_row'][...], in0=ps['pa1'][0:1, :],
                                                scalar1=1.0/HD, scalar2=None, op0=ALU.mult),
               reads=[b('pa1')], writes=[b('t_row')])
        self._ln_rows()
        S.dma('sp', (lambda tt2=tt: lambda h: h.dma_start(out=dr['mr_in'].ap()[0:1, tt2*CT:(tt2+1)*CT],
                                                          in_=sb['mr_row'][...]))(),
              reads=[b('mr_row')], writes=[b('mr_in')])
        src = dr['x1T'].ap().rearrange("(c p) t -> p c t", p=128)
        xr2ap = dr[f'xr2_in{tt}'].ap().rearrange("(c p) t -> p c t", p=128)
        for c in range(NCH):
            xs = [sb['xch0'], sb['xch1']][c % 2]
            xsb = [b('xch0'), b('xch1')][c % 2]
            S.dma('sp', (lambda c2=c, xs2=xs: lambda h: h.dma_start(
                out=xs2[...], in_=src[:, c2, tt*CT:(tt+1)*CT]))(),
                reads=[b('x1T')], writes=[xsb])
            S.comp('dve', (lambda c2=c, xs2=xs: lambda h: h.tensor_tensor(
                out=sb['xr'][:, c2, :], in0=xs2[...], in1=ps['pb1'][...], op=ALU.mult))(),
                reads=[xsb, b('pb1')], writes=[b('xr')])
            S.dma('sp', (lambda c2=c: lambda h: h.dma_start(
                out=xr2ap[:, c2, :], in_=sb['xr'][:, c2, :]))(),
                reads=[b('xr')], writes=[b(f'xr2_in{tt}')])
        S.cc((lambda tt2=tt: lambda h: h.collective_compute(
            "AllGather", ALU.bypass,
            replica_groups=[[0, 1], [2, 3], [4, 5], [6, 7]],
            ins=[dr[f'xr2_in{tt2}'].ap().opt()],
            outs=[dr[f'xr2_out{tt2}'].ap().opt()]))(),
            reads=[b(f'xr2_in{tt}')], writes=[b(f'xr2_out{tt}')])

    def emit_mr_cc(self):
        S, b = self.S, self.buf
        dr = self.dr
        S.cc(lambda h: h.collective_compute(
            "AllGather", ALU.bypass,
            replica_groups=[[0, 1], [2, 3], [4, 5], [6, 7]],
            ins=[dr['mr_in'].ap().opt()],
            outs=[dr['mr_out'].ap().opt()]),
            reads=[b('mr_in')], writes=[b('mr_out')])
        if self.dbg:
            for tt in range(self.nt):
                S.dma('pool', (lambda tt2=tt: lambda h: h.dma_start(
                    out=dr['dbg_xr2'].ap()[:, tt2*CT:(tt2+1)*CT], in_=dr[f'xr2_out{tt2}'].ap()))(),
                    reads=[b(f'xr2_out{tt}')], writes=[b('dbg_xr2')])
            S.dma('pool', lambda h: h.dma_start(out=dr['dbg_mr'].ap(), in_=dr['mr_out'].ap()),
                  reads=[b('mr_out')], writes=[b('dbg_mr')])


    def emit_attn_ct(self, g, sub=99):
        S, b = self.S, self.buf
        dr, sb, ps = self.dr, self.sb, self.ps
        first = (g == 0)
        half, tt = g % 2, g // 2
        src = dr[f'xr2_out{tt}'].ap()[half*HD:(half+1)*HD, :].rearrange("(c p) t -> p c t", p=128)
        S.dma('pool', (lambda s2=src: lambda h: h.dma_start(out=sb['xr2t'][...], in_=s2))(),
              reads=[b(f'xr2_out{tt}')], writes=[b('xr2t')])
        S.dma('sp', (lambda half2=half, tt2=tt: lambda h: h.dma_start(
            out=sb['amr_row'][...], in_=dr['mr_out'].ap()[half2:half2+1, tt2*CT:(tt2+1)*CT]))(),
            reads=[b('mr_out')], writes=[b('amr_row')])
        for p0 in [0, 32, 64]:
            S.comp('dve', (lambda p2=p0: lambda h: h.tensor_scalar(
                out=sb['amrn_bf'][p2:p2+1, :], in0=sb['amr_row'][...],
                scalar1=-1.0, scalar2=None, op0=ALU.mult))(),
                reads=[b('amr_row')], writes=[b('amrn_bf')])
        jobs = [('wq_c', 'wqT', 'swq_row', 0, 'qc'), ('wk_c', 'wkT', 'swk_row', 1, 'kc'),
                ('wv_c', 'wvT', 'swv_row', 2, 'vc')]
        wslots = [('wp0', sb['wp0']), ('wp1', sb['wp1'])]
        ups = [('up0', sb['up0']), ('up0', sb['up0'])]
        nio = 0
        for ji, (wc, wsrc, swr, ti, cn) in enumerate(jobs):
            for oc in range(8):
                sn, slot = wslots[nio % 2]
                un, ut = ups[nio % 2]
                nio += 1
                cap = dr[wc].ap().rearrange("(r p) n -> p r n", p=128)[:, :, oc*128:(oc+1)*128]
                if first:
                    sap = dr[wsrc].ap().rearrange("(r p) n -> p r n", p=128)[:, :, oc*128:(oc+1)*128]
                    S.dma('pool', (lambda d2=slot, s2=sap: lambda h: h.dma_start(out=d2[...], in_=s2))(),
                          reads=[b(wsrc)], writes=[b(sn)])
                    S.dma('sp', (lambda d2=cap, s2=slot: lambda h: h.dma_start(out=d2, in_=s2[...]))(),
                          reads=[b(sn)], writes=[b(wc)])
                else:
                    S.dma('sp', (lambda d2=slot, s2=cap: lambda h: h.dma_start(out=d2[...], in_=s2))(),
                          reads=[b(wc)], writes=[b(sn)])
                bank = ['pa0', 'pa1'][oc % 2]
                for c2 in range(NCH):
                    S.comp('pe', (lambda s2=slot, c3=c2, bank2=bank: lambda h: h.matmul(
                        ps[bank2][...], s2[:, c3, :], sb['xr2t'][:, c3, :],
                        start=(c3 == 0), stop=False))(),
                        reads=[b(sn), b('xr2t')], writes=[b(bank)])
                S.comp('pe', (lambda bank2=bank, swr2=swr, oc2=oc: lambda h: h.matmul(
                    ps[bank2][...], self.swsl(swr2, oc2),
                    sb['amrn_bf'][32*(oc2 % 3):32*(oc2 % 3)+1, :],
                    start=False, stop=True))(),
                    reads=[b(swr), b('amrn_bf')], writes=[b(bank)])
                if first:
                    S.comp('dve', (lambda u2=ut: lambda h: h.memset(u2[:, 0:3], 0.0))(),
                           writes=[b(un)])
                else:
                    S.comp('dve', (lambda u2=ut, t2=ti, o2=oc: lambda h: h.tensor_copy(
                        out=u2[:, 0:3], in_=sb['tls'][:, t2, o2, :]))(),
                        reads=[b('tls')], writes=[b(un)])
                S.comp('act', (lambda bank2=bank, u2=ut: lambda h: h.activation(
                    out=u2[:, 3:515], in_=ps[bank2][...], func=AF.Copy))(),
                    reads=[b(bank)], writes=[b(un)])
                S.comp('dve', (lambda u2=ut, t2=ti, o2=oc: lambda h: h.tensor_copy(
                    out=sb['tls'][:, t2, o2, :], in_=u2[:, 512:515]))(),
                    reads=[b(un)], writes=[b('tls')])
                wv = lambda k2, t2=ti, h2=oc: sb['c_convw'][:, t2, h2, k2:k2+1]
                S.comp('dve', (lambda u2=ut, w0=wv(0): lambda h: h.tensor_scalar(
                    out=sb['ct1'][...], in0=u2[:, 0:512], scalar1=w0, scalar2=None, op0=ALU.mult))(),
                    reads=[b(un), b('c_convw')], writes=[b('ct1')])
                S.comp('dve', (lambda u2=ut, w1=wv(1): lambda h: h.scalar_tensor_tensor(
                    out=sb['ct2'][...], in0=u2[:, 1:513], scalar=w1, in1=sb['ct1'][...],
                    op0=ALU.mult, op1=ALU.add))(),
                    reads=[b(un), b('ct1'), b('c_convw')], writes=[b('ct2')])
                S.comp('dve', (lambda u2=ut, w2=wv(2): lambda h: h.scalar_tensor_tensor(
                    out=sb['ct1'][...], in0=u2[:, 2:514], scalar=w2, in1=sb['ct2'][...],
                    op0=ALU.mult, op1=ALU.add))(),
                    reads=[b(un), b('ct2'), b('c_convw')], writes=[b('ct1')])
                S.comp('dve', (lambda u2=ut, w3=wv(3): lambda h: h.scalar_tensor_tensor(
                    out=sb['ct2'][...], in0=u2[:, 3:515], scalar=w3, in1=sb['ct1'][...],
                    op0=ALU.mult, op1=ALU.add))(),
                    reads=[b(un), b('ct1'), b('c_convw')], writes=[b('ct2')])
                S.comp('act', (lambda cn2=cn, o2=oc: lambda h: h.activation(
                    out=sb[cn2][:, o2, :], in_=sb['ct2'][...], func=AF.Silu))(),
                    reads=[b('ct2')], writes=[b(cn)])
        sn, slot = wslots[nio % 2]; nio += 1
        capg = dr['wg1_c'].ap().rearrange("(r p) n -> p r n", p=128)
        if first:
            sapg = dr['wg1T'].ap().rearrange("(r p) n -> p r n", p=128)
            S.dma('pool', (lambda d2=slot, s2=sapg: lambda h: h.dma_start(out=d2[...], in_=s2))(),
                  reads=[b('wg1T')], writes=[b(sn)])
            S.dma('sp', (lambda d2=capg, s2=slot: lambda h: h.dma_start(out=d2, in_=s2[...]))(),
                  reads=[b(sn)], writes=[b('wg1_c')])
        else:
            S.dma('sp', (lambda d2=slot, s2=capg: lambda h: h.dma_start(out=d2[...], in_=s2))(),
                  reads=[b('wg1_c')], writes=[b(sn)])
        for c2 in range(NCH):
            S.comp('pe', (lambda s2=slot, c3=c2: lambda h: h.matmul(
                ps['pa0'][...], s2[:, c3, :], sb['xr2t'][:, c3, :], start=(c3 == 0), stop=False))(),
                reads=[b(sn), b('xr2t')], writes=[b('pa0')])
        S.comp('pe', lambda h: h.matmul(ps['pa0'][...], self.swsl('swg1_row', 0),
                                        sb['amrn_bf'][0:1, :], start=False, stop=True),
               reads=[b('swg1_row'), b('amrn_bf')], writes=[b('pa0')])
        S.comp('act', lambda h: h.activation(out=sb['glt'][...], in_=ps['pa0'][...], func=AF.Copy),
               reads=[b('pa0')], writes=[b('glt')])
        if sub < 1:
            return
        for h_ in range(8):
            self.emit_head_stats(h_)
        if sub < 2:
            return
        ec = self.constcol(EPS_STAT)
        l2c = self.constcol(L2_EPS)
        def act8(out, in_ap, func, scale=None, bias=None, rd=(), wr=(), bias_buf=None):
            kw = {}
            if scale is not None: kw['scale'] = scale
            if bias is not None: kw['bias'] = bias
            rds = list(rd) + ([bias_buf] if bias_buf else [])
            S.comp('act', (lambda o2=out, i2=in_ap, f2=func, kw2=dict(kw):
                           lambda h: h.activation(out=o2, in_=i2, func=f2, **kw2))(),
                   reads=[b(x) for x in rds], writes=[b(x) for x in wr])
        act8(sb['sl2t'][...], sb['st_all'][:, 0, :], AF.Ln, bias=ec[0:8, 0:1], rd=['st_all'], wr=['sl2t'], bias_buf=self.constname(EPS_STAT))
        act8(sb['sl2t'][...], sb['sl2t'][...], AF.Exp, scale=0.5, rd=['sl2t'], wr=['sl2t'])
        act8(sb['den1'][...], sb['st_all'][:, 3, :], AF.Ln, bias=ec[0:8, 0:1], rd=['st_all'], wr=['den1'], bias_buf=self.constname(EPS_STAT))
        act8(sb['den1'][...], sb['den1'][...], AF.Exp, scale=0.5, rd=['den1'], wr=['den1'])
        act8(sb['den2'][...], sb['st_all'][:, 4, :], AF.Ln, bias=ec[0:8, 0:1], rd=['st_all'], wr=['den2'], bias_buf=self.constname(EPS_STAT))
        act8(sb['den2'][...], sb['den2'][...], AF.Exp, scale=0.5, rd=['den2'], wr=['den2'])
        S.comp('dve', lambda h: h.tensor_tensor(out=sb['den1'][...], in0=sb['den1'][...],
                                                in1=sb['den2'][...], op=ALU.mult),
               reads=[b('den1'), b('den2')], writes=[b('den1')])
        S.comp('dve', lambda h: h.tensor_scalar(out=sb['den1'][...], in0=sb['den1'][...],
                                                scalar1=EPS_STAT, scalar2=None, op0=ALU.add),
               reads=[b('den1')], writes=[b('den1')])
        S.comp('dve', lambda h: h.reciprocal(out=sb['den2'][...], in_=sb['den1'][...]),
               reads=[b('den1')], writes=[b('den2')])
        S.comp('dve', lambda h: h.tensor_tensor(out=sb['scost'][...], in0=sb['st_all'][:, 2, :],
                                                in1=sb['den2'][...], op=ALU.mult),
               reads=[b('st_all'), b('den2')], writes=[b('scost')])
        S.comp('dve', lambda h: h.tensor_scalar(out=sb['scost'][...], in0=sb['scost'][...],
                                                scalar1=-1.0, scalar2=1.0, op0=ALU.mult, op1=ALU.add),
               reads=[b('scost')], writes=[b('scost')])
        S.dma('pool', lambda h: h.dma_start(out=sb['s3'][0:1, :], in_=sb['sl2t'][...]),
              reads=[b('sl2t')], writes=[b('s3')])
        S.dma('pool', lambda h: h.dma_start(out=sb['s3'][1:2, :], in_=sb['st_all'][:, 1, :]),
              reads=[b('st_all')], writes=[b('s3')])
        S.dma('pool', lambda h: h.dma_start(out=sb['s3'][2:3, :], in_=sb['scost'][...]),
              reads=[b('scost')], writes=[b('s3')])
        for h_ in range(8):
            for w1n, w2n, bnk, b1 in [('c_wb1', 'c_wb2z', 'pa1', 'c_bb1'),
                                      ('c_wa1', 'c_wa2z', 'pa2', 'c_ba1')]:
                S.comp('pe', (lambda h2=h_, w1=w1n: lambda h: h.matmul(
                    ps['pa0'][0:32, :], sb[w1][...], sb['s3'][:, h2*CT:(h2+1)*CT],
                    start=True, stop=True))(),
                    reads=[b(w1n), b('s3')], writes=[b('pa0')])
                S.comp('act', (lambda b1n=b1: lambda h: h.activation(
                    out=sb['hb'][...], in_=ps['pa0'][0:32, :], func=AF.Silu, bias=sb[b1n][...]))(),
                    reads=[b('pa0'), b(b1)], writes=[b('hb')])
                S.comp('pe', (lambda h2=h_, w2=w2n, bnk2=bnk: lambda h: h.matmul(
                    ps[bnk2][0:8, :], sb[w2][:, h2*8:(h2+1)*8], sb['hb'][...],
                    start=(h2 == 0), stop=(h2 == 7)))(),
                    reads=[b(w2n), b('hb')], writes=[b(bnk)])
        S.comp('act', lambda h: h.activation(out=sb['beta8'][...], in_=ps['pa1'][0:8, :],
                                             func=AF.Sigmoid, bias=sb['c_bb2'][...]),
               reads=[b('pa1'), b('c_bb2')], writes=[b('beta8')])
        S.comp('act', lambda h: h.activation(out=sb['amp8'][...], in_=ps['pa2'][0:8, :],
                                             func=AF.Identity, bias=sb['c_ba2'][...]),
               reads=[b('pa2'), b('c_ba2')], writes=[b('amp8')])
        act8(sb['rsqk8'][...], sb['st_all'][:, 5, :], AF.Ln, rd=['st_all'], wr=['rsqk8'])
        act8(sb['rsqk8'][...], sb['rsqk8'][...], AF.Exp, scale=0.5, rd=['rsqk8'], wr=['rsqk8'])
        S.comp('dve', lambda h: h.tensor_scalar(out=sb['rsqk8'][...], in0=sb['rsqk8'][...],
                                                scalar1=1e-12, scalar2=None, op0=ALU.max),
               reads=[b('rsqk8')], writes=[b('rsqk8')])
        S.comp('dve', lambda h: h.reciprocal(out=sb['rsqk8'][...], in_=sb['rsqk8'][...]),
               reads=[b('rsqk8')], writes=[b('rsqk8')])
        S.comp('dve', lambda h: h.tensor_tensor(out=sb['ampz8'][...], in0=sb['amp8'][...],
                                                in1=sb['rsqk8'][...], op=ALU.mult),
               reads=[b('amp8'), b('rsqk8')], writes=[b('ampz8')])
        act8(sb['rkn8'][...], sb['st_all'][:, 5, :], AF.Ln, bias=l2c[0:8, 0:1], rd=['st_all'], wr=['rkn8'], bias_buf=self.constname(L2_EPS))
        act8(sb['rkn8'][...], sb['rkn8'][...], AF.Exp, scale=-0.5, rd=['rkn8'], wr=['rkn8'])
        act8(sb['rqn8'][...], sb['st_all'][:, 6, :], AF.Ln, bias=l2c[0:8, 0:1], rd=['st_all'], wr=['rqn8'], bias_buf=self.constname(L2_EPS))
        qkc2 = self.constcol(float(-0.5*np.log(DK)))
        act8(sb['rqn8'][...], sb['rqn8'][...], AF.Exp, scale=-0.5, bias=qkc2[0:8, 0:1], rd=['rqn8'], wr=['rqn8'], bias_buf=self.constname(float(-0.5*np.log(DK))))
        for rsrc, rdst in [('ampz8', 'ampz8c'), ('rkn8', 'rkn8c'), ('rqn8', 'rqn8c')]:
            S.comp('dve', (lambda s2=rsrc, d2=rdst: lambda h: h.tensor_copy(
                out=sb[d2][...], in_=sb[s2][...]))(),
                reads=[b(rsrc)], writes=[b(rdst)])
        for cc in range(4):
            S.comp('pe', (lambda cc2=cc: lambda h: h.transpose(
                ps['pa2'][0:128, 0:8], sb['beta8'][:, cc2*128:(cc2+1)*128], sb['c_identf'][0:8, 0:8]))(),
                reads=[b('beta8'), b('c_identf')], writes=[b('pa2')])
            S.comp('dve', (lambda cc2=cc: lambda h: h.tensor_copy(
                out=sb['betaT'][:, cc2*8:(cc2+1)*8], in_=ps['pa2'][0:128, 0:8]))(),
                reads=[b('pa2')], writes=[b('betaT')])
        S.comp('dve', lambda h: h.tensor_scalar(out=sb['betaTn'][...], in0=sb['betaT'][...],
                                                scalar1=-1.0, scalar2=None, op0=ALU.mult),
               reads=[b('betaT')], writes=[b('betaTn')])
        if sub < 3:
            return
        for h_ in range(8):
            self.emit_head_decay(g, h_, sub)
        if sub < 5:
            return
        woap = dr['wo_c'].ap().rearrange("(r p) n -> p r n", p=128)
        ppap = dr[f'pp_in{tt}'].ap().rearrange("(s c p) t -> s p c t", s=2, p=128)
        wopair = [('wosl0', sb['wosl0']), ('wosl0', sb['wosl0'])]
        for oc in range(NCH):
            bank = ['pa0', 'pa1'][oc % 2]
            wn, wt = wopair[oc % 2]
            if first:
                wsap = dr['woT'].ap().rearrange("(r p) n -> p r n", p=128)[:, :, oc*128:(oc+1)*128]
                S.dma('pool', (lambda d2=wt, s2=wsap: lambda h: h.dma_start(out=d2[...], in_=s2))(),
                      reads=[b('woT')], writes=[b(wn)])
                cap2 = woap[:, :, oc*128:(oc+1)*128]
                S.dma('sp', (lambda d2=cap2, s2=wt: lambda h: h.dma_start(out=d2, in_=s2[...]))(),
                      reads=[b(wn)], writes=[b('wo_c')])
            else:
                cap2 = woap[:, :, oc*128:(oc+1)*128]
                S.dma('sp', (lambda d2=wt, s2=cap2: lambda h: h.dma_start(out=d2[...], in_=s2))(),
                      reads=[b('wo_c')], writes=[b(wn)])
            for r in range(8):
                S.comp('pe', (lambda r2=r, bank2=bank, wt2=wt: lambda h: h.matmul(
                    ps[bank2][...], wt2[:, r2, :], sb['o_ct'][:, r2, :],
                    start=(r2 == 0), stop=(r2 == 7)))(),
                    reads=[b(wn), b('o_ct')], writes=[b(bank)])
            pn = ['zt', 'sp1'][oc % 2]
            if oc % 2 == 0:
                S.comp('act', (lambda bank2=bank, pn2=pn: lambda h: h.activation(
                    out=sb[pn2][...], in_=ps[bank2][...], func=AF.Copy))(),
                    reads=[b(bank)], writes=[b(pn)])
            else:
                S.comp('dve', (lambda bank2=bank, pn2=pn: lambda h: h.tensor_copy(
                    out=sb[pn2][...], in_=ps[bank2][...]))(),
                    reads=[b(bank)], writes=[b(pn)])
            S.dma('sp', (lambda half2=half, oc2=oc, pn2=pn: lambda h: h.dma_start(
                out=ppap[half2, :, oc2, :], in_=sb[pn2][...]))(),
                reads=[b(pn)], writes=[b(f'pp_in{tt}')])

    def emit_head_stats(self, h_):
        S = self.S
        sb, ps, b = self.par_ctx(h_)
        S.comp('pe', (lambda h2=h_: lambda h: h.matmul(
            ps['pa0'][...], sb['c_wsv'][...], sb['kc'][:, h2, :], start=True, stop=True))(),
            reads=[b('c_wsv'), b('kc')], writes=[b('pa0')])
        S.comp('act', lambda h: h.activation(out=sb['vh_h'][...], in_=ps['pa0'][...], func=AF.Copy),
               reads=[b('pa0')], writes=[b('vh_h')])
        S.comp('dve', (lambda h2=h_: lambda h: h.tensor_tensor(
            out=sb['e_h'][...], in0=sb['vh_h'][...], in1=sb['vc'][:, h2, :], op=ALU.subtract))(),
            reads=[b('vh_h'), b('vc')], writes=[b('e_h')])
        stats = [
            ('act', lambda h: h.activation(out=sb['stsc0'][...], in_=sb['e_h'][...], func=AF.Square), ['e_h'], 'stsc0'),
            ('act', lambda h: h.activation(out=sb['stsc0'][...], in_=sb['e_h'][...], func=AF.Abs), ['e_h'], 'stsc0'),
            ('dve', (lambda h2=h_: lambda h: h.tensor_tensor(out=sb['stsc0'][...], in0=sb['vh_h'][...], in1=sb['vc'][:, h2, :], op=ALU.mult))(), ['vh_h', 'vc'], 'stsc0'),
            ('act', lambda h: h.activation(out=sb['stsc0'][...], in_=sb['vh_h'][...], func=AF.Square), ['vh_h'], 'stsc0'),
            ('act', (lambda h2=h_: lambda h: h.activation(out=sb['stsc0'][...], in_=sb['vc'][:, h2, :], func=AF.Square))(), ['vc'], 'stsc0'),
            ('act', (lambda h2=h_: lambda h: h.activation(out=sb['stsc0'][...], in_=sb['kc'][:, h2, :], func=AF.Square))(), ['kc'], 'stsc0'),
            ('act', (lambda h2=h_: lambda h: h.activation(out=sb['stsc0'][...], in_=sb['qc'][:, h2, :], func=AF.Square))(), ['qc'], 'stsc0'),
        ]
        for si, (eng, fn, rd, scr) in enumerate(stats):
            S.comp(eng, fn, reads=[b(x) for x in rd], writes=[b(scr)])
            S.comp('pe', (lambda si2=si, scr2=scr: lambda h: h.matmul(
                ps['pa1'][0:8, :], sb['c_zc'][:, si2, 0:8], sb[scr2][...],
                start=(si2 == 0), stop=(si2 == 6)))(),
                reads=[b(scr), b('c_zc')], writes=[b('pa1')])
        S.comp('act', lambda h: h.activation(out=sb['tmp8'][...], in_=ps['pa1'][0:8, :], func=AF.Copy),
               reads=[b('pa1')], writes=[b('tmp8')])
        S.dma('pool', (lambda h2=h_: lambda h: h.dma_start(
            out=sb['st_all'][h2:h2+1, :, :], in_=sb['tmp8'][0:7, :]))(),
            reads=[b('tmp8')], writes=[b('st_all')])

    def emit_head_decay(self, g, h_, sub):
        S = self.S
        sb, ps, b = self.par_ctx(h_)
        par = h_ % 2
        ib = 64 if par else 0
        for rn, dst in [('ampz8c', 'ampb'), ('rkn8c', 'knb'), ('rqn8c', 'qnb')]:
            S.comp('pe', (lambda rn2=rn, h2=h_: lambda h: h.matmul(
                ps['pa0'][...], sb['c_zrow8'][:, h2, :], sb[rn2][...],
                start=True, stop=True))(),
                reads=[b(rn), b('c_zrow8')], writes=[b('pa0')])
            S.comp('act', (lambda d2=dst: lambda h: h.activation(
                out=sb[d2][...], in_=ps['pa0'][...], func=AF.Copy))(),
                reads=[b('pa0')], writes=[b(dst)])
        S.comp('dve', (lambda h2=h_: lambda h: h.tensor_tensor(
            out=sb['zt'][...], in0=sb['kc'][:, h2, :], in1=sb['ampb'][...], op=ALU.mult))(),
            reads=[b('kc'), b('ampb')], writes=[b('zt')])
        dtb_h = sb['c_dtb'][:, h_:h_+1]
        S.comp('act', (lambda d2=dtb_h: lambda h: h.activation(
            out=sb['sp1'][...], in_=sb['zt'][...], func=AF.Relu, bias=d2))(),
            reads=[b('zt'), b('c_dtb')], writes=[b('sp1')])
        S.comp('act', (lambda d2=dtb_h: lambda h: h.activation(
            out=sb['ab'][...], in_=sb['zt'][...], func=AF.Abs, bias=d2))(),
            reads=[b('zt'), b('c_dtb')], writes=[b('ab')])
        S.comp('act', lambda h: h.activation(out=sb['ab'][...], in_=sb['ab'][...],
                                             func=AF.Exp, scale=-1.0),
               reads=[b('ab')], writes=[b('ab')])
        onec = self.constcol(1.0)
        S.comp('act', (lambda o2=onec: lambda h: h.activation(
            out=sb['ab'][...], in_=sb['ab'][...], func=AF.Ln, bias=o2[:, 0:1]))(),
            reads=[b('ab'), b(self.constname(1.0))], writes=[b('ab')])
        S.comp('dve', lambda h: h.tensor_tensor(out=sb['spt'][...], in0=sb['sp1'][...],
                                                in1=sb['ab'][...], op=ALU.add),
               reads=[b('sp1'), b('ab')], writes=[b('spt')])
        S.comp('dve', (lambda h2=h_: lambda h: h.tensor_scalar(
            out=sb['gt'][...], in0=sb['spt'][...], scalar1=sb['c_negea'][:, h2:h2+1],
            scalar2=None, op0=ALU.mult))(),
            reads=[b('spt'), b('c_negea')], writes=[b('gt')])
        S.comp('act', lambda h: h.activation(out=sb['D_h'][...], in_=sb['gt'][...], func=AF.Exp),
               reads=[b('gt')], writes=[b('D_h')])
        S.comp('dve', lambda h: h.tensor_tensor_scan(out=sb['G_h'][...], data0=sb['c_ones_f'][...],
                                                     data1=sb['gt'][...], initial=0.0,
                                                     op0=ALU.mult, op1=ALU.add),
               reads=[b('c_ones_f'), b('gt')], writes=[b('G_h')])
        S.comp('dve', (lambda h2=h_: lambda h: h.tensor_tensor(
            out=sb['kn_h'][...], in0=sb['kc'][:, h2, :], in1=sb['knb'][...], op=ALU.mult))(),
            reads=[b('kc'), b('knb')], writes=[b('kn_h')])
        S.comp('dve', (lambda h2=h_: lambda h: h.tensor_tensor(
            out=sb['qn_h'][...], in0=sb['qc'][:, h2, :], in1=sb['qnb'][...], op=ALU.mult))(),
            reads=[b('qc'), b('qnb')], writes=[b('qn_h')])
        S.comp('dve', lambda h: h.tensor_tensor(out=sb['p4'][...], in0=sb['qn_h'][...],
                                                in1=sb['kn_h'][...], op=ALU.mult),
               reads=[b('qn_h'), b('kn_h')], writes=[b('p4')])
        S.comp('pe', lambda h: h.matmul(ps['pa1'][0:16, :], sb['c_zc'][:, 8, :], sb['p4'][...],
                                        start=True, stop=False),
               reads=[b('c_zc'), b('p4')], writes=[b('pa1')])
        pcur = 'D_h'
        for dd in range(1, 8):
            if dd >= 2:
                pnew = ['pc0', 'pc1'][dd % 2]
                S.comp('dve', (lambda d2=dd, pc=pcur, pn=pnew: lambda h: h.tensor_tensor(
                    out=sb[pn][:, d2-1:512], in0=sb[pc][:, d2-1:512],
                    in1=sb['D_h'][:, 0:512-(d2-1)], op=ALU.mult))(),
                    reads=[b(pcur), b('D_h')], writes=[b(pnew)])
                pcur = pnew
            S.comp('dve', (lambda d2=dd, pc=pcur: lambda h: h.tensor_tensor(
                out=sb['ttb'][:, d2:512], in0=sb[pc][:, d2:512],
                in1=sb['kn_h'][:, 0:512-d2], op=ALU.mult))(),
                reads=[b(pcur), b('kn_h')], writes=[b('ttb')])
            S.comp('dve', (lambda d2=dd: lambda h: h.tensor_tensor(
                out=sb['p3'][:, d2:512], in0=sb['ttb'][:, d2:512],
                in1=sb['kn_h'][:, d2:512], op=ALU.mult))(),
                reads=[b('ttb'), b('kn_h')], writes=[b('p3')])
            S.comp('pe', (lambda d2=dd: lambda h: h.matmul(
                ps['pa1'][0:16, :], sb['c_zc'][:, d2, :], sb['p3'][...],
                start=False, stop=False))(),
                reads=[b('c_zc'), b('p3')], writes=[b('pa1')])
            S.comp('dve', (lambda d2=dd: lambda h: h.tensor_tensor(
                out=sb['p4'][:, d2:512], in0=sb['ttb'][:, d2:512],
                in1=sb['qn_h'][:, d2:512], op=ALU.mult))(),
                reads=[b('ttb'), b('qn_h')], writes=[b('p4')])
            S.comp('pe', (lambda d2=dd: lambda h: h.matmul(
                ps['pa1'][0:16, :], sb['c_zc'][:, 8+d2, :], sb['p4'][...],
                start=False, stop=(d2 == 7)))(),
                reads=[b('c_zc'), b('p4')], writes=[b('pa1')])
        S.comp('act', lambda h: h.activation(out=sb['pmrow'][...], in_=ps['pa1'][0:16, :], func=AF.Copy),
               reads=[b('pa1')], writes=[b('pmrow')])
        for cc in range(4):
            S.comp('pe', (lambda cc2=cc, ib2=ib: lambda h: h.transpose(
                ps['patb'][:, 0:16], sb['pmrow'][:, cc2*128:(cc2+1)*128],
                sb['c_ident'][ib2:ib2+16, ib2:ib2+16]))(),
                reads=[b('pmrow'), b('c_ident')], writes=[b('patb')])
            S.comp('dve', (lambda cc2=cc: lambda h: h.tensor_copy(
                out=sb['TRt'][:, cc2, :], in_=ps['patb'][:, 0:16]))(),
                reads=[b('patb')], writes=[b('TRt')])
        S.comp('pe', (lambda h2=h_: lambda h: h.matmul(
            ps['pa0'][...], sb['c_wg2'][:, h2, :], sb['glt'][...], start=True, stop=True))(),
            reads=[b('c_wg2'), b('glt')], writes=[b('pa0')])
        S.comp('act', (lambda h2=h_: lambda h: h.activation(
            out=sb['sgate'][:, h2, :], in_=ps['pa0'][...], func=AF.Sigmoid,
            bias=sb['c_bg2'][:, h2:h2+1]))(),
            reads=[b('pa0'), b('c_bg2')], writes=[b('sgate')])
        if sub >= 4:
            for cc in range(4):
                self.emit_chunk(g, h_, cc, csub=self.csub)

    def emit_chunk(self, g, h_, cc, csub=99):
        S = self.S
        dr = self.dr
        sb, ps, b = self.par_ctx(h_)
        s0 = cc * 128
        sl = slice(s0, s0 + 128)
        Ssrc = 'Sfp0'
        Sdst = 'Sfp0'
        seq_first = (g == 0 and cc == 0)
        if cc == 0:
            colA_src = 'c_zero'
        else:
            S.comp('dve', (lambda s2=s0: lambda h: h.tensor_scalar(
                out=sb['colA'][...], in0=sb['G_h'][:, s2-1:s2], scalar1=-1.0, scalar2=None,
                op0=ALU.mult))(),
                reads=[b('G_h')], writes=[b('colA')])
            colA_src = 'colA'
        S.comp('act', (lambda s2=sl, ca=colA_src: lambda h: h.activation(
            out=sb['lv_ae'][...], in_=sb['G_h'][:, s2], func=AF.Exp, bias=sb[ca][:, 0:1]))(),
            reads=[b('G_h'), b(colA_src)], writes=[b('lv_ae')])
        S.comp('dve', (lambda s2=sl: lambda h: h.tensor_tensor(
            out=sb['m_W'][...], in0=sb['kn_h'][:, s2], in1=sb['lv_ae'][...], op=ALU.mult))(),
            reads=[b('kn_h'), b('lv_ae')], writes=[b('m_W')])
        S.comp('dve', (lambda s2=sl: lambda h: h.tensor_tensor(
            out=sb['m_qt'][...], in0=sb['qn_h'][:, s2], in1=sb['lv_ae'][...], op=ALU.mult))(),
            reads=[b('qn_h'), b('lv_ae')], writes=[b('m_qt')])
        S.comp('act', (lambda s2=sl, e2=s0+127: lambda h: h.activation(
            out=sb['lv_be'][...], in_=sb['G_h'][:, s2], func=AF.Exp, scale=-1.0,
            bias=sb['G_h'][:, e2:e2+1]))(),
            reads=[b('G_h')], writes=[b('lv_be')])
        S.comp('dve', (lambda s2=sl: lambda h: h.tensor_tensor(
            out=sb['m_Wb'][...], in0=sb['kn_h'][:, s2], in1=sb['lv_be'][...], op=ALU.mult))(),
            reads=[b('kn_h'), b('lv_be')], writes=[b('m_Wb')])
        S.comp('act', (lambda e2=s0+127, ca=colA_src: lambda h: h.activation(
            out=sb['lamc'][...], in_=sb['G_h'][:, e2:e2+1], func=AF.Exp, bias=sb[ca][:, 0:1]))(),
            reads=[b('G_h'), b(colA_src)], writes=[b('lamc')])
        S.comp('pe', (lambda h2=h_, s2=sl: lambda h: h.transpose(
            ps['patb'][:, 0:128], sb['vc'][:, h2, s2], sb['c_ident'][...]))(),
            reads=[b('vc'), b('c_ident')], writes=[b('patb')])
        S.comp('dve', (lambda h2=h_, cc2=cc: lambda h: h.tensor_scalar(
            out=sb['m_bV'][...], in0=ps['patb'][:, 0:128], scalar1=sb['betaT'][:, cc2*8+h2:cc2*8+h2+1],
            scalar2=None, op0=ALU.mult))(),
            reads=[b('patb'), b('betaT')], writes=[b('m_bV')])
        S.comp('pe', lambda h: h.transpose(ps['patb'][:, 128:256], sb['m_Wb'][...], sb['c_ident'][...]),
               reads=[b('m_Wb'), b('c_ident')], writes=[b('patb')])
        S.comp('dve', lambda h: h.tensor_copy(out=sb['m_WbT'][...], in_=ps['patb'][:, 128:256]),
               reads=[b('patb')], writes=[b('m_WbT')])
        if csub < 1:
            return
        asm_lv = csub >= 22 or csub < 20
        asm_ab = csub >= 23 or csub < 20
        asm_mb = csub >= 24 or csub < 20
        asm_ev = csub >= 25 or csub < 20
        S.comp('pe', lambda h: h.matmul(ps['pa2'][:, 0:256], sb['c_zrow'][0:1, 0:128],
                                        sb['c_zrow'][0:1, 0:256], start=True, stop=False),
               reads=[b('c_zrow')], writes=[b('pa2')])
        nlast = [None]
        for lv in ([64, 32, 16, 8] if (csub not in (13, 14) and asm_lv) else []):
            ng = 128 // (2*lv)
            gc3 = sb['G_h'][:, sl].rearrange("p (a q) -> p a q", q=2*lv)
            refb = gc3[:, :, lv-1:lv].broadcast_to((128, ng, 2*lv))
            S.comp('dve', (lambda g3=gc3, r3=refb: lambda h: h.tensor_tensor(
                out=sb['lv_ga'][...].rearrange("p (a q) -> p a q", q=g3.shape[2]),
                in0=g3, in1=r3, op=ALU.subtract))(),
                reads=[b('G_h')], writes=[b('lv_ga')])
            S.comp('dve', lambda h: h.tensor_scalar(out=sb['lv_gb'][...], in0=sb['lv_ga'][...],
                                                    scalar1=-1.0, scalar2=0.0, op0=ALU.mult, op1=ALU.min),
                   reads=[b('lv_ga')], writes=[b('lv_gb')])
            S.comp('dve', lambda h: h.tensor_scalar(out=sb['lv_ga'][...], in0=sb['lv_ga'][...],
                                                    scalar1=0.0, scalar2=None, op0=ALU.min),
                   reads=[b('lv_ga')], writes=[b('lv_ga')])
            S.comp('act', lambda h: h.activation(out=sb['lv_ae'][...], in_=sb['lv_ga'][...], func=AF.Exp),
                   reads=[b('lv_ga')], writes=[b('lv_ae')])
            S.comp('act', lambda h: h.activation(out=sb['lv_be'][...], in_=sb['lv_gb'][...], func=AF.Exp),
                   reads=[b('lv_gb')], writes=[b('lv_be')])
            S.comp('dve', (lambda s2=sl: lambda h: h.tensor_tensor(
                out=sb['lv_ak'][...], in0=sb['kn_h'][:, s2], in1=sb['lv_ae'][...], op=ALU.mult))(),
                reads=[b('kn_h'), b('lv_ae')], writes=[b('lv_ak')])
            S.comp('dve', (lambda s2=sl: lambda h: h.tensor_tensor(
                out=sb['lv_aq'][...], in0=sb['qn_h'][:, s2], in1=sb['lv_ae'][...], op=ALU.mult))(),
                reads=[b('qn_h'), b('lv_ae')], writes=[b('lv_aq')])
            S.comp('dve', (lambda s2=sl: lambda h: h.tensor_tensor(
                out=sb['lv_bk'][...], in0=sb['kn_h'][:, s2], in1=sb['lv_be'][...], op=ALU.mult))(),
                reads=[b('kn_h'), b('lv_be')], writes=[b('lv_bk')])
            for a in range(ng):
                r0 = 2*lv*a + lv
                c0 = 2*lv*a
                if r0 % 32 == 0 and r0 != 96:
                    S.comp('pe', (lambda r2=r0, c2=c0, l2=lv: lambda h: h.matmul(
                        ps['pa2'][r2:r2+l2, c2:c2+l2], sb['lv_ak'][:, r2:r2+l2],
                        sb['lv_bk'][:, c2:c2+l2], start=False, stop=False))(),
                        reads=[b('lv_ak'), b('lv_bk')], writes=[b('pa2')])
                else:
                    zone = r0 % 64
                    win = r0 - zone
                    padn = f'padM_{zone}_{lv}'
                    S.comp('dve', (lambda p2=padn, z2=zone, r2=r0, l2=lv: lambda h: h.tensor_copy(
                        out=sb[p2][:, z2:z2+l2], in_=sb['lv_ak'][:, r2:r2+l2]))(),
                        reads=[b('lv_ak')], writes=[b(padn)])
                    S.comp('pe', (lambda p2=padn, w2=win, c2=c0, l2=lv: lambda h: h.matmul(
                        ps['pa2'][w2:w2+64, c2:c2+l2], sb[p2][...],
                        sb['lv_bk'][:, c2:c2+l2], start=False, stop=False))(),
                        reads=[b(padn), b('lv_bk')], writes=[b('pa2')])
                if c0 % 32 == 0 and c0 != 96:
                    S.comp('pe', (lambda r2=r0, c2=c0, l2=lv: lambda h: h.matmul(
                        ps['pa2'][c2:c2+l2, 128+r2:128+r2+l2], sb['lv_bk'][:, c2:c2+l2],
                        sb['lv_aq'][:, r2:r2+l2], start=False, stop=False))(),
                        reads=[b('lv_bk'), b('lv_aq')], writes=[b('pa2')])
                else:
                    zone = c0 % 64
                    win = c0 - zone
                    padn = f'padA_{zone}_{lv}'
                    S.comp('dve', (lambda p2=padn, z2=zone, c2=c0, l2=lv: lambda h: h.tensor_copy(
                        out=sb[p2][:, z2:z2+l2], in_=sb['lv_bk'][:, c2:c2+l2]))(),
                        reads=[b('lv_bk')], writes=[b(padn)])
                    S.comp('pe', (lambda p2=padn, w2=win, r2=r0, l2=lv: lambda h: h.matmul(
                        ps['pa2'][w2:w2+64, 128+r2:128+r2+l2], sb[p2][...],
                        sb['lv_aq'][:, r2:r2+l2], start=False, stop=False))(),
                        reads=[b(padn), b('lv_aq')], writes=[b('pa2')])
        for dd in (range(0, 8) if (csub not in (11, 14) and asm_ab) else []):
            S.comp('dve', (lambda d2=dd, cc2=cc: lambda h: h.tensor_scalar(
                out=sb['diagt'][...], in0=sb['c_ident'][...], scalar1=sb['TRt'][:, cc2, 8+d2:9+d2],
                scalar2=None, op0=ALU.mult))(),
                reads=[b('c_ident'), b('TRt')], writes=[b('diagt')])
            shn_ap = (lambda d2=dd: (sb['c_ident'][...] if d2 == 0 else sb['c_shift'][:, d2-1, :]))()
            S.comp('pe', (lambda s2=shn_ap: lambda h: h.matmul(
                ps['pa2'][:, 128:256], s2, sb['diagt'][...], start=False, stop=False))(),
                reads=[b('c_shift' if dd > 0 else 'c_ident'), b('diagt')], writes=[b('pa2')])
        for dd in (range(1, 8) if (csub not in (11, 12, 14) and asm_mb) else []):
            S.comp('dve', (lambda d2=dd, cc2=cc: lambda h: h.tensor_scalar(
                out=sb['diagt'][...], in0=sb['c_ident'][...], scalar1=sb['TRt'][:, cc2, d2:d2+1],
                scalar2=None, op0=ALU.mult))(),
                reads=[b('c_ident'), b('TRt')], writes=[b('diagt')])
            S.comp('pe', (lambda d2=dd: lambda h: h.matmul(
                ps['pa2'][:, 0:128], sb['diagt'][...], sb['c_shift'][:, d2-1, :],
                start=False, stop=(d2 == 7)))(),
                reads=[b('diagt'), b('c_shift')], writes=[b('pa2')])
        if not asm_ev:
            return
        S.comp('dve', (lambda h2=h_, cc2=cc: lambda h: h.tensor_scalar(
            out=sb['m_N1'][...], in0=ps['pa2'][:, 0:128],
            scalar1=sb['betaTn'][:, cc2*8+h2:cc2*8+h2+1], scalar2=None, op0=ALU.mult))(),
            reads=[b('pa2'), b('betaTn')], writes=[b('m_N1')])
        S.comp('act', lambda h: h.activation(out=sb['m_At'][...], in_=ps['pa2'][:, 128:256], func=AF.Copy),
               reads=[b('pa2')], writes=[b('m_At')])
        S.comp('pe', lambda h: h.transpose(ps['patb'][:, 256:384], sb['m_N1'][...], sb['c_ident'][...]),
               reads=[b('m_N1'), b('c_ident')], writes=[b('patb')])
        S.comp('dve', lambda h: h.tensor_copy(out=sb['m_X1'][...], in_=ps['patb'][:, 256:384]),
               reads=[b('patb')], writes=[b('m_X1')])
        if csub < 2:
            return
        sqjobs = [(1, 2), (2, 4), (4, 8), (8, 16), (16, 32), (32, 64)]
        pb_i = 0
        for i, j in sqjobs:
            bank = ['pa0', 'pa1'][pb_i % 2]; pb_i += 1
            S.comp('pe', (lambda i2=i, bank2=bank: lambda h: h.matmul(
                ps[bank2][:, 0:128], sb[f'm_X{i2}'][...], sb[f'm_N{i2}'][...], start=True, stop=True))(),
                reads=[b(f'm_X{i}'), b(f'm_N{i}')], writes=[b(bank)])
            eng = 'act' if pb_i % 2 == 0 else 'dve'
            if eng == 'act':
                S.comp('act', (lambda j2=j, bank2=bank: lambda h: h.activation(
                    out=sb[f'm_N{j2}'][...], in_=ps[bank2][:, 0:128], func=AF.Copy))(),
                    reads=[b(bank)], writes=[b(f'm_N{j}')])
            else:
                S.comp('dve', (lambda j2=j, bank2=bank: lambda h: h.tensor_copy(
                    out=sb[f'm_N{j2}'][...], in_=ps[bank2][:, 0:128]))(),
                    reads=[b(bank)], writes=[b(f'm_N{j}')])
            if j < 64:
                bank = ['pa0', 'pa1'][pb_i % 2]; pb_i += 1
                S.comp('pe', (lambda i2=i, bank2=bank: lambda h: h.matmul(
                    ps[bank2][:, 0:128], sb[f'm_N{i2}'][...], sb[f'm_X{i2}'][...], start=True, stop=True))(),
                    reads=[b(f'm_N{i}'), b(f'm_X{i}')], writes=[b(bank)])
                eng = 'act' if pb_i % 2 == 0 else 'dve'
                if eng == 'act':
                    S.comp('act', (lambda j2=j, bank2=bank: lambda h: h.activation(
                        out=sb[f'm_X{j2}'][...], in_=ps[bank2][:, 0:128], func=AF.Copy))(),
                        reads=[b(bank)], writes=[b(f'm_X{j}')])
                else:
                    S.comp('dve', (lambda j2=j, bank2=bank: lambda h: h.tensor_copy(
                        out=sb[f'm_X{j2}'][...], in_=ps[bank2][:, 0:128]))(),
                        reads=[b(bank)], writes=[b(f'm_X{j}')])
        qprev = 'c_ident'
        for i in [1, 2, 4, 8, 16, 32, 64]:
            bank = ['pa0', 'pa1'][pb_i % 2]; pb_i += 1
            S.comp('pe', (lambda i2=i, q2=qprev, bank2=bank: lambda h: h.matmul(
                ps[bank2][:, 0:128], sb[f'm_N{i2}'][...], sb[q2][...], start=True, stop=False))(),
                reads=[b(f'm_N{i}'), b(qprev)], writes=[b(bank)])
            S.comp('pe', (lambda q2=qprev, bank2=bank: lambda h: h.matmul(
                ps[bank2][:, 0:128], sb['c_ident'][...], sb[q2][...], start=False, stop=True))(),
                reads=[b('c_ident'), b(qprev)], writes=[b(bank)])
            S.comp('dve' if pb_i % 2 == 0 else 'act',
                   (lambda bank2=bank, e2=('dve' if pb_i % 2 == 0 else 'act'): lambda h: (
                       h.tensor_copy(out=sb['m_Q'][...], in_=ps[bank2][:, 0:128]) if e2 == 'dve'
                       else h.activation(out=sb['m_Q'][...], in_=ps[bank2][:, 0:128], func=AF.Copy)))(),
                   reads=[b(bank)], writes=[b('m_Q')])
            qprev = 'm_Q'
        if csub < 3:
            return
        S.comp('pe', (lambda h2=h_: lambda h: h.matmul(
            ps['pa0'][:, 0:128], sb['m_W'][...], sb['Sbf'][:, h2, :], start=True, stop=True))(),
            reads=[b('m_W'), b('Sbf')], writes=[b('pa0')])
        S.comp('dve', (lambda h2=h_, cc2=cc: lambda h: h.scalar_tensor_tensor(
            out=sb['m_R'][...], in0=ps['pa0'][:, 0:128],
            scalar=sb['betaTn'][:, cc2*8+h2:cc2*8+h2+1], in1=sb['m_bV'][...],
            op0=ALU.mult, op1=ALU.add))(),
            reads=[b('pa0'), b('betaTn'), b('m_bV')], writes=[b('m_R')])
        S.comp('pe', lambda h: h.matmul(ps['pa1'][:, 0:128], sb['m_Q'][...], sb['m_R'][...],
                                        start=True, stop=True),
               reads=[b('m_Q'), b('m_R')], writes=[b('pa1')])
        S.comp('dve', lambda h: h.tensor_copy(out=sb['m_U'][...], in_=ps['pa1'][:, 0:128]),
               reads=[b('pa1')], writes=[b('m_U')])
        S.comp('pe', (lambda h2=h_: lambda h: h.matmul(
            ps['pa2'][:, 256:384], sb['m_qt'][...], sb['Sbf'][:, h2, :], start=True, stop=False))(),
            reads=[b('m_qt'), b('Sbf')], writes=[b('pa2')])
        S.comp('pe', lambda h: h.matmul(ps['pa2'][:, 256:384], sb['m_At'][...], sb['m_U'][...],
                                        start=False, stop=True),
               reads=[b('m_At'), b('m_U')], writes=[b('pa2')])
        S.comp('pe', lambda h: h.matmul(ps['pa0'][:, 0:128], sb['m_WbT'][...], sb['m_U'][...],
                                        start=True, stop=True),
               reads=[b('m_WbT'), b('m_U')], writes=[b('pa0')])
        S.comp('dve', (lambda h2=h_: lambda h: h.scalar_tensor_tensor(
            out=sb['Stmp'][...], in0=sb['Sfp0'][:, h2, :], scalar=sb['lamc'][:, 0:1],
            in1=ps['pa0'][:, 0:128], op0=ALU.mult, op1=ALU.add))(),
            reads=[b('Sfp0'), b('lamc'), b('pa0')], writes=[b('Stmp')])
        S.comp('dve', (lambda h2=h_: lambda h: h.tensor_copy(
            out=sb['Sfp0'][:, h2, :], in_=sb['Stmp'][...]))(),
            reads=[b('Stmp')], writes=[b('Sfp0')])
        S.comp('act', (lambda h2=h_: lambda h: h.activation(
            out=sb['Sbf'][:, h2, :], in_=sb['Stmp'][...], func=AF.Copy))(),
            reads=[b('Stmp')], writes=[b('Sbf')])
        if csub < 4:
            return
        S.comp('act', lambda h: h.activation(out=sb['otm'][...], in_=ps['pa2'][:, 256:384], func=AF.Copy),
               reads=[b('pa2')], writes=[b('otm')])
        S.comp('act', lambda h: h.activation(out=sb['diagt'][...], in_=sb['otm'][...], func=AF.Square),
               reads=[b('otm')], writes=[b('diagt')])
        S.comp('dve', lambda h: h.tensor_reduce(out=sb['rms_c'][...], in_=sb['diagt'][...],
                                                axis=AX.X, op=ALU.add),
               reads=[b('diagt')], writes=[b('rms_c')])
        S.comp('dve', lambda h: h.tensor_scalar(out=sb['rms_c'][...], in0=sb['rms_c'][...],
                                                scalar1=1.0/DV, scalar2=NORM_EPS,
                                                op0=ALU.mult, op1=ALU.add),
               reads=[b('rms_c')], writes=[b('rms_c')])
        S.comp('act', lambda h: h.activation(out=sb['rms_c'][...], in_=sb['rms_c'][...], func=AF.Ln),
               reads=[b('rms_c')], writes=[b('rms_c')])
        S.comp('act', lambda h: h.activation(out=sb['rms_c'][...], in_=sb['rms_c'][...],
                                             func=AF.Exp, scale=-0.5),
               reads=[b('rms_c')], writes=[b('rms_c')])
        S.comp('dve', lambda h: h.tensor_scalar(out=sb['otm2'][...], in0=sb['otm'][...],
                                                scalar1=sb['rms_c'][:, 0:1], scalar2=None, op0=ALU.mult),
               reads=[b('otm'), b('rms_c')], writes=[b('otm2')])
        S.comp('pe', lambda h: h.transpose(ps['patb'][:, 384:512], sb['otm2'][...], sb['c_ident'][...]),
               reads=[b('otm2'), b('c_ident')], writes=[b('patb')])
        S.comp('dve', (lambda h2=h_, s2=sl: lambda h: h.scalar_tensor_tensor(
            out=sb['o_ct'][:, h2, s2], in0=ps['patb'][:, 384:512], scalar=sb['c_normw'][:, 0:1],
            in1=sb['sgate'][:, h2, s2], op0=ALU.mult, op1=ALU.mult))(),
            reads=[b('patb'), b('c_normw'), b('sgate')], writes=[b('o_ct')])

    def emit_pp_cc(self, tt):
        S, b = self.S, self.buf
        dr = self.dr
        S.cc((lambda tt2=tt: lambda h: h.collective_compute(
            "AllReduce", ALU.add,
            replica_groups=[[0, 1], [2, 3], [4, 5], [6, 7]],
            ins=[dr[f'pp_in{tt2}'].ap().opt()],
            outs=[dr[f'pp_out{tt2}'].ap().opt()]))(),
            reads=[b(f'pp_in{tt}')], writes=[b(f'pp_out{tt}')])

    def emit_wo_f2(self, tt):
        S, b = self.S, self.buf
        dr, sb, ps = self.dr, self.sb, self.ps
        ppo = dr[f'pp_out{tt}'].ap().rearrange("(s c p) t -> s p c t", s=2, p=128)
        x1ap = dr['x1T'].ap().rearrange("(c p) t -> p c t", p=128)
        x2ap = dr['x2T'].ap().rearrange("(c p) t -> p c t", p=128)
        for oc in range(NCH):
            b0 = ['x2b0', 'x2b1'][0]; b1s = ['x2b0', 'x2b1'][1]
            S.dma('sp', (lambda oc2=oc: lambda h: h.dma_start(
                out=sb['ct1'][...], in_=ppo[0, :, oc2, :]))(),
                reads=[b(f'pp_out{tt}')], writes=[b('ct1')])
            S.dma('sp', (lambda oc2=oc: lambda h: h.dma_start(
                out=sb['ct2'][...], in_=ppo[1, :, oc2, :]))(),
                reads=[b(f'pp_out{tt}')], writes=[b('ct2')])
            xs = [sb['xch0'], sb['xch1']][oc % 2]
            xsb = [b('xch0'), b('xch1')][oc % 2]
            S.dma('sp', (lambda oc2=oc, xs2=xs: lambda h: h.dma_start(
                out=xs2[...], in_=x1ap[:, oc2, tt*CT:(tt+1)*CT]))(),
                reads=[b('x1T')], writes=[xsb])
            S.comp('dve', (lambda xs2=xs: lambda h: h.scalar_tensor_tensor(
                out=sb['toc'][...], in0=sb['ct1'][...], scalar=sb['msel'][:, 0:1], in1=xs2[...],
                op0=ALU.mult, op1=ALU.add))(),
                reads=[b('ct1'), b('msel'), xsb], writes=[b('toc')])
            xoc = [sb['x1oc0'], sb['x1oc1']][oc % 2]
            xb = [b('x1oc0'), b('x1oc1')][oc % 2]
            S.comp('dve', (lambda xoc2=xoc: lambda h: h.scalar_tensor_tensor(
                out=xoc2[...], in0=sb['ct2'][...], scalar=sb['msel'][:, 1:2], in1=sb['toc'][...],
                op0=ALU.mult, op1=ALU.add))(),
                reads=[b('ct2'), b('msel'), b('toc')], writes=[xb])
            S.dma('sp', (lambda oc2=oc, xoc2=xoc: lambda h: h.dma_start(
                out=x2ap[:, oc2, tt*CT:(tt+1)*CT], in_=xoc2[...]))(),
                reads=[xb], writes=[b('x2T')])
            S.comp('pe', (lambda oc2=oc, xoc2=xoc: lambda h: h.matmul(
                ps['pb0'][0:1, :], sb['c_ones_col'][:, 0:1], xoc2[...],
                start=(oc2 == 0), stop=(oc2 == NCH-1)))(),
                reads=[xb, b('c_ones_col')], writes=[b('pb0')])
            sq = sb['xsq0']
            sqb = b('xsq0')
            S.comp('act', (lambda xoc2=xoc, sq2=sq: lambda h: h.activation(
                out=sq2[...], in_=xoc2[...], func=AF.Square))(),
                reads=[xb], writes=[sqb])
            S.comp('pe', (lambda oc2=oc, sq2=sq: lambda h: h.matmul(
                ps['pb2'][0:1, :], sb['c_ones_col'][:, 0:1], sq2[...],
                start=(oc2 == 0), stop=(oc2 == NCH-1)))(),
                reads=[sqb, b('c_ones_col')], writes=[b('pb2')])
        S.comp('dve', lambda h: h.tensor_scalar(out=sb['m_row'][...], in0=ps['pb0'][0:1, :],
                                                scalar1=1.0/HD, scalar2=None, op0=ALU.mult),
               reads=[b('pb0')], writes=[b('m_row')])
        S.comp('dve', lambda h: h.tensor_scalar(out=sb['t_row'][...], in0=ps['pb2'][0:1, :],
                                                scalar1=1.0/HD, scalar2=None, op0=ALU.mult),
               reads=[b('pb2')], writes=[b('t_row')])
        self._ln_rows()
        src = dr['x2T'].ap().rearrange("(c p) t -> p c t", p=128)
        for c in range(NCH):
            xs = [sb['xch0'], sb['xch1']][c % 2]
            xsb = [b('xch0'), b('xch1')][c % 2]
            S.dma('sp', (lambda c2=c, xs2=xs: lambda h: h.dma_start(
                out=xs2[...], in_=src[:, c2, tt*CT:(tt+1)*CT]))(),
                reads=[b('x2T')], writes=[xsb])
            S.comp('dve', (lambda c2=c, xs2=xs: lambda h: h.tensor_tensor(
                out=sb['xr'][:, c2, :], in0=xs2[...], in1=ps['pb1'][...], op=ALU.mult))(),
                reads=[xsb, b('pb1')], writes=[b('xr')])
        self.emit_ffn(tt, 'f2')

    def load_wosl(self, g):
        pass

    def build_full(self, upto=99, sub=99, csub=99):
        self.sub = sub
        self.csub = csub
        self.alloc()
        self.emit_const_load()
        self.emit_wconv_pre()
        if upto >= 1:
            for tt in range(self.nt):
                self.emit_ln(tt, 'xT', 'xr', 'pb0', 'pb2')
                self.emit_ffn(tt, 'f1')
                self.emit_ln2_and_export(tt)
            self.emit_mr_cc()
        if upto >= 2:
            self.arena_barrier('a')
            for g in range(2 * self.nt):
                if upto < 3 + g:
                    break
                self.emit_attn_ct(g, sub=self.sub)
                if g % 2 == 1 and self.sub >= 5:
                    self.emit_pp_cc(g // 2)
        if upto >= 3 + 2*self.nt:
            self.arena_barrier('f')
            for tt in range(self.nt):
                self.emit_wo_f2(tt)
        self.S.finalize()
        sems = self.sems
        with self.nc.Block() as block:
            block.tensor(self.S.emitter('pe', sems))
            block.scalar(self.S.emitter('act', sems))
            block.vector(self.S.emitter('dve', sems))
            block.gpsimd(self.S.emitter('pool', sems))
            block.sync(self.S.emitter('sp', sems))
        return self.nc

    def build_f1_only(self):
        self.alloc()
        self.emit_const_load()
        self.emit_wconv_pre()
        for tt in range(self.nt):
            self.emit_ln(tt, 'xT', 'xr', 'pb0', 'pb2')
            self.emit_ffn(tt, 'f1')
            self.emit_ln2_and_export(tt)
        self.emit_mr_cc()
        self.S.finalize()
        sems = self.sems
        with self.nc.Block() as block:
            block.tensor(self.S.emitter('pe', sems))
            block.scalar(self.S.emitter('act', sems))
            block.vector(self.S.emitter('dve', sems))
            block.gpsimd(self.S.emitter('pool', sems))
            block.sync(self.S.emitter('sp', sems))
        return self.nc



def kernel(**inputs):
    from concourse.bass_utils import run_bass_kernel_spmd
    n_tiles = 4
    B = Build(n_tiles, dbg=False)
    nc = B.build_full()
    in_maps = host_prep(inputs, n_tiles)
    res = run_bass_kernel_spmd(nc, in_maps, core_ids=list(range(8)))
    out = np.zeros((4, 4096, HD), np.float32)
    for core in range(8):
        bi, hg = core // 2, core % 2
        ot = np.asarray(res.results[core]['outT'], np.float32)
        for tt in range(n_tiles):
            gt = 2 * tt + hg
            out[bi, gt*CT:(gt+1)*CT, :] = ot[:, tt*CT:(tt+1)*CT].T
    return out


# version 03
# speedup vs baseline: 1.1490x; 1.1485x over previous
import numpy as np
import ml_dtypes
from contextlib import ExitStack
import concourse.bass as bass
import concourse.mybir as mybir

F32 = mybir.dt.float32
BF16 = mybir.dt.bfloat16
AF = mybir.ActivationFunctionType
ALU = mybir.AluOpType
AX = mybir.AxisListType
npbf = ml_dtypes.bfloat16

NH, DK, DV = 16, 128, 128
HD = 2048
NCH = 16
FFH = 8192
NFH = 64
CONV = 4
C = 128
CB = 8
CT = 512
EPS_STAT = 1e-6
NORM_EPS = 1e-5
L2_EPS = 1e-6

PAR_TILES = {
    'vh_h', 'e_h', 'tmp8', 'ampb', 'knb', 'qnb', 'zt', 'sp1', 'ab', 'spt', 'gt',
    'D_h', 'G_h', 'kn_h', 'qn_h', 'pc0', 'pc1', 'ttb', 'p3', 'p4', 'pmrow',
    'TRt', 'lv_gb', 'lv_ga', 'lv_ae', 'lv_ak', 'lv_aq', 'lv_be', 'lv_bk',
    'm_W', 'm_qt', 'm_Wb', 'm_WbT', 'm_bV', 'm_At', 'm_R', 'm_U', 'm_Q',
    'm_N1', 'm_N2', 'm_N4', 'm_N8', 'm_N16', 'm_N32', 'm_N64',
    'm_X1', 'm_X2', 'm_X4', 'm_X8', 'm_X16', 'm_X32',
    'otm', 'otm2', 'diagt', 'colA', 'lamc', 'rms_c', 'Stmp',
}
PAR_BANKS = {'pa0': 'pb0', 'pa1': 'pb1', 'pa2': 'pb2', 'patb': 'patb2'}
PAR_STATE = {'Sfp0', 'Sbf', 'sgate'}

ENGINES = ("pe", "act", "dve", "pool", "sp")
DMA_NCLASS = {"sp": 6, "pool": 6, "act": 2, "dve": 2}
CC_DOM = "ccD"
DMA_DOMS = [f"{e}D{r}" for e in DMA_NCLASS for r in range(DMA_NCLASS[e])]
ALL_DOMS = list(ENGINES) + DMA_DOMS + [CC_DOM]

class View:
    def __init__(self, ap):
        self._ap = ap
    def __getitem__(self, idx):
        if idx is Ellipsis:
            return self._ap
        return self._ap[idx]

class Buf:
    __slots__ = ("name", "writer", "readers", "is_psum")
    def __init__(self, name):
        self.name = name
        self.writer = None
        self.readers = []
        self.is_psum = False

class Op:
    __slots__ = ("eng", "fn", "deps", "kind", "inc", "tag", "count", "waits", "gidx", "eidx")
    def __init__(self, eng, fn, deps, kind, tag, gidx, eidx):
        self.eng = eng; self.fn = fn; self.deps = deps; self.kind = kind
        self.inc = False; self.tag = tag; self.count = None; self.waits = None
        self.gidx = gidx; self.eidx = eidx

class Sched:
    def __init__(self, debug_serial=False):
        self.debug_serial = debug_serial
        self.ops = {e: [] for e in ENGINES}
        self.glist = []
        self.tag = ""

    def _rec(self, eng, kind, fn, reads, writes):
        dom = eng
        idx = len(self.ops[eng])
        me = (eng, idx)
        def need(dep):
            e2, i2 = dep
            if e2 != eng:
                return True
            if self.ops[e2][i2].kind != "c":
                return True
            return eng != "pe"
        deps = set()
        for b in reads:
            if b.writer is not None and need(b.writer):
                deps.add(b.writer)
            if b.is_psum:
                for r in b.readers:
                    if r[0] != eng:
                        deps.add(r)
        for b in writes:
            if b.writer is not None and need(b.writer):
                deps.add(b.writer)
            for r in b.readers:
                if need(r):
                    deps.add(r)
        op = Op(eng, fn, sorted(deps), kind, self.tag, len(self.glist), idx)
        self.ops[eng].append(op)
        self.glist.append(op)
        for b in reads:
            b.readers.append(me)
        for b in writes:
            b.writer = me
            b.readers = [me]
        return op

    def comp(self, eng, fn, reads=(), writes=(), glue=False):
        return self._rec(eng, "c", fn, reads, writes)

    def dma(self, eng, fn, reads=(), writes=(), glue=False):
        return self._rec(eng, "d", fn, reads, writes)

    def cc(self, fn, reads=(), writes=(), glue=False):
        return self._rec("pool", "cc", fn, reads, writes)

    def finalize(self):
        if self.debug_serial:
            prev = None
            for op in self.glist:
                op.deps = [(prev.eng, prev.eidx)] if (prev is not None and prev.eng != op.eng) else []
                prev = op
        for e in ENGINES:
            for op in self.ops[e]:
                for (e2, i2) in op.deps:
                    d = self.ops[e2][i2]
                    if d.kind == "c":
                        d.inc = True
        ccn = 0
        for e in ENGINES:
            c_comp = 0
            nd = 0
            for op in self.ops[e]:
                if op.kind == "c":
                    if op.inc:
                        c_comp += 1
                    op.count = ("c", c_comp, None)
                elif op.kind == "d":
                    r = nd % DMA_NCLASS[e]
                    i = nd // DMA_NCLASS[e]
                    op.count = ("d", i, r)
                    nd += 1
                else:
                    op.count = ("cc", ccn, None)
                    ccn += 1
        for e in ENGINES:
            for op in self.ops[e]:
                w = {}
                for (e2, i2) in op.deps:
                    d = self.ops[e2][i2]
                    k, v, r = d.count
                    if k == "c":
                        w[e2] = max(w.get(e2, 0), v)
                    elif k == "d":
                        dom = f"{e2}D{r}"
                        w[dom] = max(w.get(dom, 0), (v + 1) * 16)
                    else:
                        w[CC_DOM] = max(w.get(CC_DOM, 0), v + 1)
                if op.kind == "d":
                    k, i, r = op.count
                    if i > 0:
                        dom = f"{e}D{r}"
                        w[dom] = max(w.get(dom, 0), i * 16)
                elif op.kind == "cc":
                    k, i, r = op.count
                    if i > 0:
                        w[CC_DOM] = max(w.get(CC_DOM, 0), i)
                op.waits = w

    def _idx_of(self, op):
        return self.ops[op.eng].index(op)

    def stats(self):
        return {e: len(self.ops[e]) for e in ENGINES}

    def totals(self):
        tot = {}
        for e in ENGINES:
            nd = sum(1 for op in self.ops[e] if op.kind == "d")
            if e in DMA_NCLASS:
                for r in range(DMA_NCLASS[e]):
                    cnt = nd // DMA_NCLASS[e] + (1 if nd % DMA_NCLASS[e] > r else 0)
                    if cnt:
                        tot[f"{e}D{r}"] = cnt * 16
            ncc = sum(1 for op in self.ops[e] if op.kind == "cc")
            if ncc:
                tot[CC_DOM] = tot.get(CC_DOM, 0) + ncc
        return tot

    def emitter(self, e, sems):
        ops = self.ops[e]
        tot = self.totals()
        def run(h):
            wm = {d: 0 for d in ALL_DOMS}
            for op in ops:
                for d in sorted(op.waits):
                    v = op.waits[d]
                    if v > wm[d]:
                        h.wait_ge(sems[d], v)
                        wm[d] = v
                inst = op.fn(h)
                if op.kind == "c":
                    if op.inc:
                        assert inst is not None, f"op '{op.tag}' on {e} needs inc but returned None"
                        inst.then_inc(sems[e], 1)
                elif op.kind == "d":
                    k, i, r = op.count
                    inst.then_inc(sems[f"{e}D{r}"], 16)
                else:
                    inst.then_inc(sems[CC_DOM], 1)
            for d, v in sorted(tot.items()):
                if v > wm.get(d, 0):
                    h.wait_ge(sems[d], v)
        return run


def host_prep(inputs, n_tiles, dbg=False):
    T = n_tiles * 2 * CT
    x = inputs['L0_attn_hidden_states'][:, :T, :]
    f32 = np.float32
    def T_(a):
        return np.ascontiguousarray(np.asarray(a, f32).T)
    WqT = T_(inputs['L0_attn_Wq'])
    WkT = T_(inputs['L0_attn_Wk'])
    WvT = T_(inputs['L0_attn_Wv'])
    Wg1T = T_(inputs['L0_attn_Wg1'])
    Wg2T = T_(inputs['L0_attn_Wg2'])
    WoT = T_(inputs['L0_attn_Wo'])
    WsvT = T_(inputs['L0_attn_Wsv'])
    WgT = T_(inputs['L0_ffn_w_gate'])
    WuT = T_(inputs['L0_ffn_w_up'])
    WdT = T_(inputs['L0_ffn_w_down'])
    Wb1T = T_(inputs['L0_attn_Wb1'])
    Wa1T = T_(inputs['L0_attn_Wa1'])
    def z8(W2):
        z = np.zeros((32, 64), f32)
        for h in range(8):
            z[:, 8*h + h] = W2[0]
        return z
    Wb2Tz = z8(np.asarray(inputs['L0_attn_Wb2'], f32))
    Wa2Tz = z8(np.asarray(inputs['L0_attn_Wa2'], f32))
    ident = np.eye(128, dtype=npbf)
    shift = np.zeros((7, 128, 128), dtype=npbf)
    for d in range(1, 8):
        for j in range(128):
            m = j - d
            if m >= 0 and (j // CB) == (m // CB):
                shift[d-1, j, m] = 1
    zc = np.zeros((16, 128, 16), dtype=npbf)
    for r in range(16):
        zc[r, :, r] = 1
    zrow8 = np.zeros((8, 8, 128), dtype=npbf)
    for hh in range(8):
        zrow8[hh, hh, :] = 1
    ones_row = np.ones((1, 128), dtype=npbf)
    ones_col = np.ones((128, 1), f32)
    in_maps = []
    for core in range(8):
        b, hg = core // 2, core % 2
        own = np.concatenate([x[b, (2*tt+hg)*CT:(2*tt+hg+1)*CT, :] for tt in range(n_tiles)], 0)
        hsl = slice(hg*1024, (hg+1)*1024)
        m = {
            'xT': np.ascontiguousarray(own.T),
            'wgT': WgT, 'wuT': WuT, 'wdT': WdT,
            'wqT': np.ascontiguousarray(WqT[:, hsl]),
            'wkT': np.ascontiguousarray(WkT[:, hsl]),
            'wvT': np.ascontiguousarray(WvT[:, hsl]),
            'wg1T': Wg1T,
            'wg2T': np.ascontiguousarray(Wg2T[:, hsl]),
            'woT': np.ascontiguousarray(WoT[hg*1024:(hg+1)*1024, :]),
            'wsvT': np.ascontiguousarray(WsvT.astype(npbf)),
            'wb1T': np.ascontiguousarray(Wb1T.astype(npbf)),
            'wa1T': np.ascontiguousarray(Wa1T.astype(npbf)),
            'wb2Tz': np.ascontiguousarray(Wb2Tz.astype(npbf)),
            'wa2Tz': np.ascontiguousarray(Wa2Tz.astype(npbf)),
            'bb1c': np.asarray(inputs['L0_attn_bb1'], f32).reshape(32, 1),
            'ba1c': np.asarray(inputs['L0_attn_ba1'], f32).reshape(32, 1),
            'bb2c': np.full((8, 1), float(inputs['L0_attn_bb2'][0]), f32),
            'ba2c': np.full((8, 1), float(inputs['L0_attn_ba2'][0]), f32),
            'convw': np.stack([
                np.asarray(inputs['L0_attn_conv_q'], f32)[hg*1024:(hg+1)*1024].reshape(8, 128, 4).transpose(1, 0, 2),
                np.asarray(inputs['L0_attn_conv_k'], f32)[hg*1024:(hg+1)*1024].reshape(8, 128, 4).transpose(1, 0, 2),
                np.asarray(inputs['L0_attn_conv_v'], f32)[hg*1024:(hg+1)*1024].reshape(8, 128, 4).transpose(1, 0, 2),
            ], 0).copy(),
            'dtb': np.asarray(inputs['L0_attn_dt_bias'], f32)[hg*1024:(hg+1)*1024].reshape(8, 128).T.copy(),
            'alog': np.asarray(inputs['L0_attn_A_log'], f32)[hg*8:(hg+1)*8].reshape(1, 8).copy(),
            'bg2c': np.asarray(inputs['L0_attn_bg2'], f32)[hg*1024:(hg+1)*1024].reshape(8, 128).T.copy(),
            'normw': np.asarray(inputs['L0_attn_norm_w'], f32).reshape(128, 1).copy(),
            'ident': ident, 'shift': shift, 'zc': zc,
            'ones_row': ones_row, 'ones_col': ones_col,
            'identf': np.eye(16, dtype=f32),
            'zrow8': zrow8,
            'mselh': np.stack([np.full(128, 1.0 if hg == 0 else 0.0, f32),
                               np.full(128, 1.0 if hg == 1 else 0.0, f32)], 1),
        }
        in_maps.append(m)
    return in_maps


class Build:
    def __init__(self, n_tiles, dbg=False, debug_serial=False):
        self.nt = n_tiles
        self.T = n_tiles * 2 * CT
        self.dbg = dbg
        self.nc = bass.Bass("TRN2", target_bir_lowering=False, debug=False)
        self.S = Sched(debug_serial)
        self.es = ExitStack()
        self.bufs = {}
        self.vw_f = []
        self.vw_a = []
        self.sb = {}
        self.dr = {}
        self.ps = {}

    def buf(self, name):
        if name not in self.bufs:
            self.bufs[name] = Buf(name)
        return self.bufs[name]

    def dram(self, name, shape, dt, kind=None):
        t = self.nc.dram_tensor(name, shape, dt, kind=kind) if kind else self.nc.dram_tensor(name, shape, dt)
        self.dr[name] = t
        self.buf(name)
        return t

    def sbuf(self, name, shape, dt):
        t = self.es.enter_context(self.nc.sbuf_tensor(name, shape, dt))
        self.sb[name] = t
        self.buf(name)
        return t

    def psum(self, name, shape, dt=F32):
        t = self.es.enter_context(self.nc.psum_tensor(name, shape, dt))
        self.ps[name] = t
        self.buf(name).is_psum = True
        return t

    def alloc(self):
        nc, nt, T = self.nc, self.nt, self.T
        D = self.dram
        NT = nt * CT
        D('xT', [HD, NT], F32, kind="ExternalInput")
        for n, sh in [('wgT', [HD, FFH]), ('wuT', [HD, FFH]), ('wdT', [FFH, HD]),
                      ('wqT', [HD, 1024]), ('wkT', [HD, 1024]), ('wvT', [HD, 1024]),
                      ('wg1T', [HD, 128]), ('wg2T', [128, 1024]), ('woT', [1024, HD])]:
            D(n, sh, F32, kind="ExternalInput")
        for n, sh in [('wsvT', [128, 128]), ('wb1T', [3, 32]), ('wa1T', [3, 32]),
                      ('wb2Tz', [32, 64]), ('wa2Tz', [32, 64]),
                      ('ident', [128, 128]), ('shift', [7, 128, 128]), ('zc', [16, 128, 16]),
                      ('ones_row', [1, 128])]:
            D(n, sh, BF16, kind="ExternalInput")
        for n, sh in [('bb1c', [32, 1]), ('ba1c', [32, 1]), ('bb2c', [8, 1]), ('ba2c', [8, 1]),
                      ('convw', [3, 128, 8, 4]), ('dtb', [128, 8]), ('alog', [1, 8]),
                      ('bg2c', [128, 8]), ('normw', [128, 1]), ('ones_col', [128, 1]),
                      ('mselh', [128, 2]), ('identf', [16, 16])]:
            D(n, sh, F32, kind="ExternalInput")
        D('zrow8', [8, 8, 128], BF16, kind="ExternalInput")
        D('outT', [HD, NT], F32, kind="ExternalOutput")
        ik = "ExternalOutput" if self.dbg else None
        D('x1T', [HD, NT], F32, kind=ik)
        D('x2T', [HD, NT], F32, kind=ik)
        D('wg_c', [HD, FFH], BF16)
        D('wu_c', [HD, FFH], BF16)
        D('wd_c', [FFH, HD], BF16)
        D('wq_c', [HD, 1024], BF16)
        D('wk_c', [HD, 1024], BF16)
        D('wv_c', [HD, 1024], BF16)
        D('wg1_c', [HD, 128], BF16)
        D('wo_c', [1024, HD], BF16)
        for tt in range(nt):
            D(f'xr2_in{tt}', [HD, CT], BF16)
            D(f'xr2_out{tt}', [2*HD, CT], BF16)
        D('mr_in', [1, NT], F32)
        D('mr_out', [2, NT], F32)
        for tt in range(nt):
            D(f'pp_in{tt}', [2*HD, CT], F32)
            D(f'pp_out{tt}', [2*HD, CT], F32)
        if self.dbg:
            D('dbg_xr2', [2*HD, NT], BF16, kind="ExternalOutput")
            D('dbg_mr', [2, NT], F32, kind="ExternalOutput")

        B = self.sbuf
        B('c_ident', [128, 128], BF16)
        B('c_identf', [16, 16], F32)
        B('c_shift', [128, 7, 128], BF16)
        B('c_zc', [128, 16, 16], BF16)
        B('c_zrow8', [8, 8, 128], BF16)
        B('c_ones_row', [1, 128], BF16)
        B('c_ones_col', [128, 1], F32)
        B('c_ones_colb', [128, 1], BF16)
        B('c_onesrow_f', [1, 128], F32)
        B('c_wsv', [128, 128], BF16)
        B('c_wb1', [3, 32], BF16)
        B('c_wa1', [3, 32], BF16)
        B('c_wb2z', [32, 64], BF16)
        B('c_wa2z', [32, 64], BF16)
        B('c_bb1', [32, 1], F32)
        B('c_ba1', [32, 1], F32)
        B('c_bb2', [8, 1], F32)
        B('c_ba2', [8, 1], F32)
        B('c_convw', [128, 3, 8, 4], F32)
        B('c_dtb', [128, 8], F32)
        B('c_negea', [128, 8], F32)
        B('c_bg2', [128, 8], F32)
        B('c_normw', [128, 1], F32)
        B('c_wg2', [128, 8, 128], BF16)
        B('c_alog', [1, 8], F32)
        B('xch0', [128, CT], F32); B('xch1', [128, CT], F32)
        B('xsq0', [128, CT], F32)
        B('wgu0', [128, 2, 16, 128], BF16); B('wgu1', [128, 2, 16, 128], BF16)
        B('wdsl0', [128, 16, 128], BF16); B('wdsl1', [128, 16, 128], BF16)
        B('x1oc0', [128, CT], F32); B('x1oc1', [128, CT], F32)
        arena = self.sbuf('arena', [128, 17408], F32)
        av = arena[...]
        def fview(name, ap):
            self.vw_f.append(name)
            self.sb[name] = View(ap)
            self.buf(name)
        def aview(name, ap):
            self.vw_a.append(name)
            self.sb[name] = View(ap)
            self.buf(name)
        del self.sb['arena']
        fview('facc', av[:, 0:8192].rearrange("p (a b) -> p a b", a=16))
        fview('hti', av[:, 8192:12288].bitcast(BF16).rearrange("p (a b) -> p a b", a=16))
        fview('xr', av[:, 12288:16384].bitcast(BF16).rearrange("p (a b) -> p a b", a=16))
        fview('sil', av[:, 16384:16896])
        fview('toc', av[:, 16896:17408])
        aview('qc', av[:, 0:2048].bitcast(BF16).rearrange("p (a b) -> p a b", a=8))
        aview('kc', av[:, 2048:4096].bitcast(BF16).rearrange("p (a b) -> p a b", a=8))
        aview('vc', av[:, 4096:6144].bitcast(BF16).rearrange("p (a b) -> p a b", a=8))
        aview('sgate', av[:, 6144:8192].bitcast(BF16).rearrange("p (a b) -> p a b", a=8))
        aview('o_ct', av[:, 8192:10240].bitcast(BF16).rearrange("p (a b) -> p a b", a=8))
        aview('zt', av[:, 10240:10752])
        aview('sp1', av[:, 10752:11264])
        aview('ab', av[:, 11264:11776])
        aview('spt', av[:, 11776:12288])
        aview('gt', av[:, 12288:12800])
        aview('xr2t', av[:, 12800:16896].bitcast(BF16).rearrange("p (a b) -> p a b", a=16))
        aview('wosl0', av[:, 16896:17408].bitcast(BF16).rearrange("p (a b) -> p a b", a=8))
        rowsA = self.sbuf('rowsA', [128, CT], F32)
        rowsB = self.sbuf('rowsB', [128, CT], F32)
        rowsC = self.sbuf('rowsC', [128, CT], BF16)
        rowsD = self.sbuf('rowsD', [128, CT], BF16)
        del self.sb['rowsA']; del self.sb['rowsB']; del self.sb['rowsC']; del self.sb['rowsD']
        for nm, t2, p0 in [('rstd_row', rowsA, 0), ('t_row', rowsA, 32), ('mr_row', rowsA, 64),
                           ('amr_row', rowsA, 96), ('m_row', rowsB, 0), ('v_row', rowsB, 32)]:
            self.sb[nm] = View(t2[p0:p0+1, :])
            self.buf(nm)
        self.sb['mrn_bf'] = View(rowsC[...])
        self.buf('mrn_bf')
        self.sb['amrn_bf'] = View(rowsD[...])
        self.buf('amrn_bf')
        B('swg_row', [128, 22, 128], BF16)
        B('swu_row', [128, 22, 128], BF16)
        B('swq_row', [128, 3, 128], BF16)
        B('swk_row', [128, 3, 128], BF16)
        B('swv_row', [128, 3, 128], BF16)
        B('swg1_row', [128, 1, 128], BF16)
        B('up0', [128, 516], BF16)
        B('tls', [128, 3, 8, 3], BF16)
        B('glt', [128, CT], BF16)
        B('c_ones_f', [128, CT], F32)
        B('wp0', [128, 16, 128], BF16); B('wp1', [128, 16, 128], BF16)
        B('ct1', [128, CT], F32); B('ct2', [128, CT], F32)
        B('st_all', [8, 7, CT], BF16)
        st8A = self.sbuf('st8A', [128, CT], F32)
        st8B = self.sbuf('st8B', [128, CT], F32)
        st8C = self.sbuf('st8C', [128, CT], F32)
        for dn in ['st8A', 'st8B', 'st8C']:
            del self.sb[dn]
        st8E = self.sbuf('st8E', [128, CT], F32)
        del self.sb['st8E']
        for nm, t2, p0 in [('tmp8', st8A, 0), ('sl2t', st8A, 32), ('scost', st8A, 64), ('rqn8', st8A, 96),
                           ('beta8', st8B, 0), ('amp8', st8B, 32), ('ampz8', st8B, 64), ('rkn8', st8B, 96),
                           ('den2', st8C, 0), ('rsqk8', st8C, 32),
                           ('den1', st8E, 0)]:
            self.sb[nm] = View(t2[p0:p0+8, :])
            self.buf(nm)
        B('ampz8c', [8, CT], BF16); B('rkn8c', [8, CT], BF16); B('rqn8c', [8, CT], BF16)
        B('s3', [3, 8*CT], BF16)
        B('hb', [32, CT], BF16)
        B('betaT', [128, 32], F32); B('betaTn', [128, 32], F32)
        B('stsc0', [128, CT], BF16)
        B('vh_h', [128, CT], BF16); B('e_h', [128, CT], BF16)
        B('ampb', [128, CT], BF16); B('knb', [128, CT], BF16); B('qnb', [128, CT], BF16)
        B('D_h', [128, CT], BF16); B('G_h', [128, CT], F32)
        B('kn_h', [128, CT], BF16); B('qn_h', [128, CT], BF16)
        B('pc0', [128, CT], BF16); B('pc1', [128, CT], BF16)
        B('ttb', [128, CT], BF16); B('p3', [128, CT], BF16); B('p4', [128, CT], BF16)
        B('pmrow', [16, CT], BF16)
        B('TRt', [128, 4, 16], F32)
        B('c_zrow', [1, 256], BF16)
        B('m_bV', [128, 128], F32)
        for n in ['m_W', 'm_qt', 'm_Wb', 'm_WbT', 'm_At', 'm_R', 'm_U', 'm_Q',
                  'm_N1', 'm_N2', 'm_N4', 'm_N8', 'm_N16', 'm_N32', 'm_N64',
                  'm_X1', 'm_X2', 'm_X4', 'm_X8', 'm_X16', 'm_X32',
                  'lv_ae', 'lv_ak', 'lv_aq', 'lv_be', 'lv_bk',
                  'otm', 'otm2', 'diagt']:
            B(n, [128, 128], BF16)
        B('lv_ga', [128, 128], F32)
        self.sb['lv_gb'] = View(av[:, 12288:12416])
        self.bufs['lv_gb'] = self.buf('gt')
        for zn, wd in [(8, 8), (24, 8), (40, 8), (56, 8), (16, 16), (48, 16), (32, 32)]:
            B(f'padM_{zn}_{wd}', [128, 64], BF16)
        for zn, wd in [(16, 8), (48, 8), (32, 16), (32, 8)]:
            B(f'padA_{zn}_{wd}', [128, 64], BF16)
        B('colA', [128, 1], F32); B('lamc', [128, 1], F32); B('rms_c', [128, 1], F32)
        B('c_zero', [128, 1], F32)
        B('Sfp0', [128, 8, 128], F32)
        B('Stmp', [128, 128], F32)
        B('Sbf', [128, 8, 128], BF16)
        B('msel', [128, 2], F32)
        borrowed = ['wgu0', 'wgu1', 'wdsl0', 'wdsl1', 'xch0', 'xch1', 'xsq0',
                    'x1oc0', 'x1oc1', 'mrn_bf']
        regions = [
            self.sb['wgu0'][...].rearrange("p a b c -> p (a b c)").bitcast(F32),
            self.sb['wgu1'][...].rearrange("p a b c -> p (a b c)").bitcast(F32),
            self.sb['wdsl0'][...].rearrange("p a b -> p (a b)").bitcast(F32),
            self.sb['wdsl1'][...].rearrange("p a b -> p (a b)").bitcast(F32),
            self.sb['xch0'][...], self.sb['xch1'][...], self.sb['xsq0'][...],
            self.sb['x1oc0'][...], self.sb['x1oc1'][...],
            self.sb['mrn_bf'][...].bitcast(F32),
        ]
        _cur = [[ap, 0] for ap in regions]
        def carve(nf32):
            for ent in _cur:
                if ent[0].shape[-1] - ent[1] >= nf32:
                    ent[1] += nf32
                    return ent[0][:, ent[1]-nf32:ent[1]]
            raise RuntimeError(f"no room for twin of {nf32} f32 cols")
        def twin(name, nf32, dt2=BF16, shape3=None):
            apv = carve(nf32)
            if dt2 == BF16:
                apv = apv.bitcast(BF16)
            if shape3 is not None:
                apv = apv.rearrange("p (a b) -> p a b", a=shape3)
            self.vw_a.append(name)
            self.sb[name] = View(apv)
            self.buf(name)
            return apv
        for nm in ['zt1', 'sp11', 'ab1', 'spt1', 'G_h1']:
            twin(nm, 512, dt2=F32)
        gt1ap = twin('gt1', 512, dt2=F32)
        self.sb['lv_gb1'] = View(gt1ap[:, 0:128])
        self.bufs['lv_gb1'] = self.buf('gt1')
        for nm in ['kn_h1', 'qn_h1', 'D_h1', 'vh_h1', 'e_h1', 'ampb1', 'knb1',
                   'qnb1', 'pc01', 'pc11', 'ttb1', 'p31', 'p41']:
            twin(nm, 256)
        for nm in ['m_bV1', 'lv_ga1', 'Stmp1']:
            twin(nm, 128, dt2=F32)
        twin('TRt1', 64, dt2=F32, shape3=4)
        for nm in ['m_W1', 'm_qt1', 'm_Wb1', 'm_WbT1', 'm_At1', 'm_R1', 'm_U1',
                   'm_Q1', 'm_N11', 'm_N21', 'm_N41', 'm_N81', 'm_N161', 'm_N321',
                   'm_N641', 'm_X11', 'm_X21', 'm_X41', 'm_X81', 'm_X161', 'm_X321',
                   'lv_ae1', 'lv_ak1', 'lv_aq1', 'lv_be1', 'lv_bk1',
                   'otm1', 'otm21', 'diagt1']:
            twin(nm, 64)
        for nm in ['colA1', 'lamc1', 'rms_c1']:
            twin(nm, 1, dt2=F32)
        self.sb['pmrow1'] = View(st8C[64:80, 0:256].bitcast(BF16))
        self.buf('pmrow1'); self.vw_a.append('pmrow1')
        self.sb['tmp81'] = View(st8C[96:104, :])
        self.buf('tmp81'); self.vw_a.append('tmp81')
        self.vw_f.extend(borrowed)
        P = self.psum
        P('pb0', [128, CT]); P('pb1', [128, CT]); P('pb2', [128, CT]); P('pb3', [128, CT])
        P('pa0', [128, CT]); P('pa1', [128, CT]); P('pa2', [128, CT])
        P('patb', [128, 1024], BF16)
        self.ps['patb2'] = View(self.ps['pb3'][...].bitcast(BF16))
        self.bufs['patb2'] = self.buf('pb3')
        self.sems = {}
        for d in ALL_DOMS:
            self.sems[d] = self.es.enter_context(self.nc.semaphore(f"sem_{d}"))

    def _bufs_of(self, names):
        out = []
        for n in names:
            if n in PAR_STATE:
                out += [self.buf(n + 'p0'), self.buf(n + 'p1')]
            else:
                out.append(self.buf(n))
        return out

    def arena_barrier(self, to_side):
        S = self.S
        allv = self._bufs_of(self.vw_f) + self._bufs_of(self.vw_a)
        S.comp('dve', lambda h: h.memset(self.sb['ct1'][0:1, 0:1], 0.0),
               reads=[], writes=allv + [self.buf('ct1')])

    def record(self, emitfn, *args):
        class _Rec:
            def __init__(self): self.items = []
            def comp(self, eng, fn, reads=(), writes=(), glue=False):
                self.items.append(('c', eng, fn, list(reads), list(writes), glue))
            def dma(self, eng, fn, reads=(), writes=(), glue=False):
                self.items.append(('d', eng, fn, list(reads), list(writes), glue))
            def cc(self, fn, reads=(), writes=(), glue=False):
                self.items.append(('cc', 'pool', fn, list(reads), list(writes), glue))
        old = self.S
        rec = _Rec()
        self.S = rec
        try:
            emitfn(*args)
        finally:
            self.S = old
        return rec.items

    def play_interleaved(self, A, B):
        def blocks(L):
            out = []
            for it in L:
                if it[5] and out:
                    out[-1].append(it)
                else:
                    out.append([it])
            return out
        BA, BB = blocks(A), blocks(B)
        for i in range(max(len(BA), len(BB))):
            for blk in ([BA[i]] if i < len(BA) else []) + ([BB[i]] if i < len(BB) else []):
                for kind, eng, fn, rd, wr, _g in blk:
                    if kind == 'c':
                        self.S.comp(eng, fn, rd, wr)
                    elif kind == 'd':
                        self.S.dma(eng, fn, rd, wr)
                    else:
                        self.S.cc(fn, rd, wr)

    def par_ctx(self, par):
        par = par % 2
        class _M:
            def __init__(self, base, mapper):
                self._base = base; self._map = mapper
            def __getitem__(self, k):
                return self._base[self._map(k)]
        def tmap(k):
            return k + '1' if (par and k in PAR_TILES) else k
        def pmap(k):
            return PAR_BANKS[k] if (par and k in PAR_BANKS) else k
        def bmap(n):
            if n in PAR_STATE:
                return self.buf(n + ('p1' if par else 'p0'))
            if n in PAR_BANKS:
                return self.buf(pmap(n))
            return self.buf(tmap(n))
        return _M(self.sb, tmap), _M(self.ps, pmap), bmap

    def swsl(self, name, i):
        return self.sb[name][32*(i % 3):32*(i % 3)+1, i // 3, :]

    def constname(self, val):
        return f"cc_{val!r}".replace('-', 'm').replace('.', 'p').replace('+', '')

    def constcol(self, val):
        key = self.constname(val)
        if key not in self.sb:
            t = self.sbuf(key, [128, 1], F32)
            self.S.comp('dve', (lambda t2=t: lambda h: h.memset(t2[...], float(val)))(),
                        writes=[self.buf(key)])
        return self.sb[key]

    def t(self, name):
        return self.sb[name] if name in self.sb else (self.ps[name] if name in self.ps else self.dr[name])

    def emit_const_load(self):
        S, t, b = self.S, self.t, self.buf
        dr, sb = self.dr, self.sb
        pairs = [
            ('ident', 'c_ident', None), ('ones_row', 'c_ones_row', None),
            ('ones_col', 'c_ones_col', None),
            ('wsvT', 'c_wsv', None), ('wb1T', 'c_wb1', None), ('wa1T', 'c_wa1', None),
            ('wb2Tz', 'c_wb2z', None), ('wa2Tz', 'c_wa2z', None),
            ('bb1c', 'c_bb1', None), ('ba1c', 'c_ba1', None), ('bb2c', 'c_bb2', None), ('ba2c', 'c_ba2', None),
            ('dtb', 'c_dtb', None), ('bg2c', 'c_bg2', None), ('normw', 'c_normw', None),
            ('alog', 'c_alog', None), ('identf', 'c_identf', None),
        ]
        for src, dst, _ in pairs:
            S.dma('sp', (lambda s=src, d2=dst: (lambda h: h.dma_start(out=self.sb[d2][...], in_=self.dr[s].ap())))(),
                  reads=[b(src)], writes=[b(dst)])
        S.dma('sp', lambda h: h.dma_start(out=sb['c_shift'][...],
                                          in_=dr['shift'].ap().rearrange("s p m -> p s m")),
              reads=[b('shift')], writes=[b('c_shift')])
        S.dma('sp', lambda h: h.dma_start(out=sb['c_zc'][...],
                                          in_=dr['zc'].ap().rearrange("s p m -> p s m")),
              reads=[b('zc')], writes=[b('c_zc')])
        S.dma('sp', lambda h: h.dma_start(out=sb['c_zrow8'][...], in_=dr['zrow8'].ap()),
              reads=[b('zrow8')], writes=[b('c_zrow8')])
        S.dma('sp', lambda h: h.dma_start(out=sb['c_convw'][...],
                                          in_=dr['convw'].ap().rearrange("s p h k -> p s h k")),
              reads=[b('convw')], writes=[b('c_convw')])
        S.comp('dve', lambda h: h.memset(sb['c_onesrow_f'][...], 1.0), writes=[b('c_onesrow_f')])
        S.comp('dve', lambda h: h.memset(sb['c_ones_colb'][...], 1.0), writes=[b('c_ones_colb')])
        S.comp('dve', lambda h: h.memset(sb['c_ones_f'][...], 1.0), writes=[b('c_ones_f')])

        S.comp('dve', lambda h: h.memset(sb['c_zrow'][...], 0.0), writes=[b('c_zrow')])
        S.comp('dve', lambda h: h.memset(sb['c_zero'][...], 0.0), writes=[b('c_zero')])
        padnames = [f'padM_{z}_{w}' for z, w in [(8,8),(24,8),(40,8),(56,8),(16,16),(48,16),(32,32)]] + \
                   [f'padA_{z}_{w}' for z, w in [(16,8),(48,8),(32,16),(32,8)]]
        for n in padnames + ['p3', 'p4', 'ttb', 'pc0', 'pc1']:
            S.comp('dve', (lambda n2=n: lambda h: h.memset(sb[n2][...], 0.0))(), writes=[b(n)])
        for n in ['Sfp0', 'Sbf']:
            S.comp('dve', (lambda n2=n: lambda h: h.memset(sb[n2][...], 0.0))(),
                   writes=[b(n + 'p0'), b(n + 'p1')])
        S.dma('sp', lambda h: h.dma_start(out=sb['msel'][...], in_=dr['mselh'].ap()),
              reads=[b('mselh')], writes=[b('msel')])
        S.comp('act', lambda h: h.activation(out=sb['c_alog'][...], in_=sb['c_alog'][...], func=AF.Exp),
               reads=[b('c_alog')], writes=[b('c_alog')])
        S.comp('pe', lambda h: h.matmul(self.ps['pa0'][:, 0:8], sb['c_onesrow_f'][0:1, :],
                                        sb['c_alog'][...], start=True, stop=True),
               reads=[b('c_onesrow_f'), b('c_alog')], writes=[b('pa0')])
        S.comp('act', lambda h: h.activation(out=sb['c_negea'][...], in_=self.ps['pa0'][:, 0:8], func=AF.Copy),
               reads=[b('pa0')], writes=[b('c_negea')])
        S.comp('dve', lambda h: h.tensor_scalar(out=sb['c_negea'][...], in0=sb['c_negea'][...],
                                                scalar1=-1.0, scalar2=None, op0=ALU.mult),
               reads=[b('c_negea')], writes=[b('c_negea')])

    def emit_wconv_pre(self):
        S, b = self.S, self.buf
        dr, sb, ps = self.dr, self.sb, self.ps
        jobs = [
            ('wgT', 'wg_c', FFH, 128, 'swg_row'),
            ('wuT', 'wu_c', FFH, 128, 'swu_row'),
            ('wqT', 'wq_c', 1024, 128, 'swq_row'),
            ('wkT', 'wk_c', 1024, 128, 'swk_row'),
            ('wvT', 'wv_c', 1024, 128, 'swv_row'),
            ('wg1T', 'wg1_c', 128, 128, 'swg1_row'),
            ('wdT', 'wd_c', HD, 128, None),
            ('woT', 'wo_c', HD, 128, None),
        ]
        S.dma('pool', lambda h: h.dma_start(out=sb['c_wg2'][...],
                                            in_=dr['wg2T'].ap().rearrange("p (h v) -> p h v", h=8)),
              reads=[b('wg2T')], writes=[b('c_wg2')])
        stage_flat = sb['facc'][...].rearrange("p a b -> p (a b)").bitcast(BF16)
        for src, cache, ncols, cch, swr in jobs:
            srcT = dr[src]
            nrow_ch = srcT.shape[0] // 128
            n_sl = ncols // cch
            half = 8192
            for isl in range(n_sl):
                slot = 'facc'
                off = (isl % 2) * half
                dst = stage_flat[:, off:off + nrow_ch*cch].rearrange("p (r c) -> p r c", r=nrow_ch)
                src_ap = srcT.ap().rearrange("(r p) n -> p r n", p=128)[:, :, isl*cch:(isl+1)*cch]
                S.dma('pool', (lambda d2=dst, s2=src_ap: lambda h: h.dma_start(out=d2, in_=s2))(),
                      reads=[b(src)], writes=[b(slot)])
                cache_ap = dr[cache].ap().rearrange("(r p) n -> p r n", p=128)[:, :, isl*cch:(isl+1)*cch]
                S.dma('sp', (lambda d2=cache_ap, s2=dst: lambda h: h.dma_start(out=d2, in_=s2))(),
                      reads=[b(slot)], writes=[b(cache)])
                if swr is not None:
                    for r in range(nrow_ch):
                        S.comp('pe', (lambda r2=r, d2=dst, st=(r == 0), nr=nrow_ch, cc2=cch:
                                      lambda h: h.matmul(ps['pa1'][0:1, 0:cc2], sb['c_ones_colb'][:, 0:1],
                                                         d2[:, r2, :], start=st, stop=(r2 == nr-1)))(),
                               reads=[b(slot), b('c_ones_colb')], writes=[b('pa1')])
                    S.comp('act', (lambda i2=isl, swr2=swr, cc2=cch:
                                   lambda h: h.activation(out=self.swsl(swr2, i2),
                                                          in_=ps['pa1'][0:1, 0:cc2], func=AF.Copy))(),
                           reads=[b('pa1')], writes=[b(swr)])

    def _ln_rows(self):
        S, b = self.S, self.buf
        sb, ps = self.sb, self.ps
        S.comp('dve', lambda h: h.scalar_tensor_tensor(out=sb['v_row'][...], in0=sb['m_row'][...],
                                                       scalar=-1.0, in1=sb['m_row'][...],
                                                       op0=ALU.mult, op1=ALU.mult),
               reads=[b('m_row')], writes=[b('v_row')])
        S.comp('dve', lambda h: h.tensor_tensor(out=sb['v_row'][...], in0=sb['t_row'][...],
                                                in1=sb['v_row'][...], op=ALU.add),
               reads=[b('t_row'), b('v_row')], writes=[b('v_row')])
        epsc = self.constcol(NORM_EPS)
        S.comp('act', (lambda e2=epsc: lambda h: h.activation(out=sb['t_row'][...], in_=sb['v_row'][...],
                                             func=AF.Ln, bias=e2[0:1, 0:1]))(),
               reads=[b('v_row'), b(self.constname(NORM_EPS))], writes=[b('t_row')])
        S.comp('act', lambda h: h.activation(out=sb['rstd_row'][...], in_=sb['t_row'][...],
                                             func=AF.Exp, scale=-0.5),
               reads=[b('t_row')], writes=[b('rstd_row')])
        S.comp('dve', lambda h: h.tensor_tensor(out=sb['mr_row'][...], in0=sb['m_row'][...],
                                                in1=sb['rstd_row'][...], op=ALU.mult),
               reads=[b('m_row'), b('rstd_row')], writes=[b('mr_row')])
        for p0 in [0, 32, 64]:
            S.comp('dve', (lambda p2=p0: lambda h: h.tensor_scalar(
                out=sb['mrn_bf'][p2:p2+1, :], in0=sb['mr_row'][...],
                scalar1=-1.0, scalar2=None, op0=ALU.mult))(),
                reads=[b('mr_row')], writes=[b('mrn_bf')])
        S.comp('pe', lambda h: h.matmul(ps['pb1'][...], sb['c_onesrow_f'][0:1, :], sb['rstd_row'][...],
                                        start=True, stop=True),
               reads=[b('c_onesrow_f'), b('rstd_row')], writes=[b('pb1')])

    def emit_ln(self, tt, src_dram, xr_dst, bank, bank2):
        S, b = self.S, self.buf
        dr, sb, ps = self.dr, self.sb, self.ps
        xsl = [sb['xch0'], sb['xch1']]
        xsqs = [sb['xsq0'], sb['xsq0']]
        xbufs = [b('xch0'), b('xch1')]
        xsqb = [b('xsq0'), b('xsq0')]
        src = dr[src_dram].ap().rearrange("(c p) t -> p c t", p=128)
        for c in range(NCH):
            sl = xsl[c % 2]; bb2 = xbufs[c % 2]
            S.dma('sp', (lambda c2=c, sl2=sl: lambda h: h.dma_start(
                out=sl2[...], in_=src[:, c2, tt*CT:(tt+1)*CT]))(),
                reads=[b(src_dram)], writes=[bb2])
            S.comp('pe', (lambda c2=c, sl2=sl: lambda h: h.matmul(
                ps[bank][0:1, :], sb['c_ones_col'][:, 0:1], sl2[...], start=(c2 == 0), stop=(c2 == NCH-1)))(),
                reads=[bb2, b('c_ones_col')], writes=[b(bank)])
            S.comp('act', (lambda c2=c, sl2=sl, q=xsqs[c % 2]: lambda h: h.activation(
                out=q[...], in_=sl2[...], func=AF.Square))(),
                reads=[bb2], writes=[xsqb[c % 2]])
            S.comp('pe', (lambda c2=c, q=xsqs[c % 2]: lambda h: h.matmul(
                ps[bank2][0:1, :], sb['c_ones_col'][:, 0:1], q[...], start=(c2 == 0), stop=(c2 == NCH-1)))(),
                reads=[xsqb[c % 2], b('c_ones_col')], writes=[b(bank2)])
        S.comp('dve', lambda h: h.tensor_scalar(out=sb['m_row'][...], in0=ps[bank][0:1, :],
                                                scalar1=1.0/HD, scalar2=None, op0=ALU.mult),
               reads=[b(bank)], writes=[b('m_row')])
        S.comp('dve', lambda h: h.tensor_scalar(out=sb['t_row'][...], in0=ps[bank2][0:1, :],
                                                scalar1=1.0/HD, scalar2=None, op0=ALU.mult),
               reads=[b(bank2)], writes=[b('t_row')])
        S.comp('dve', lambda h: h.scalar_tensor_tensor(out=sb['v_row'][...], in0=sb['m_row'][...],
                                                       scalar=-1.0, in1=sb['m_row'][...],
                                                       op0=ALU.mult, op1=ALU.mult),
               reads=[b('m_row')], writes=[b('v_row')])
        S.comp('dve', lambda h: h.tensor_tensor(out=sb['v_row'][...], in0=sb['t_row'][...],
                                                in1=sb['v_row'][...], op=ALU.add),
               reads=[b('t_row'), b('v_row')], writes=[b('v_row')])
        epsc = self.constcol(NORM_EPS)
        S.comp('act', (lambda e2=epsc: lambda h: h.activation(out=sb['t_row'][...], in_=sb['v_row'][...],
                                             func=AF.Ln, bias=e2[0:1, 0:1]))(),
               reads=[b('v_row'), b(self.constname(NORM_EPS))], writes=[b('t_row')])
        S.comp('act', lambda h: h.activation(out=sb['rstd_row'][...], in_=sb['t_row'][...],
                                             func=AF.Exp, scale=-0.5),
               reads=[b('t_row')], writes=[b('rstd_row')])
        S.comp('dve', lambda h: h.tensor_tensor(out=sb['mr_row'][...], in0=sb['m_row'][...],
                                                in1=sb['rstd_row'][...], op=ALU.mult),
               reads=[b('m_row'), b('rstd_row')], writes=[b('mr_row')])
        for p0 in [0, 32, 64]:
            S.comp('dve', (lambda p2=p0: lambda h: h.tensor_scalar(
                out=sb['mrn_bf'][p2:p2+1, :], in0=sb['mr_row'][...],
                scalar1=-1.0, scalar2=None, op0=ALU.mult))(),
                reads=[b('mr_row')], writes=[b('mrn_bf')])
        S.comp('pe', lambda h: h.matmul(ps['pb1'][...], sb['c_onesrow_f'][0:1, :], sb['rstd_row'][...],
                                        start=True, stop=True),
               reads=[b('c_onesrow_f'), b('rstd_row')], writes=[b('pb1')])
        for c in range(NCH):
            sl = xsl[c % 2]; bb2 = xbufs[c % 2]
            S.dma('sp', (lambda c2=c, sl2=sl: lambda h: h.dma_start(
                out=sl2[...], in_=src[:, c2, tt*CT:(tt+1)*CT]))(),
                reads=[b(src_dram)], writes=[bb2])
            S.comp('dve', (lambda c2=c, sl2=sl: lambda h: h.tensor_tensor(
                out=sb[xr_dst][:, c2, :], in0=sl2[...], in1=ps['pb1'][...], op=ALU.mult))(),
                reads=[bb2, b('pb1')], writes=[b(xr_dst)])

    def emit_ffn(self, tt, phase):
        S, b = self.S, self.buf
        dr, sb, ps = self.dr, self.sb, self.ps
        wgu = [sb['wgu0'], sb['wgu1']]
        wgub = [b('wgu0'), b('wgu1')]
        wds = [sb['wdsl0'], sb['wdsl1']]
        wdsb = [b('wdsl0'), b('wdsl1')]
        wgc = dr['wg_c'].ap().rearrange("(r p) n -> p r n", p=128)
        wuc = dr['wu_c'].ap().rearrange("(r p) n -> p r n", p=128)
        wdc = dr['wd_c'].ap().rearrange("(r p) n -> p r n", p=128)
        NG = 4
        GH = NFH // NG
        dbank = ['pb0', 'pb1']
        src_x = dr['xT' if phase == 'f1' else 'x2T'].ap().rearrange("(c p) t -> p c t", p=128)
        dstn = 'x1T' if phase == 'f1' else 'outT'
        dst_ap = dr[dstn].ap().rearrange("(c p) t -> p c t", p=128)
        stat_bank = 'pb2'
        for G in range(NG):
            for hcl in range(GH):
                hc = G*GH + hcl
                slot = wgu[hc % 2]; slb = wgub[hc % 2]
                S.dma('sp', (lambda hc2=hc, s2=slot: lambda h: h.dma_start(
                    out=s2[:, 0, :, :], in_=wgc[:, :, hc2*128:(hc2+1)*128]))(),
                    reads=[b('wg_c')], writes=[slb])
                S.dma('sp', (lambda hc2=hc, s2=slot: lambda h: h.dma_start(
                    out=s2[:, 1, :, :], in_=wuc[:, :, hc2*128:(hc2+1)*128]))(),
                    reads=[b('wg_c')], writes=[slb])
                for gi, bank in [(0, 'pb2'), (1, 'pb3')]:
                    for c2 in range(NCH):
                        S.comp('pe', (lambda s2=slot, gi2=gi, c3=c2, bank2=bank:
                                      lambda h: h.matmul(ps[bank2][...], s2[:, gi2, c3, :],
                                                         sb['xr'][:, c3, :], start=(c3 == 0), stop=False))(),
                               reads=[slb, b('xr')], writes=[b(bank)])
                    swr = 'swg_row' if gi == 0 else 'swu_row'
                    S.comp('pe', (lambda bank2=bank, swr2=swr, hc2=hc:
                                  lambda h: h.matmul(ps[bank2][...], self.swsl(swr2, hc2),
                                                     sb['mrn_bf'][32*(hc2 % 3):32*(hc2 % 3)+1, :],
                                                     start=False, stop=True))(),
                           reads=[b(swr), b('mrn_bf')], writes=[b(bank)])
                S.comp('act', (lambda: lambda h: h.activation(out=sb['sil'][...], in_=ps['pb2'][...],
                                                              func=AF.Silu))(),
                       reads=[b('pb2')], writes=[b('sil')])
                S.comp('dve', (lambda hcl2=hcl: lambda h: h.tensor_tensor(
                    out=sb['hti'][:, hcl2, :], in0=ps['pb3'][...], in1=sb['sil'][...], op=ALU.mult))(),
                    reads=[b('pb3'), b('sil')], writes=[b('hti')])
            for oc in range(NCH):
                dbk = dbank[oc % 2]
                slot = wds[oc % 2]; slb = wdsb[oc % 2]
                S.dma('sp', (lambda G2=G, oc2=oc, s2=slot: lambda h: h.dma_start(
                    out=s2[...], in_=wdc[:, G2*GH:(G2+1)*GH, oc2*128:(oc2+1)*128]))(),
                    reads=[b('wd_c')], writes=[slb])
                for hcl in range(GH):
                    S.comp('pe', (lambda s2=slot, hcl2=hcl, dbk2=dbk:
                                  lambda h: h.matmul(ps[dbk2][...], s2[:, hcl2, :],
                                                     sb['hti'][:, hcl2, :], start=(hcl2 == 0),
                                                     stop=(hcl2 == GH-1)))(),
                           reads=[slb, b('hti')], writes=[b(dbk)])
                if G == 0:
                    S.comp('dve', (lambda oc2=oc, dbk2=dbk: lambda h: h.tensor_copy(
                        out=sb['facc'][:, oc2, :], in_=ps[dbk2][...]))(),
                        reads=[b(dbk)], writes=[b('facc')])
                elif G < NG-1:
                    S.comp('dve', (lambda oc2=oc, dbk2=dbk: lambda h: h.tensor_tensor(
                        out=sb['facc'][:, oc2, :], in0=ps[dbk2][...], in1=sb['facc'][:, oc2, :],
                        op=ALU.add))(),
                        reads=[b(dbk)], writes=[b('facc')])
                else:
                    xb = [b('x1oc0'), b('x1oc1')][oc % 2]
                    xoc = [sb['x1oc0'], sb['x1oc1']][oc % 2]
                    S.comp('dve', (lambda oc2=oc, dbk2=dbk: lambda h: h.tensor_tensor(
                        out=sb['toc'][...], in0=ps[dbk2][...], in1=sb['facc'][:, oc2, :], op=ALU.add))(),
                        reads=[b(dbk), b('facc')], writes=[b('toc')])
                    xs = [sb['xch0'], sb['xch1']][oc % 2]
                    xsb = [b('xch0'), b('xch1')][oc % 2]
                    S.dma('sp', (lambda oc2=oc, xs2=xs: lambda h: h.dma_start(
                        out=xs2[...], in_=src_x[:, oc2, tt*CT:(tt+1)*CT]))(),
                        reads=[b('xT' if phase == 'f1' else 'x2T')], writes=[xsb])
                    S.comp('dve', (lambda xs2=xs, xoc2=xoc: lambda h: h.scalar_tensor_tensor(
                        out=xoc2[...], in0=sb['toc'][...], scalar=0.5, in1=xs2[...],
                        op0=ALU.mult, op1=ALU.add))(),
                        reads=[b('toc'), xsb], writes=[xb])
                    S.dma('sp', (lambda oc2=oc, xoc2=xoc: lambda h: h.dma_start(
                        out=dst_ap[:, oc2, tt*CT:(tt+1)*CT], in_=xoc2[...]))(),
                        reads=[xb], writes=[b(dstn)])
                    if phase == 'f1':
                        S.comp('pe', (lambda oc2=oc, xoc2=xoc: lambda h: h.matmul(
                            ps['pa0'][0:1, :], sb['c_ones_col'][:, 0:1], xoc2[...],
                            start=(oc2 == 0), stop=(oc2 == NCH-1)))(),
                            reads=[xb, b('c_ones_col')], writes=[b('pa0')])
                        sq = sb['xsq0']
                        sqb = b('xsq0')
                        S.comp('act', (lambda xoc2=xoc, sq2=sq: lambda h: h.activation(
                            out=sq2[...], in_=xoc2[...], func=AF.Square))(),
                            reads=[xb], writes=[sqb])
                        S.comp('pe', (lambda oc2=oc, sq2=sq: lambda h: h.matmul(
                            ps['pa1'][0:1, :], sb['c_ones_col'][:, 0:1], sq2[...],
                            start=(oc2 == 0), stop=(oc2 == NCH-1)))(),
                            reads=[sqb, b('c_ones_col')], writes=[b('pa1')])

    def emit_ln2_and_export(self, tt):
        S, b = self.S, self.buf
        dr, sb, ps = self.dr, self.sb, self.ps
        S.comp('dve', lambda h: h.tensor_scalar(out=sb['m_row'][...], in0=ps['pa0'][0:1, :],
                                                scalar1=1.0/HD, scalar2=None, op0=ALU.mult),
               reads=[b('pa0')], writes=[b('m_row')])
        S.comp('dve', lambda h: h.tensor_scalar(out=sb['t_row'][...], in0=ps['pa1'][0:1, :],
                                                scalar1=1.0/HD, scalar2=None, op0=ALU.mult),
               reads=[b('pa1')], writes=[b('t_row')])
        self._ln_rows()
        S.dma('sp', (lambda tt2=tt: lambda h: h.dma_start(out=dr['mr_in'].ap()[0:1, tt2*CT:(tt2+1)*CT],
                                                          in_=sb['mr_row'][...]))(),
              reads=[b('mr_row')], writes=[b('mr_in')])
        src = dr['x1T'].ap().rearrange("(c p) t -> p c t", p=128)
        xr2ap = dr[f'xr2_in{tt}'].ap().rearrange("(c p) t -> p c t", p=128)
        for c in range(NCH):
            xs = [sb['xch0'], sb['xch1']][c % 2]
            xsb = [b('xch0'), b('xch1')][c % 2]
            S.dma('sp', (lambda c2=c, xs2=xs: lambda h: h.dma_start(
                out=xs2[...], in_=src[:, c2, tt*CT:(tt+1)*CT]))(),
                reads=[b('x1T')], writes=[xsb])
            S.comp('dve', (lambda c2=c, xs2=xs: lambda h: h.tensor_tensor(
                out=sb['xr'][:, c2, :], in0=xs2[...], in1=ps['pb1'][...], op=ALU.mult))(),
                reads=[xsb, b('pb1')], writes=[b('xr')])
            S.dma('sp', (lambda c2=c: lambda h: h.dma_start(
                out=xr2ap[:, c2, :], in_=sb['xr'][:, c2, :]))(),
                reads=[b('xr')], writes=[b(f'xr2_in{tt}')])
        S.cc((lambda tt2=tt: lambda h: h.collective_compute(
            "AllGather", ALU.bypass,
            replica_groups=[[0, 1], [2, 3], [4, 5], [6, 7]],
            ins=[dr[f'xr2_in{tt2}'].ap().opt()],
            outs=[dr[f'xr2_out{tt2}'].ap().opt()]))(),
            reads=[b(f'xr2_in{tt}')], writes=[b(f'xr2_out{tt}')])

    def emit_mr_cc(self):
        S, b = self.S, self.buf
        dr = self.dr
        S.cc(lambda h: h.collective_compute(
            "AllGather", ALU.bypass,
            replica_groups=[[0, 1], [2, 3], [4, 5], [6, 7]],
            ins=[dr['mr_in'].ap().opt()],
            outs=[dr['mr_out'].ap().opt()]),
            reads=[b('mr_in')], writes=[b('mr_out')])
        if self.dbg:
            for tt in range(self.nt):
                S.dma('pool', (lambda tt2=tt: lambda h: h.dma_start(
                    out=dr['dbg_xr2'].ap()[:, tt2*CT:(tt2+1)*CT], in_=dr[f'xr2_out{tt2}'].ap()))(),
                    reads=[b(f'xr2_out{tt}')], writes=[b('dbg_xr2')])
            S.dma('pool', lambda h: h.dma_start(out=dr['dbg_mr'].ap(), in_=dr['mr_out'].ap()),
                  reads=[b('mr_out')], writes=[b('dbg_mr')])


    def emit_attn_ct(self, g, sub=99):
        S, b = self.S, self.buf
        dr, sb, ps = self.dr, self.sb, self.ps
        first = (g == 0)
        half, tt = g % 2, g // 2
        src = dr[f'xr2_out{tt}'].ap()[half*HD:(half+1)*HD, :].rearrange("(c p) t -> p c t", p=128)
        S.dma('pool', (lambda s2=src: lambda h: h.dma_start(out=sb['xr2t'][...], in_=s2))(),
              reads=[b(f'xr2_out{tt}')], writes=[b('xr2t')])
        S.dma('sp', (lambda half2=half, tt2=tt: lambda h: h.dma_start(
            out=sb['amr_row'][...], in_=dr['mr_out'].ap()[half2:half2+1, tt2*CT:(tt2+1)*CT]))(),
            reads=[b('mr_out')], writes=[b('amr_row')])
        for p0 in [0, 32, 64]:
            S.comp('dve', (lambda p2=p0: lambda h: h.tensor_scalar(
                out=sb['amrn_bf'][p2:p2+1, :], in0=sb['amr_row'][...],
                scalar1=-1.0, scalar2=None, op0=ALU.mult))(),
                reads=[b('amr_row')], writes=[b('amrn_bf')])
        jobs = [('wq_c', 'wqT', 'swq_row', 0, 'qc'), ('wk_c', 'wkT', 'swk_row', 1, 'kc'),
                ('wv_c', 'wvT', 'swv_row', 2, 'vc')]
        wslots = [('wp0', sb['wp0']), ('wp1', sb['wp1'])]
        ups = [('up0', sb['up0']), ('up0', sb['up0'])]
        nio = 0
        for ji, (wc, wsrc, swr, ti, cn) in enumerate(jobs):
            for oc in range(8):
                sn, slot = wslots[nio % 2]
                un, ut = ups[nio % 2]
                nio += 1
                cap = dr[wc].ap().rearrange("(r p) n -> p r n", p=128)[:, :, oc*128:(oc+1)*128]
                if first:
                    sap = dr[wsrc].ap().rearrange("(r p) n -> p r n", p=128)[:, :, oc*128:(oc+1)*128]
                    S.dma('pool', (lambda d2=slot, s2=sap: lambda h: h.dma_start(out=d2[...], in_=s2))(),
                          reads=[b(wsrc)], writes=[b(sn)])
                    S.dma('sp', (lambda d2=cap, s2=slot: lambda h: h.dma_start(out=d2, in_=s2[...]))(),
                          reads=[b(sn)], writes=[b(wc)])
                else:
                    S.dma('sp', (lambda d2=slot, s2=cap: lambda h: h.dma_start(out=d2[...], in_=s2))(),
                          reads=[b(wc)], writes=[b(sn)])
                bank = ['pa0', 'pa1'][oc % 2]
                for c2 in range(NCH):
                    S.comp('pe', (lambda s2=slot, c3=c2, bank2=bank: lambda h: h.matmul(
                        ps[bank2][...], s2[:, c3, :], sb['xr2t'][:, c3, :],
                        start=(c3 == 0), stop=False))(),
                        reads=[b(sn), b('xr2t')], writes=[b(bank)])
                S.comp('pe', (lambda bank2=bank, swr2=swr, oc2=oc: lambda h: h.matmul(
                    ps[bank2][...], self.swsl(swr2, oc2),
                    sb['amrn_bf'][32*(oc2 % 3):32*(oc2 % 3)+1, :],
                    start=False, stop=True))(),
                    reads=[b(swr), b('amrn_bf')], writes=[b(bank)])
                if first:
                    S.comp('dve', (lambda u2=ut: lambda h: h.memset(u2[:, 0:3], 0.0))(),
                           writes=[b(un)])
                else:
                    S.comp('dve', (lambda u2=ut, t2=ti, o2=oc: lambda h: h.tensor_copy(
                        out=u2[:, 0:3], in_=sb['tls'][:, t2, o2, :]))(),
                        reads=[b('tls')], writes=[b(un)])
                S.comp('act', (lambda bank2=bank, u2=ut: lambda h: h.activation(
                    out=u2[:, 3:515], in_=ps[bank2][...], func=AF.Copy))(),
                    reads=[b(bank)], writes=[b(un)])
                S.comp('dve', (lambda u2=ut, t2=ti, o2=oc: lambda h: h.tensor_copy(
                    out=sb['tls'][:, t2, o2, :], in_=u2[:, 512:515]))(),
                    reads=[b(un)], writes=[b('tls')])
                wv = lambda k2, t2=ti, h2=oc: sb['c_convw'][:, t2, h2, k2:k2+1]
                S.comp('dve', (lambda u2=ut, w0=wv(0): lambda h: h.tensor_scalar(
                    out=sb['ct1'][...], in0=u2[:, 0:512], scalar1=w0, scalar2=None, op0=ALU.mult))(),
                    reads=[b(un), b('c_convw')], writes=[b('ct1')])
                S.comp('dve', (lambda u2=ut, w1=wv(1): lambda h: h.scalar_tensor_tensor(
                    out=sb['ct2'][...], in0=u2[:, 1:513], scalar=w1, in1=sb['ct1'][...],
                    op0=ALU.mult, op1=ALU.add))(),
                    reads=[b(un), b('ct1'), b('c_convw')], writes=[b('ct2')])
                S.comp('dve', (lambda u2=ut, w2=wv(2): lambda h: h.scalar_tensor_tensor(
                    out=sb['ct1'][...], in0=u2[:, 2:514], scalar=w2, in1=sb['ct2'][...],
                    op0=ALU.mult, op1=ALU.add))(),
                    reads=[b(un), b('ct2'), b('c_convw')], writes=[b('ct1')])
                S.comp('dve', (lambda u2=ut, w3=wv(3): lambda h: h.scalar_tensor_tensor(
                    out=sb['ct2'][...], in0=u2[:, 3:515], scalar=w3, in1=sb['ct1'][...],
                    op0=ALU.mult, op1=ALU.add))(),
                    reads=[b(un), b('ct1'), b('c_convw')], writes=[b('ct2')])
                S.comp('act', (lambda cn2=cn, o2=oc: lambda h: h.activation(
                    out=sb[cn2][:, o2, :], in_=sb['ct2'][...], func=AF.Silu))(),
                    reads=[b('ct2')], writes=[b(cn)])
        sn, slot = wslots[nio % 2]; nio += 1
        capg = dr['wg1_c'].ap().rearrange("(r p) n -> p r n", p=128)
        if first:
            sapg = dr['wg1T'].ap().rearrange("(r p) n -> p r n", p=128)
            S.dma('pool', (lambda d2=slot, s2=sapg: lambda h: h.dma_start(out=d2[...], in_=s2))(),
                  reads=[b('wg1T')], writes=[b(sn)])
            S.dma('sp', (lambda d2=capg, s2=slot: lambda h: h.dma_start(out=d2, in_=s2[...]))(),
                  reads=[b(sn)], writes=[b('wg1_c')])
        else:
            S.dma('sp', (lambda d2=slot, s2=capg: lambda h: h.dma_start(out=d2[...], in_=s2))(),
                  reads=[b('wg1_c')], writes=[b(sn)])
        for c2 in range(NCH):
            S.comp('pe', (lambda s2=slot, c3=c2: lambda h: h.matmul(
                ps['pa0'][...], s2[:, c3, :], sb['xr2t'][:, c3, :], start=(c3 == 0), stop=False))(),
                reads=[b(sn), b('xr2t')], writes=[b('pa0')])
        S.comp('pe', lambda h: h.matmul(ps['pa0'][...], self.swsl('swg1_row', 0),
                                        sb['amrn_bf'][0:1, :], start=False, stop=True),
               reads=[b('swg1_row'), b('amrn_bf')], writes=[b('pa0')])
        S.comp('act', lambda h: h.activation(out=sb['glt'][...], in_=ps['pa0'][...], func=AF.Copy),
               reads=[b('pa0')], writes=[b('glt')])
        if sub < 1:
            return
        for hp in range(4):
            A = self.record(self.emit_head_stats, 2*hp)
            Bi = self.record(self.emit_head_stats, 2*hp + 1)
            self.play_interleaved(A, Bi)
        if sub < 2:
            return
        ec = self.constcol(EPS_STAT)
        l2c = self.constcol(L2_EPS)
        def act8(out, in_ap, func, scale=None, bias=None, rd=(), wr=(), bias_buf=None):
            kw = {}
            if scale is not None: kw['scale'] = scale
            if bias is not None: kw['bias'] = bias
            rds = list(rd) + ([bias_buf] if bias_buf else [])
            S.comp('act', (lambda o2=out, i2=in_ap, f2=func, kw2=dict(kw):
                           lambda h: h.activation(out=o2, in_=i2, func=f2, **kw2))(),
                   reads=[b(x) for x in rds], writes=[b(x) for x in wr])
        act8(sb['sl2t'][...], sb['st_all'][:, 0, :], AF.Ln, bias=ec[0:8, 0:1], rd=['st_all'], wr=['sl2t'], bias_buf=self.constname(EPS_STAT))
        act8(sb['sl2t'][...], sb['sl2t'][...], AF.Exp, scale=0.5, rd=['sl2t'], wr=['sl2t'])
        act8(sb['den1'][...], sb['st_all'][:, 3, :], AF.Ln, bias=ec[0:8, 0:1], rd=['st_all'], wr=['den1'], bias_buf=self.constname(EPS_STAT))
        act8(sb['den1'][...], sb['den1'][...], AF.Exp, scale=0.5, rd=['den1'], wr=['den1'])
        act8(sb['den2'][...], sb['st_all'][:, 4, :], AF.Ln, bias=ec[0:8, 0:1], rd=['st_all'], wr=['den2'], bias_buf=self.constname(EPS_STAT))
        act8(sb['den2'][...], sb['den2'][...], AF.Exp, scale=0.5, rd=['den2'], wr=['den2'])
        S.comp('dve', lambda h: h.tensor_tensor(out=sb['den1'][...], in0=sb['den1'][...],
                                                in1=sb['den2'][...], op=ALU.mult),
               reads=[b('den1'), b('den2')], writes=[b('den1')])
        S.comp('dve', lambda h: h.tensor_scalar(out=sb['den1'][...], in0=sb['den1'][...],
                                                scalar1=EPS_STAT, scalar2=None, op0=ALU.add),
               reads=[b('den1')], writes=[b('den1')])
        S.comp('dve', lambda h: h.reciprocal(out=sb['den2'][...], in_=sb['den1'][...]),
               reads=[b('den1')], writes=[b('den2')])
        S.comp('dve', lambda h: h.tensor_tensor(out=sb['scost'][...], in0=sb['st_all'][:, 2, :],
                                                in1=sb['den2'][...], op=ALU.mult),
               reads=[b('st_all'), b('den2')], writes=[b('scost')])
        S.comp('dve', lambda h: h.tensor_scalar(out=sb['scost'][...], in0=sb['scost'][...],
                                                scalar1=-1.0, scalar2=1.0, op0=ALU.mult, op1=ALU.add),
               reads=[b('scost')], writes=[b('scost')])
        S.dma('pool', lambda h: h.dma_start(out=sb['s3'][0:1, :], in_=sb['sl2t'][...]),
              reads=[b('sl2t')], writes=[b('s3')])
        S.dma('pool', lambda h: h.dma_start(out=sb['s3'][1:2, :], in_=sb['st_all'][:, 1, :]),
              reads=[b('st_all')], writes=[b('s3')])
        S.dma('pool', lambda h: h.dma_start(out=sb['s3'][2:3, :], in_=sb['scost'][...]),
              reads=[b('scost')], writes=[b('s3')])
        for h_ in range(8):
            for w1n, w2n, bnk, b1 in [('c_wb1', 'c_wb2z', 'pa1', 'c_bb1'),
                                      ('c_wa1', 'c_wa2z', 'pa2', 'c_ba1')]:
                S.comp('pe', (lambda h2=h_, w1=w1n: lambda h: h.matmul(
                    ps['pa0'][0:32, :], sb[w1][...], sb['s3'][:, h2*CT:(h2+1)*CT],
                    start=True, stop=True))(),
                    reads=[b(w1n), b('s3')], writes=[b('pa0')])
                S.comp('act', (lambda b1n=b1: lambda h: h.activation(
                    out=sb['hb'][...], in_=ps['pa0'][0:32, :], func=AF.Silu, bias=sb[b1n][...]))(),
                    reads=[b('pa0'), b(b1)], writes=[b('hb')])
                S.comp('pe', (lambda h2=h_, w2=w2n, bnk2=bnk: lambda h: h.matmul(
                    ps[bnk2][0:8, :], sb[w2][:, h2*8:(h2+1)*8], sb['hb'][...],
                    start=(h2 == 0), stop=(h2 == 7)))(),
                    reads=[b(w2n), b('hb')], writes=[b(bnk)])
        S.comp('act', lambda h: h.activation(out=sb['beta8'][...], in_=ps['pa1'][0:8, :],
                                             func=AF.Sigmoid, bias=sb['c_bb2'][...]),
               reads=[b('pa1'), b('c_bb2')], writes=[b('beta8')])
        S.comp('act', lambda h: h.activation(out=sb['amp8'][...], in_=ps['pa2'][0:8, :],
                                             func=AF.Identity, bias=sb['c_ba2'][...]),
               reads=[b('pa2'), b('c_ba2')], writes=[b('amp8')])
        act8(sb['rsqk8'][...], sb['st_all'][:, 5, :], AF.Ln, rd=['st_all'], wr=['rsqk8'])
        act8(sb['rsqk8'][...], sb['rsqk8'][...], AF.Exp, scale=0.5, rd=['rsqk8'], wr=['rsqk8'])
        S.comp('dve', lambda h: h.tensor_scalar(out=sb['rsqk8'][...], in0=sb['rsqk8'][...],
                                                scalar1=1e-12, scalar2=None, op0=ALU.max),
               reads=[b('rsqk8')], writes=[b('rsqk8')])
        S.comp('dve', lambda h: h.reciprocal(out=sb['rsqk8'][...], in_=sb['rsqk8'][...]),
               reads=[b('rsqk8')], writes=[b('rsqk8')])
        S.comp('dve', lambda h: h.tensor_tensor(out=sb['ampz8'][...], in0=sb['amp8'][...],
                                                in1=sb['rsqk8'][...], op=ALU.mult),
               reads=[b('amp8'), b('rsqk8')], writes=[b('ampz8')])
        act8(sb['rkn8'][...], sb['st_all'][:, 5, :], AF.Ln, bias=l2c[0:8, 0:1], rd=['st_all'], wr=['rkn8'], bias_buf=self.constname(L2_EPS))
        act8(sb['rkn8'][...], sb['rkn8'][...], AF.Exp, scale=-0.5, rd=['rkn8'], wr=['rkn8'])
        act8(sb['rqn8'][...], sb['st_all'][:, 6, :], AF.Ln, bias=l2c[0:8, 0:1], rd=['st_all'], wr=['rqn8'], bias_buf=self.constname(L2_EPS))
        qkc2 = self.constcol(float(-0.5*np.log(DK)))
        act8(sb['rqn8'][...], sb['rqn8'][...], AF.Exp, scale=-0.5, bias=qkc2[0:8, 0:1], rd=['rqn8'], wr=['rqn8'], bias_buf=self.constname(float(-0.5*np.log(DK))))
        for rsrc, rdst in [('ampz8', 'ampz8c'), ('rkn8', 'rkn8c'), ('rqn8', 'rqn8c')]:
            S.comp('dve', (lambda s2=rsrc, d2=rdst: lambda h: h.tensor_copy(
                out=sb[d2][...], in_=sb[s2][...]))(),
                reads=[b(rsrc)], writes=[b(rdst)])
        for cc in range(4):
            S.comp('pe', (lambda cc2=cc: lambda h: h.transpose(
                ps['pa2'][0:128, 0:8], sb['beta8'][:, cc2*128:(cc2+1)*128], sb['c_identf'][0:8, 0:8]))(),
                reads=[b('beta8'), b('c_identf')], writes=[b('pa2')])
            S.comp('dve', (lambda cc2=cc: lambda h: h.tensor_copy(
                out=sb['betaT'][:, cc2*8:(cc2+1)*8], in_=ps['pa2'][0:128, 0:8]))(),
                reads=[b('pa2')], writes=[b('betaT')])
        S.comp('dve', lambda h: h.tensor_scalar(out=sb['betaTn'][...], in0=sb['betaT'][...],
                                                scalar1=-1.0, scalar2=None, op0=ALU.mult),
               reads=[b('betaT')], writes=[b('betaTn')])
        if sub < 3:
            return
        for hp in range(4):
            A = self.record(self.emit_head_decay, g, 2*hp, sub)
            Bi = self.record(self.emit_head_decay, g, 2*hp + 1, sub)
            self.play_interleaved(A, Bi)
        if sub < 5:
            return
        woap = dr['wo_c'].ap().rearrange("(r p) n -> p r n", p=128)
        ppap = dr[f'pp_in{tt}'].ap().rearrange("(s c p) t -> s p c t", s=2, p=128)
        wopair = [('wosl0', sb['wosl0']), ('wosl0', sb['wosl0'])]
        for oc in range(NCH):
            bank = ['pa0', 'pa1'][oc % 2]
            wn, wt = wopair[oc % 2]
            if first:
                wsap = dr['woT'].ap().rearrange("(r p) n -> p r n", p=128)[:, :, oc*128:(oc+1)*128]
                S.dma('pool', (lambda d2=wt, s2=wsap: lambda h: h.dma_start(out=d2[...], in_=s2))(),
                      reads=[b('woT')], writes=[b(wn)])
                cap2 = woap[:, :, oc*128:(oc+1)*128]
                S.dma('sp', (lambda d2=cap2, s2=wt: lambda h: h.dma_start(out=d2, in_=s2[...]))(),
                      reads=[b(wn)], writes=[b('wo_c')])
            else:
                cap2 = woap[:, :, oc*128:(oc+1)*128]
                S.dma('sp', (lambda d2=wt, s2=cap2: lambda h: h.dma_start(out=d2[...], in_=s2))(),
                      reads=[b('wo_c')], writes=[b(wn)])
            for r in range(8):
                S.comp('pe', (lambda r2=r, bank2=bank, wt2=wt: lambda h: h.matmul(
                    ps[bank2][...], wt2[:, r2, :], sb['o_ct'][:, r2, :],
                    start=(r2 == 0), stop=(r2 == 7)))(),
                    reads=[b(wn), b('o_ct')], writes=[b(bank)])
            pn = ['zt', 'sp1'][oc % 2]
            if oc % 2 == 0:
                S.comp('act', (lambda bank2=bank, pn2=pn: lambda h: h.activation(
                    out=sb[pn2][...], in_=ps[bank2][...], func=AF.Copy))(),
                    reads=[b(bank)], writes=[b(pn)])
            else:
                S.comp('dve', (lambda bank2=bank, pn2=pn: lambda h: h.tensor_copy(
                    out=sb[pn2][...], in_=ps[bank2][...]))(),
                    reads=[b(bank)], writes=[b(pn)])
            S.dma('sp', (lambda half2=half, oc2=oc, pn2=pn: lambda h: h.dma_start(
                out=ppap[half2, :, oc2, :], in_=sb[pn2][...]))(),
                reads=[b(pn)], writes=[b(f'pp_in{tt}')])

    def emit_head_stats(self, h_):
        S = self.S
        sb, ps, b = self.par_ctx(h_)
        S.comp('pe', (lambda h2=h_: lambda h: h.matmul(
            ps['pa0'][...], sb['c_wsv'][...], sb['kc'][:, h2, :], start=True, stop=True))(),
            reads=[b('c_wsv'), b('kc')], writes=[b('pa0')])
        S.comp('act', lambda h: h.activation(out=sb['vh_h'][...], in_=ps['pa0'][...], func=AF.Copy),
               reads=[b('pa0')], writes=[b('vh_h')])
        S.comp('dve', (lambda h2=h_: lambda h: h.tensor_tensor(
            out=sb['e_h'][...], in0=sb['vh_h'][...], in1=sb['vc'][:, h2, :], op=ALU.subtract))(),
            reads=[b('vh_h'), b('vc')], writes=[b('e_h')])
        stats = [
            ('act', lambda h: h.activation(out=sb['stsc0'][...], in_=sb['e_h'][...], func=AF.Square), ['e_h'], 'stsc0'),
            ('act', lambda h: h.activation(out=sb['stsc0'][...], in_=sb['e_h'][...], func=AF.Abs), ['e_h'], 'stsc0'),
            ('dve', (lambda h2=h_: lambda h: h.tensor_tensor(out=sb['stsc0'][...], in0=sb['vh_h'][...], in1=sb['vc'][:, h2, :], op=ALU.mult))(), ['vh_h', 'vc'], 'stsc0'),
            ('act', lambda h: h.activation(out=sb['stsc0'][...], in_=sb['vh_h'][...], func=AF.Square), ['vh_h'], 'stsc0'),
            ('act', (lambda h2=h_: lambda h: h.activation(out=sb['stsc0'][...], in_=sb['vc'][:, h2, :], func=AF.Square))(), ['vc'], 'stsc0'),
            ('act', (lambda h2=h_: lambda h: h.activation(out=sb['stsc0'][...], in_=sb['kc'][:, h2, :], func=AF.Square))(), ['kc'], 'stsc0'),
            ('act', (lambda h2=h_: lambda h: h.activation(out=sb['stsc0'][...], in_=sb['qc'][:, h2, :], func=AF.Square))(), ['qc'], 'stsc0'),
        ]
        for si, (eng, fn, rd, scr) in enumerate(stats):
            S.comp(eng, fn, reads=[b(x) for x in rd], writes=[b(scr)])
            S.comp('pe', (lambda si2=si, scr2=scr: lambda h: h.matmul(
                ps['pa1'][0:8, :], sb['c_zc'][:, si2, 0:8], sb[scr2][...],
                start=(si2 == 0), stop=(si2 == 6)))(),
                reads=[b(scr), b('c_zc')], writes=[b('pa1')], glue=True)
        S.comp('act', lambda h: h.activation(out=sb['tmp8'][...], in_=ps['pa1'][0:8, :], func=AF.Copy),
               reads=[b('pa1')], writes=[b('tmp8')])
        S.dma('pool', (lambda h2=h_: lambda h: h.dma_start(
            out=sb['st_all'][h2:h2+1, :, :], in_=sb['tmp8'][0:7, :]))(),
            reads=[b('tmp8')], writes=[b('st_all')])

    def emit_head_decay(self, g, h_, sub):
        S = self.S
        sb, ps, b = self.par_ctx(h_)
        par = h_ % 2
        ib = 64 if par else 0
        for rn, dst in [('ampz8c', 'ampb'), ('rkn8c', 'knb'), ('rqn8c', 'qnb')]:
            S.comp('pe', (lambda rn2=rn, h2=h_: lambda h: h.matmul(
                ps['pa0'][...], sb['c_zrow8'][:, h2, :], sb[rn2][...],
                start=True, stop=True))(),
                reads=[b(rn), b('c_zrow8')], writes=[b('pa0')])
            S.comp('act', (lambda d2=dst: lambda h: h.activation(
                out=sb[d2][...], in_=ps['pa0'][...], func=AF.Copy))(),
                reads=[b('pa0')], writes=[b(dst)])
        S.comp('dve', (lambda h2=h_: lambda h: h.tensor_tensor(
            out=sb['zt'][...], in0=sb['kc'][:, h2, :], in1=sb['ampb'][...], op=ALU.mult))(),
            reads=[b('kc'), b('ampb')], writes=[b('zt')])
        dtb_h = sb['c_dtb'][:, h_:h_+1]
        S.comp('act', (lambda d2=dtb_h: lambda h: h.activation(
            out=sb['sp1'][...], in_=sb['zt'][...], func=AF.Relu, bias=d2))(),
            reads=[b('zt'), b('c_dtb')], writes=[b('sp1')])
        S.comp('act', (lambda d2=dtb_h: lambda h: h.activation(
            out=sb['ab'][...], in_=sb['zt'][...], func=AF.Abs, bias=d2))(),
            reads=[b('zt'), b('c_dtb')], writes=[b('ab')])
        S.comp('act', lambda h: h.activation(out=sb['ab'][...], in_=sb['ab'][...],
                                             func=AF.Exp, scale=-1.0),
               reads=[b('ab')], writes=[b('ab')])
        onec = self.constcol(1.0)
        S.comp('act', (lambda o2=onec: lambda h: h.activation(
            out=sb['ab'][...], in_=sb['ab'][...], func=AF.Ln, bias=o2[:, 0:1]))(),
            reads=[b('ab'), b(self.constname(1.0))], writes=[b('ab')])
        S.comp('dve', lambda h: h.tensor_tensor(out=sb['spt'][...], in0=sb['sp1'][...],
                                                in1=sb['ab'][...], op=ALU.add),
               reads=[b('sp1'), b('ab')], writes=[b('spt')])
        S.comp('dve', (lambda h2=h_: lambda h: h.tensor_scalar(
            out=sb['gt'][...], in0=sb['spt'][...], scalar1=sb['c_negea'][:, h2:h2+1],
            scalar2=None, op0=ALU.mult))(),
            reads=[b('spt'), b('c_negea')], writes=[b('gt')])
        S.comp('act', lambda h: h.activation(out=sb['D_h'][...], in_=sb['gt'][...], func=AF.Exp),
               reads=[b('gt')], writes=[b('D_h')])
        S.comp('dve', lambda h: h.tensor_tensor_scan(out=sb['G_h'][...], data0=sb['c_ones_f'][...],
                                                     data1=sb['gt'][...], initial=0.0,
                                                     op0=ALU.mult, op1=ALU.add),
               reads=[b('c_ones_f'), b('gt')], writes=[b('G_h')])
        S.comp('dve', (lambda h2=h_: lambda h: h.tensor_tensor(
            out=sb['kn_h'][...], in0=sb['kc'][:, h2, :], in1=sb['knb'][...], op=ALU.mult))(),
            reads=[b('kc'), b('knb')], writes=[b('kn_h')])
        S.comp('dve', (lambda h2=h_: lambda h: h.tensor_tensor(
            out=sb['qn_h'][...], in0=sb['qc'][:, h2, :], in1=sb['qnb'][...], op=ALU.mult))(),
            reads=[b('qc'), b('qnb')], writes=[b('qn_h')])
        S.comp('dve', lambda h: h.tensor_tensor(out=sb['p4'][...], in0=sb['qn_h'][...],
                                                in1=sb['kn_h'][...], op=ALU.mult),
               reads=[b('qn_h'), b('kn_h')], writes=[b('p4')])
        S.comp('pe', lambda h: h.matmul(ps['pa1'][0:16, :], sb['c_zc'][:, 8, :], sb['p4'][...],
                                        start=True, stop=False),
               reads=[b('c_zc'), b('p4')], writes=[b('pa1')])
        pcur = 'D_h'
        for dd in range(1, 8):
            if dd >= 2:
                pnew = ['pc0', 'pc1'][dd % 2]
                S.comp('dve', (lambda d2=dd, pc=pcur, pn=pnew: lambda h: h.tensor_tensor(
                    out=sb[pn][:, d2-1:512], in0=sb[pc][:, d2-1:512],
                    in1=sb['D_h'][:, 0:512-(d2-1)], op=ALU.mult))(),
                    reads=[b(pcur), b('D_h')], writes=[b(pnew)])
                pcur = pnew
            S.comp('dve', (lambda d2=dd, pc=pcur: lambda h: h.tensor_tensor(
                out=sb['ttb'][:, d2:512], in0=sb[pc][:, d2:512],
                in1=sb['kn_h'][:, 0:512-d2], op=ALU.mult))(),
                reads=[b(pcur), b('kn_h')], writes=[b('ttb')])
            S.comp('dve', (lambda d2=dd: lambda h: h.tensor_tensor(
                out=sb['p3'][:, d2:512], in0=sb['ttb'][:, d2:512],
                in1=sb['kn_h'][:, d2:512], op=ALU.mult))(),
                reads=[b('ttb'), b('kn_h')], writes=[b('p3')])
            S.comp('pe', (lambda d2=dd: lambda h: h.matmul(
                ps['pa1'][0:16, :], sb['c_zc'][:, d2, :], sb['p3'][...],
                start=False, stop=False))(),
                reads=[b('c_zc'), b('p3')], writes=[b('pa1')])
            S.comp('dve', (lambda d2=dd: lambda h: h.tensor_tensor(
                out=sb['p4'][:, d2:512], in0=sb['ttb'][:, d2:512],
                in1=sb['qn_h'][:, d2:512], op=ALU.mult))(),
                reads=[b('ttb'), b('qn_h')], writes=[b('p4')])
            S.comp('pe', (lambda d2=dd: lambda h: h.matmul(
                ps['pa1'][0:16, :], sb['c_zc'][:, 8+d2, :], sb['p4'][...],
                start=False, stop=(d2 == 7)))(),
                reads=[b('c_zc'), b('p4')], writes=[b('pa1')])
        S.comp('act', lambda h: h.activation(out=sb['pmrow'][...], in_=ps['pa1'][0:16, :], func=AF.Copy),
               reads=[b('pa1')], writes=[b('pmrow')])
        for cc in range(4):
            S.comp('pe', (lambda cc2=cc, ib2=ib: lambda h: h.transpose(
                ps['patb'][:, 0:16], sb['pmrow'][:, cc2*128:(cc2+1)*128],
                sb['c_ident'][ib2:ib2+16, ib2:ib2+16]))(),
                reads=[b('pmrow'), b('c_ident')], writes=[b('patb')])
            S.comp('dve', (lambda cc2=cc: lambda h: h.tensor_copy(
                out=sb['TRt'][:, cc2, :], in_=ps['patb'][:, 0:16]))(),
                reads=[b('patb')], writes=[b('TRt')])
        S.comp('pe', (lambda h2=h_: lambda h: h.matmul(
            ps['pa0'][...], sb['c_wg2'][:, h2, :], sb['glt'][...], start=True, stop=True))(),
            reads=[b('c_wg2'), b('glt')], writes=[b('pa0')])
        S.comp('act', (lambda h2=h_: lambda h: h.activation(
            out=sb['sgate'][:, h2, :], in_=ps['pa0'][...], func=AF.Sigmoid,
            bias=sb['c_bg2'][:, h2:h2+1]))(),
            reads=[b('pa0'), b('c_bg2')], writes=[b('sgate')])
        if sub >= 4:
            for cc in range(4):
                self.emit_chunk(g, h_, cc, csub=self.csub)

    def emit_chunk(self, g, h_, cc, csub=99):
        S = self.S
        dr = self.dr
        sb, ps, b = self.par_ctx(h_)
        s0 = cc * 128
        sl = slice(s0, s0 + 128)
        Ssrc = 'Sfp0'
        Sdst = 'Sfp0'
        seq_first = (g == 0 and cc == 0)
        if cc == 0:
            colA_src = 'c_zero'
        else:
            S.comp('dve', (lambda s2=s0: lambda h: h.tensor_scalar(
                out=sb['colA'][...], in0=sb['G_h'][:, s2-1:s2], scalar1=-1.0, scalar2=None,
                op0=ALU.mult))(),
                reads=[b('G_h')], writes=[b('colA')])
            colA_src = 'colA'
        S.comp('act', (lambda s2=sl, ca=colA_src: lambda h: h.activation(
            out=sb['lv_ae'][...], in_=sb['G_h'][:, s2], func=AF.Exp, bias=sb[ca][:, 0:1]))(),
            reads=[b('G_h'), b(colA_src)], writes=[b('lv_ae')])
        S.comp('dve', (lambda s2=sl: lambda h: h.tensor_tensor(
            out=sb['m_W'][...], in0=sb['kn_h'][:, s2], in1=sb['lv_ae'][...], op=ALU.mult))(),
            reads=[b('kn_h'), b('lv_ae')], writes=[b('m_W')])
        S.comp('dve', (lambda s2=sl: lambda h: h.tensor_tensor(
            out=sb['m_qt'][...], in0=sb['qn_h'][:, s2], in1=sb['lv_ae'][...], op=ALU.mult))(),
            reads=[b('qn_h'), b('lv_ae')], writes=[b('m_qt')])
        S.comp('act', (lambda s2=sl, e2=s0+127: lambda h: h.activation(
            out=sb['lv_be'][...], in_=sb['G_h'][:, s2], func=AF.Exp, scale=-1.0,
            bias=sb['G_h'][:, e2:e2+1]))(),
            reads=[b('G_h')], writes=[b('lv_be')])
        S.comp('dve', (lambda s2=sl: lambda h: h.tensor_tensor(
            out=sb['m_Wb'][...], in0=sb['kn_h'][:, s2], in1=sb['lv_be'][...], op=ALU.mult))(),
            reads=[b('kn_h'), b('lv_be')], writes=[b('m_Wb')])
        S.comp('act', (lambda e2=s0+127, ca=colA_src: lambda h: h.activation(
            out=sb['lamc'][...], in_=sb['G_h'][:, e2:e2+1], func=AF.Exp, bias=sb[ca][:, 0:1]))(),
            reads=[b('G_h'), b(colA_src)], writes=[b('lamc')])
        S.comp('pe', (lambda h2=h_, s2=sl: lambda h: h.transpose(
            ps['patb'][:, 0:128], sb['vc'][:, h2, s2], sb['c_ident'][...]))(),
            reads=[b('vc'), b('c_ident')], writes=[b('patb')])
        S.comp('dve', (lambda h2=h_, cc2=cc: lambda h: h.tensor_scalar(
            out=sb['m_bV'][...], in0=ps['patb'][:, 0:128], scalar1=sb['betaT'][:, cc2*8+h2:cc2*8+h2+1],
            scalar2=None, op0=ALU.mult))(),
            reads=[b('patb'), b('betaT')], writes=[b('m_bV')])
        S.comp('pe', lambda h: h.transpose(ps['patb'][:, 128:256], sb['m_Wb'][...], sb['c_ident'][...]),
               reads=[b('m_Wb'), b('c_ident')], writes=[b('patb')])
        S.comp('dve', lambda h: h.tensor_copy(out=sb['m_WbT'][...], in_=ps['patb'][:, 128:256]),
               reads=[b('patb')], writes=[b('m_WbT')])
        if csub < 1:
            return
        asm_lv = csub >= 22 or csub < 20
        asm_ab = csub >= 23 or csub < 20
        asm_mb = csub >= 24 or csub < 20
        asm_ev = csub >= 25 or csub < 20
        S.comp('pe', lambda h: h.matmul(ps['pa2'][:, 0:256], sb['c_zrow'][0:1, 0:128],
                                        sb['c_zrow'][0:1, 0:256], start=True, stop=False),
               reads=[b('c_zrow')], writes=[b('pa2')])
        nlast = [None]
        for lv in ([64, 32, 16, 8] if (csub not in (13, 14) and asm_lv) else []):
            ng = 128 // (2*lv)
            gc3 = sb['G_h'][:, sl].rearrange("p (a q) -> p a q", q=2*lv)
            refb = gc3[:, :, lv-1:lv].broadcast_to((128, ng, 2*lv))
            S.comp('dve', (lambda g3=gc3, r3=refb: lambda h: h.tensor_tensor(
                out=sb['lv_ga'][...].rearrange("p (a q) -> p a q", q=g3.shape[2]),
                in0=g3, in1=r3, op=ALU.subtract))(),
                reads=[b('G_h')], writes=[b('lv_ga')])
            S.comp('dve', lambda h: h.tensor_scalar(out=sb['lv_gb'][...], in0=sb['lv_ga'][...],
                                                    scalar1=-1.0, scalar2=0.0, op0=ALU.mult, op1=ALU.min),
                   reads=[b('lv_ga')], writes=[b('lv_gb')])
            S.comp('dve', lambda h: h.tensor_scalar(out=sb['lv_ga'][...], in0=sb['lv_ga'][...],
                                                    scalar1=0.0, scalar2=None, op0=ALU.min),
                   reads=[b('lv_ga')], writes=[b('lv_ga')])
            S.comp('act', lambda h: h.activation(out=sb['lv_ae'][...], in_=sb['lv_ga'][...], func=AF.Exp),
                   reads=[b('lv_ga')], writes=[b('lv_ae')])
            S.comp('act', lambda h: h.activation(out=sb['lv_be'][...], in_=sb['lv_gb'][...], func=AF.Exp),
                   reads=[b('lv_gb')], writes=[b('lv_be')])
            S.comp('dve', (lambda s2=sl: lambda h: h.tensor_tensor(
                out=sb['lv_ak'][...], in0=sb['kn_h'][:, s2], in1=sb['lv_ae'][...], op=ALU.mult))(),
                reads=[b('kn_h'), b('lv_ae')], writes=[b('lv_ak')])
            S.comp('dve', (lambda s2=sl: lambda h: h.tensor_tensor(
                out=sb['lv_aq'][...], in0=sb['qn_h'][:, s2], in1=sb['lv_ae'][...], op=ALU.mult))(),
                reads=[b('qn_h'), b('lv_ae')], writes=[b('lv_aq')])
            S.comp('dve', (lambda s2=sl: lambda h: h.tensor_tensor(
                out=sb['lv_bk'][...], in0=sb['kn_h'][:, s2], in1=sb['lv_be'][...], op=ALU.mult))(),
                reads=[b('kn_h'), b('lv_be')], writes=[b('lv_bk')])
            for a in range(ng):
                r0 = 2*lv*a + lv
                c0 = 2*lv*a
                if r0 % 32 == 0 and r0 != 96:
                    S.comp('pe', (lambda r2=r0, c2=c0, l2=lv: lambda h: h.matmul(
                        ps['pa2'][r2:r2+l2, c2:c2+l2], sb['lv_ak'][:, r2:r2+l2],
                        sb['lv_bk'][:, c2:c2+l2], start=False, stop=False))(),
                        reads=[b('lv_ak'), b('lv_bk')], writes=[b('pa2')])
                else:
                    zone = r0 % 64
                    win = r0 - zone
                    padn = f'padM_{zone}_{lv}'
                    S.comp('dve', (lambda p2=padn, z2=zone, r2=r0, l2=lv: lambda h: h.tensor_copy(
                        out=sb[p2][:, z2:z2+l2], in_=sb['lv_ak'][:, r2:r2+l2]))(),
                        reads=[b('lv_ak')], writes=[b(padn)])
                    S.comp('pe', (lambda p2=padn, w2=win, c2=c0, l2=lv: lambda h: h.matmul(
                        ps['pa2'][w2:w2+64, c2:c2+l2], sb[p2][...],
                        sb['lv_bk'][:, c2:c2+l2], start=False, stop=False))(),
                        reads=[b(padn), b('lv_bk')], writes=[b('pa2')], glue=True)
                if c0 % 32 == 0 and c0 != 96:
                    S.comp('pe', (lambda r2=r0, c2=c0, l2=lv: lambda h: h.matmul(
                        ps['pa2'][c2:c2+l2, 128+r2:128+r2+l2], sb['lv_bk'][:, c2:c2+l2],
                        sb['lv_aq'][:, r2:r2+l2], start=False, stop=False))(),
                        reads=[b('lv_bk'), b('lv_aq')], writes=[b('pa2')])
                else:
                    zone = c0 % 64
                    win = c0 - zone
                    padn = f'padA_{zone}_{lv}'
                    S.comp('dve', (lambda p2=padn, z2=zone, c2=c0, l2=lv: lambda h: h.tensor_copy(
                        out=sb[p2][:, z2:z2+l2], in_=sb['lv_bk'][:, c2:c2+l2]))(),
                        reads=[b('lv_bk')], writes=[b(padn)])
                    S.comp('pe', (lambda p2=padn, w2=win, r2=r0, l2=lv: lambda h: h.matmul(
                        ps['pa2'][w2:w2+64, 128+r2:128+r2+l2], sb[p2][...],
                        sb['lv_aq'][:, r2:r2+l2], start=False, stop=False))(),
                        reads=[b(padn), b('lv_aq')], writes=[b('pa2')], glue=True)
        for dd in (range(0, 8) if (csub not in (11, 14) and asm_ab) else []):
            S.comp('dve', (lambda d2=dd, cc2=cc: lambda h: h.tensor_scalar(
                out=sb['diagt'][...], in0=sb['c_ident'][...], scalar1=sb['TRt'][:, cc2, 8+d2:9+d2],
                scalar2=None, op0=ALU.mult))(),
                reads=[b('c_ident'), b('TRt')], writes=[b('diagt')])
            shn_ap = (lambda d2=dd: (sb['c_ident'][...] if d2 == 0 else sb['c_shift'][:, d2-1, :]))()
            S.comp('pe', (lambda s2=shn_ap: lambda h: h.matmul(
                ps['pa2'][:, 128:256], s2, sb['diagt'][...], start=False, stop=False))(),
                reads=[b('c_shift' if dd > 0 else 'c_ident'), b('diagt')], writes=[b('pa2')])
        for dd in (range(1, 8) if (csub not in (11, 12, 14) and asm_mb) else []):
            S.comp('dve', (lambda d2=dd, cc2=cc: lambda h: h.tensor_scalar(
                out=sb['diagt'][...], in0=sb['c_ident'][...], scalar1=sb['TRt'][:, cc2, d2:d2+1],
                scalar2=None, op0=ALU.mult))(),
                reads=[b('c_ident'), b('TRt')], writes=[b('diagt')])
            S.comp('pe', (lambda d2=dd: lambda h: h.matmul(
                ps['pa2'][:, 0:128], sb['diagt'][...], sb['c_shift'][:, d2-1, :],
                start=False, stop=(d2 == 7)))(),
                reads=[b('diagt'), b('c_shift')], writes=[b('pa2')])
        if not asm_ev:
            return
        S.comp('dve', (lambda h2=h_, cc2=cc: lambda h: h.tensor_scalar(
            out=sb['m_N1'][...], in0=ps['pa2'][:, 0:128],
            scalar1=sb['betaTn'][:, cc2*8+h2:cc2*8+h2+1], scalar2=None, op0=ALU.mult))(),
            reads=[b('pa2'), b('betaTn')], writes=[b('m_N1')])
        S.comp('act', lambda h: h.activation(out=sb['m_At'][...], in_=ps['pa2'][:, 128:256], func=AF.Copy),
               reads=[b('pa2')], writes=[b('m_At')])
        S.comp('pe', lambda h: h.transpose(ps['patb'][:, 256:384], sb['m_N1'][...], sb['c_ident'][...]),
               reads=[b('m_N1'), b('c_ident')], writes=[b('patb')])
        S.comp('dve', lambda h: h.tensor_copy(out=sb['m_X1'][...], in_=ps['patb'][:, 256:384]),
               reads=[b('patb')], writes=[b('m_X1')])
        if csub < 2:
            return
        sqjobs = [(1, 2), (2, 4), (4, 8), (8, 16), (16, 32), (32, 64)]
        pb_i = 0
        for i, j in sqjobs:
            bank = ['pa0', 'pa1'][pb_i % 2]; pb_i += 1
            S.comp('pe', (lambda i2=i, bank2=bank: lambda h: h.matmul(
                ps[bank2][:, 0:128], sb[f'm_X{i2}'][...], sb[f'm_N{i2}'][...], start=True, stop=True))(),
                reads=[b(f'm_X{i}'), b(f'm_N{i}')], writes=[b(bank)])
            eng = 'act' if pb_i % 2 == 0 else 'dve'
            if eng == 'act':
                S.comp('act', (lambda j2=j, bank2=bank: lambda h: h.activation(
                    out=sb[f'm_N{j2}'][...], in_=ps[bank2][:, 0:128], func=AF.Copy))(),
                    reads=[b(bank)], writes=[b(f'm_N{j}')])
            else:
                S.comp('dve', (lambda j2=j, bank2=bank: lambda h: h.tensor_copy(
                    out=sb[f'm_N{j2}'][...], in_=ps[bank2][:, 0:128]))(),
                    reads=[b(bank)], writes=[b(f'm_N{j}')])
            if j < 64:
                bank = ['pa0', 'pa1'][pb_i % 2]; pb_i += 1
                S.comp('pe', (lambda i2=i, bank2=bank: lambda h: h.matmul(
                    ps[bank2][:, 0:128], sb[f'm_N{i2}'][...], sb[f'm_X{i2}'][...], start=True, stop=True))(),
                    reads=[b(f'm_N{i}'), b(f'm_X{i}')], writes=[b(bank)])
                eng = 'act' if pb_i % 2 == 0 else 'dve'
                if eng == 'act':
                    S.comp('act', (lambda j2=j, bank2=bank: lambda h: h.activation(
                        out=sb[f'm_X{j2}'][...], in_=ps[bank2][:, 0:128], func=AF.Copy))(),
                        reads=[b(bank)], writes=[b(f'm_X{j}')])
                else:
                    S.comp('dve', (lambda j2=j, bank2=bank: lambda h: h.tensor_copy(
                        out=sb[f'm_X{j2}'][...], in_=ps[bank2][:, 0:128]))(),
                        reads=[b(bank)], writes=[b(f'm_X{j}')])
        qprev = 'c_ident'
        for i in [1, 2, 4, 8, 16, 32, 64]:
            bank = ['pa0', 'pa1'][pb_i % 2]; pb_i += 1
            S.comp('pe', (lambda i2=i, q2=qprev, bank2=bank: lambda h: h.matmul(
                ps[bank2][:, 0:128], sb[f'm_N{i2}'][...], sb[q2][...], start=True, stop=False))(),
                reads=[b(f'm_N{i}'), b(qprev)], writes=[b(bank)])
            S.comp('pe', (lambda q2=qprev, bank2=bank: lambda h: h.matmul(
                ps[bank2][:, 0:128], sb['c_ident'][...], sb[q2][...], start=False, stop=True))(),
                reads=[b('c_ident'), b(qprev)], writes=[b(bank)])
            S.comp('dve' if pb_i % 2 == 0 else 'act',
                   (lambda bank2=bank, e2=('dve' if pb_i % 2 == 0 else 'act'): lambda h: (
                       h.tensor_copy(out=sb['m_Q'][...], in_=ps[bank2][:, 0:128]) if e2 == 'dve'
                       else h.activation(out=sb['m_Q'][...], in_=ps[bank2][:, 0:128], func=AF.Copy)))(),
                   reads=[b(bank)], writes=[b('m_Q')])
            qprev = 'm_Q'
        if csub < 3:
            return
        S.comp('pe', (lambda h2=h_: lambda h: h.matmul(
            ps['pa0'][:, 0:128], sb['m_W'][...], sb['Sbf'][:, h2, :], start=True, stop=True))(),
            reads=[b('m_W'), b('Sbf')], writes=[b('pa0')])
        S.comp('dve', (lambda h2=h_, cc2=cc: lambda h: h.scalar_tensor_tensor(
            out=sb['m_R'][...], in0=ps['pa0'][:, 0:128],
            scalar=sb['betaTn'][:, cc2*8+h2:cc2*8+h2+1], in1=sb['m_bV'][...],
            op0=ALU.mult, op1=ALU.add))(),
            reads=[b('pa0'), b('betaTn'), b('m_bV')], writes=[b('m_R')])
        S.comp('pe', lambda h: h.matmul(ps['pa1'][:, 0:128], sb['m_Q'][...], sb['m_R'][...],
                                        start=True, stop=True),
               reads=[b('m_Q'), b('m_R')], writes=[b('pa1')])
        S.comp('dve', lambda h: h.tensor_copy(out=sb['m_U'][...], in_=ps['pa1'][:, 0:128]),
               reads=[b('pa1')], writes=[b('m_U')])
        S.comp('pe', (lambda h2=h_: lambda h: h.matmul(
            ps['pa2'][:, 256:384], sb['m_qt'][...], sb['Sbf'][:, h2, :], start=True, stop=False))(),
            reads=[b('m_qt'), b('Sbf')], writes=[b('pa2')])
        S.comp('pe', lambda h: h.matmul(ps['pa2'][:, 256:384], sb['m_At'][...], sb['m_U'][...],
                                        start=False, stop=True),
               reads=[b('m_At'), b('m_U')], writes=[b('pa2')])
        S.comp('pe', lambda h: h.matmul(ps['pa0'][:, 0:128], sb['m_WbT'][...], sb['m_U'][...],
                                        start=True, stop=True),
               reads=[b('m_WbT'), b('m_U')], writes=[b('pa0')])
        S.comp('dve', (lambda h2=h_: lambda h: h.scalar_tensor_tensor(
            out=sb['Stmp'][...], in0=sb['Sfp0'][:, h2, :], scalar=sb['lamc'][:, 0:1],
            in1=ps['pa0'][:, 0:128], op0=ALU.mult, op1=ALU.add))(),
            reads=[b('Sfp0'), b('lamc'), b('pa0')], writes=[b('Stmp')])
        S.comp('dve', (lambda h2=h_: lambda h: h.tensor_copy(
            out=sb['Sfp0'][:, h2, :], in_=sb['Stmp'][...]))(),
            reads=[b('Stmp')], writes=[b('Sfp0')])
        S.comp('act', (lambda h2=h_: lambda h: h.activation(
            out=sb['Sbf'][:, h2, :], in_=sb['Stmp'][...], func=AF.Copy))(),
            reads=[b('Stmp')], writes=[b('Sbf')])
        if csub < 4:
            return
        S.comp('act', lambda h: h.activation(out=sb['otm'][...], in_=ps['pa2'][:, 256:384], func=AF.Copy),
               reads=[b('pa2')], writes=[b('otm')])
        S.comp('act', lambda h: h.activation(out=sb['diagt'][...], in_=sb['otm'][...], func=AF.Square),
               reads=[b('otm')], writes=[b('diagt')])
        S.comp('dve', lambda h: h.tensor_reduce(out=sb['rms_c'][...], in_=sb['diagt'][...],
                                                axis=AX.X, op=ALU.add),
               reads=[b('diagt')], writes=[b('rms_c')])
        S.comp('dve', lambda h: h.tensor_scalar(out=sb['rms_c'][...], in0=sb['rms_c'][...],
                                                scalar1=1.0/DV, scalar2=NORM_EPS,
                                                op0=ALU.mult, op1=ALU.add),
               reads=[b('rms_c')], writes=[b('rms_c')])
        S.comp('act', lambda h: h.activation(out=sb['rms_c'][...], in_=sb['rms_c'][...], func=AF.Ln),
               reads=[b('rms_c')], writes=[b('rms_c')])
        S.comp('act', lambda h: h.activation(out=sb['rms_c'][...], in_=sb['rms_c'][...],
                                             func=AF.Exp, scale=-0.5),
               reads=[b('rms_c')], writes=[b('rms_c')])
        S.comp('dve', lambda h: h.tensor_scalar(out=sb['otm2'][...], in0=sb['otm'][...],
                                                scalar1=sb['rms_c'][:, 0:1], scalar2=None, op0=ALU.mult),
               reads=[b('otm'), b('rms_c')], writes=[b('otm2')])
        S.comp('pe', lambda h: h.transpose(ps['patb'][:, 384:512], sb['otm2'][...], sb['c_ident'][...]),
               reads=[b('otm2'), b('c_ident')], writes=[b('patb')])
        S.comp('dve', (lambda h2=h_, s2=sl: lambda h: h.scalar_tensor_tensor(
            out=sb['o_ct'][:, h2, s2], in0=ps['patb'][:, 384:512], scalar=sb['c_normw'][:, 0:1],
            in1=sb['sgate'][:, h2, s2], op0=ALU.mult, op1=ALU.mult))(),
            reads=[b('patb'), b('c_normw'), b('sgate')], writes=[b('o_ct')])

    def emit_pp_cc(self, tt):
        S, b = self.S, self.buf
        dr = self.dr
        S.cc((lambda tt2=tt: lambda h: h.collective_compute(
            "AllReduce", ALU.add,
            replica_groups=[[0, 1], [2, 3], [4, 5], [6, 7]],
            ins=[dr[f'pp_in{tt2}'].ap().opt()],
            outs=[dr[f'pp_out{tt2}'].ap().opt()]))(),
            reads=[b(f'pp_in{tt}')], writes=[b(f'pp_out{tt}')])

    def emit_wo_f2(self, tt):
        S, b = self.S, self.buf
        dr, sb, ps = self.dr, self.sb, self.ps
        ppo = dr[f'pp_out{tt}'].ap().rearrange("(s c p) t -> s p c t", s=2, p=128)
        x1ap = dr['x1T'].ap().rearrange("(c p) t -> p c t", p=128)
        x2ap = dr['x2T'].ap().rearrange("(c p) t -> p c t", p=128)
        for oc in range(NCH):
            b0 = ['x2b0', 'x2b1'][0]; b1s = ['x2b0', 'x2b1'][1]
            S.dma('sp', (lambda oc2=oc: lambda h: h.dma_start(
                out=sb['ct1'][...], in_=ppo[0, :, oc2, :]))(),
                reads=[b(f'pp_out{tt}')], writes=[b('ct1')])
            S.dma('sp', (lambda oc2=oc: lambda h: h.dma_start(
                out=sb['ct2'][...], in_=ppo[1, :, oc2, :]))(),
                reads=[b(f'pp_out{tt}')], writes=[b('ct2')])
            xs = [sb['xch0'], sb['xch1']][oc % 2]
            xsb = [b('xch0'), b('xch1')][oc % 2]
            S.dma('sp', (lambda oc2=oc, xs2=xs: lambda h: h.dma_start(
                out=xs2[...], in_=x1ap[:, oc2, tt*CT:(tt+1)*CT]))(),
                reads=[b('x1T')], writes=[xsb])
            S.comp('dve', (lambda xs2=xs: lambda h: h.scalar_tensor_tensor(
                out=sb['toc'][...], in0=sb['ct1'][...], scalar=sb['msel'][:, 0:1], in1=xs2[...],
                op0=ALU.mult, op1=ALU.add))(),
                reads=[b('ct1'), b('msel'), xsb], writes=[b('toc')])
            xoc = [sb['x1oc0'], sb['x1oc1']][oc % 2]
            xb = [b('x1oc0'), b('x1oc1')][oc % 2]
            S.comp('dve', (lambda xoc2=xoc: lambda h: h.scalar_tensor_tensor(
                out=xoc2[...], in0=sb['ct2'][...], scalar=sb['msel'][:, 1:2], in1=sb['toc'][...],
                op0=ALU.mult, op1=ALU.add))(),
                reads=[b('ct2'), b('msel'), b('toc')], writes=[xb])
            S.dma('sp', (lambda oc2=oc, xoc2=xoc: lambda h: h.dma_start(
                out=x2ap[:, oc2, tt*CT:(tt+1)*CT], in_=xoc2[...]))(),
                reads=[xb], writes=[b('x2T')])
            S.comp('pe', (lambda oc2=oc, xoc2=xoc: lambda h: h.matmul(
                ps['pb0'][0:1, :], sb['c_ones_col'][:, 0:1], xoc2[...],
                start=(oc2 == 0), stop=(oc2 == NCH-1)))(),
                reads=[xb, b('c_ones_col')], writes=[b('pb0')])
            sq = sb['xsq0']
            sqb = b('xsq0')
            S.comp('act', (lambda xoc2=xoc, sq2=sq: lambda h: h.activation(
                out=sq2[...], in_=xoc2[...], func=AF.Square))(),
                reads=[xb], writes=[sqb])
            S.comp('pe', (lambda oc2=oc, sq2=sq: lambda h: h.matmul(
                ps['pb2'][0:1, :], sb['c_ones_col'][:, 0:1], sq2[...],
                start=(oc2 == 0), stop=(oc2 == NCH-1)))(),
                reads=[sqb, b('c_ones_col')], writes=[b('pb2')])
        S.comp('dve', lambda h: h.tensor_scalar(out=sb['m_row'][...], in0=ps['pb0'][0:1, :],
                                                scalar1=1.0/HD, scalar2=None, op0=ALU.mult),
               reads=[b('pb0')], writes=[b('m_row')])
        S.comp('dve', lambda h: h.tensor_scalar(out=sb['t_row'][...], in0=ps['pb2'][0:1, :],
                                                scalar1=1.0/HD, scalar2=None, op0=ALU.mult),
               reads=[b('pb2')], writes=[b('t_row')])
        self._ln_rows()
        src = dr['x2T'].ap().rearrange("(c p) t -> p c t", p=128)
        for c in range(NCH):
            xs = [sb['xch0'], sb['xch1']][c % 2]
            xsb = [b('xch0'), b('xch1')][c % 2]
            S.dma('sp', (lambda c2=c, xs2=xs: lambda h: h.dma_start(
                out=xs2[...], in_=src[:, c2, tt*CT:(tt+1)*CT]))(),
                reads=[b('x2T')], writes=[xsb])
            S.comp('dve', (lambda c2=c, xs2=xs: lambda h: h.tensor_tensor(
                out=sb['xr'][:, c2, :], in0=xs2[...], in1=ps['pb1'][...], op=ALU.mult))(),
                reads=[xsb, b('pb1')], writes=[b('xr')])
        self.emit_ffn(tt, 'f2')

    def load_wosl(self, g):
        pass

    def build_full(self, upto=99, sub=99, csub=99):
        self.sub = sub
        self.csub = csub
        self.alloc()
        self.emit_const_load()
        self.emit_wconv_pre()
        if upto >= 1:
            for tt in range(self.nt):
                self.emit_ln(tt, 'xT', 'xr', 'pb0', 'pb2')
                self.emit_ffn(tt, 'f1')
                self.emit_ln2_and_export(tt)
            self.emit_mr_cc()
        if upto >= 2:
            self.arena_barrier('a')
            for n in ['p31', 'p41', 'ttb1', 'pc01', 'pc11']:
                self.S.comp('dve', (lambda n2=n: lambda h: h.memset(self.sb[n2][...], 0.0))(),
                            writes=[self.buf(n)])
            for g in range(2 * self.nt):
                if upto < 3 + g:
                    break
                self.emit_attn_ct(g, sub=self.sub)
                if g % 2 == 1 and self.sub >= 5:
                    self.emit_pp_cc(g // 2)
        if upto >= 3 + 2*self.nt:
            self.arena_barrier('f')
            for tt in range(self.nt):
                self.emit_wo_f2(tt)
        self.S.finalize()
        sems = self.sems
        with self.nc.Block() as block:
            block.tensor(self.S.emitter('pe', sems))
            block.scalar(self.S.emitter('act', sems))
            block.vector(self.S.emitter('dve', sems))
            block.gpsimd(self.S.emitter('pool', sems))
            block.sync(self.S.emitter('sp', sems))
        return self.nc

    def build_f1_only(self):
        self.alloc()
        self.emit_const_load()
        self.emit_wconv_pre()
        for tt in range(self.nt):
            self.emit_ln(tt, 'xT', 'xr', 'pb0', 'pb2')
            self.emit_ffn(tt, 'f1')
            self.emit_ln2_and_export(tt)
        self.emit_mr_cc()
        self.S.finalize()
        sems = self.sems
        with self.nc.Block() as block:
            block.tensor(self.S.emitter('pe', sems))
            block.scalar(self.S.emitter('act', sems))
            block.vector(self.S.emitter('dve', sems))
            block.gpsimd(self.S.emitter('pool', sems))
            block.sync(self.S.emitter('sp', sems))
        return self.nc


def kernel(**inputs):
    from concourse.bass_utils import run_bass_kernel_spmd
    n_tiles = 4
    B = Build(n_tiles, dbg=False)
    nc = B.build_full()
    in_maps = host_prep(inputs, n_tiles)
    res = run_bass_kernel_spmd(nc, in_maps, core_ids=list(range(8)))
    out = np.zeros((4, 4096, HD), np.float32)
    for core in range(8):
        bi, hg = core // 2, core % 2
        ot = np.asarray(res.results[core]['outT'], np.float32)
        for tt in range(n_tiles):
            gt = 2 * tt + hg
            out[bi, gt*CT:(gt+1)*CT, :] = ot[:, tt*CT:(tt+1)*CT].T
    return out
